# version 14
# baseline (speedup 1.0000x reference)
import numpy as np
from contextlib import ExitStack

import concourse.bass as bass
import concourse.mybir as mybir
from concourse.bass_utils import run_bass_kernel_spmd

F32 = mybir.dt.float32
BF16 = mybir.dt.bfloat16
AF = mybir.ActivationFunctionType
ALU = mybir.AluOpType
AX = mybir.AxisListType

D = 1024
H = 16
NE = 16
INTER = 512
ROUTE_SCALE = 2.5
EPS = 1e-5
NEG = -1.0e9

ENGS = ("tensor", "scalar", "vector", "gpsimd", "sync")


class Walk:

    def __init__(self, name, sem_names):
        self.name = name
        self.eng = None
        self.sems = None
        self.c = {k: 0 for k in sem_names}
        self._lastw = {}

    def _emit(self, owner, fn, incs):
        inst = fn(self.eng) if self.name == owner else None
        for sem, n in incs:
            self.c[sem] += n
            if inst is not None:
                inst.then_inc(self.sems[sem], n)

    def t(self, fn, incs=()):
        self._emit("tensor", fn, incs)

    def s(self, fn, incs=()):
        self._emit("scalar", fn, incs)

    def v(self, fn, incs=()):
        self._emit("vector", fn, incs)

    def g(self, fn, incs=()):
        self._emit("gpsimd", fn, incs)

    def y(self, fn, incs=()):
        self._emit("sync", fn, incs)

    def wait(self, eng_name, sem, val):
        if val <= 0 or self.name != eng_name:
            return
        if self._lastw.get(sem, 0) >= val:
            return
        self._lastw[sem] = val
        self.eng.wait_ge(self.sems[sem], val)

    def drain(self, eng_name):
        if self.name == eng_name:
            self.eng.drain()


def build_program(S=4096, taps=()):
    taps = set(taps)
    NQB = S // 512
    NKT = S // 128
    NCH = S // 512
    NPT = S // 128
    QTR = S // 4
    NQT = NPT // 4
    BCH = min(512, QTR)
    NBCH = QTR // BCH
    NST = BCH // 128
    SQW = 2048

    nc = bass.Bass("TRN2", target_bir_lowering=False)

    d_x = nc.dram_tensor("x", [S, D], F32, kind="ExternalInput").ap()
    d_xT = nc.dram_tensor("xT", [D, S], F32, kind="ExternalInput").ap()
    d_wqT = nc.dram_tensor("wqT", [D, D], F32, kind="ExternalInput").ap()
    d_wkT = nc.dram_tensor("wkT", [D, D], F32, kind="ExternalInput").ap()
    d_wvT = nc.dram_tensor("wvT", [D, D], F32, kind="ExternalInput").ap()
    d_woT = nc.dram_tensor("woT", [D, D], F32, kind="ExternalInput").ap()
    d_gwT = nc.dram_tensor("gwT", [D, NE], F32, kind="ExternalInput").ap()
    d_gb = nc.dram_tensor("gb", [NE, 1], F32, kind="ExternalInput").ap()
    d_gbT = nc.dram_tensor("gbT", [1, NE], F32, kind="ExternalInput").ap()
    d_w13T = nc.dram_tensor("w13T", [NE, D, D], F32, kind="ExternalInput").ap()
    d_w2T = nc.dram_tensor("w2T", [NE, INTER, D], F32, kind="ExternalInput").ap()
    d_w13sT = nc.dram_tensor("w13sT", [D, D], F32, kind="ExternalInput").ap()
    d_w2sT = nc.dram_tensor("w2sT", [INTER, D], F32, kind="ExternalInput").ap()
    d_ident = nc.dram_tensor("ident", [128, 128], F32, kind="ExternalInput").ap()
    d_ltri = nc.dram_tensor("ltri", [128, 128], F32, kind="ExternalInput").ap()
    d_ones = nc.dram_tensor("ones", [1, 128], F32, kind="ExternalInput").ap()
    d_ones2w = nc.dram_tensor("ones2w", [128, 128], F32, kind="ExternalInput").ap()
    d_ones64 = nc.dram_tensor("ones64", [128, 64], F32, kind="ExternalInput").ap()

    d_out = nc.dram_tensor("out", [S, D], F32, kind="ExternalOutput").ap()
    d_x1h = nc.dram_tensor("x1h", [S, D], F32, kind="Internal").ap()

    tap_t = {}
    for name, shape, dt in (
        ("tap_qt", [128, 4, S], BF16), ("tap_kt", [128, 4, S], BF16),
        ("tap_v", [128, NKT, 8, 65], BF16), ("tap_ytn", [2, 128, 4, S], BF16),
        ("tap_x1", [S, D], F32), ("tap_comb", [128, NPT, NE], F32),
        ("tap_x1t", [128, 8, S], BF16), ("tap_negm", [128, 2], F32),
        ("tap_yacc", [128, NQT, D], F32),
    ):
        if name in taps:
            tap_t[name] = nc.dram_tensor(name, shape, dt,
                                         kind="ExternalOutput").ap()

    sem_names = ("MM", "SC", "VE", "CONSTG", "CONSTS", "WQKV", "XTB",
                 "WO", "XRES0", "XRES1", "XRES2", "X1F0", "X1F1", "GW",
                 "GBS", "XTB2",
                 "W13A", "W13B", "X2RES0", "X2RES1", "OUT0", "OUT1",
                 "YST0", "YST1", "TAP")
    walks = {n: Walk(n, sem_names) for n in ENGS}

    st_persist = ExitStack()
    E = st_persist.enter_context
    semd = {n: E(nc.semaphore(n)) for n in sem_names}
    for wk in walks.values():
        wk.sems = semd

    c_id_b = E(nc.sbuf_tensor("c_id_b", [128, 128], BF16))
    c_id_f = E(nc.sbuf_tensor("c_id_f", [128, 128], F32))
    c_ones_f = E(nc.sbuf_tensor("c_ones_f", [1, 128], F32))
    c_o64_f = E(nc.sbuf_tensor("c_o64_f", [128, 64], F32))
    negm_b = E(nc.sbuf_tensor("negm_b", [128, 2], F32))

    st_ytn = ExitStack()
    yTn = [st_ytn.enter_context(nc.sbuf_tensor(f"yTn{g}", [128, 4, S], BF16))
           for g in range(2)]

    st_a = ExitStack()
    EA = st_a.enter_context
    c_ltri_b = EA(nc.sbuf_tensor("c_ltri_b", [128, 128], BF16))
    c_o64_b = EA(nc.sbuf_tensor("c_o64_b", [128, 64], BF16))
    c_o2w_b = EA(nc.sbuf_tensor("c_o2w_b", [128, 128], BF16))
    xTb = EA(nc.sbuf_tensor("xTb", [128, 8, 512], BF16))
    xTb2 = yTn[1][:, 0, :].rearrange("p (a b) -> p a b", a=8)
    xtbs = [xTb[:, :, :], xTb2]
    wq_b = EA(nc.sbuf_tensor("wq_b", [128, 8, 512], BF16))
    wk_b = EA(nc.sbuf_tensor("wk_b", [128, 8, 512], BF16))
    wv_b = EA(nc.sbuf_tensor("wv_b", [128, 8, 512], BF16))
    QT = EA(nc.sbuf_tensor("QT", [128, 4, S], BF16))
    KT = EA(nc.sbuf_tensor("KT", [128, 4, S], BF16))
    Vb = EA(nc.sbuf_tensor("Vb", [128, NKT, 8, 65], BF16))
    sq2 = EA(nc.sbuf_tensor("sq2", [128, SQW], BF16))
    ptb = [[sq2[:, (2 * i + e) * 512:(2 * i + e + 1) * 512]
            for i in range(2)] for e in range(2)]
    ptb3 = [[ptb[e][0], ptb[e][1], xTb[:, e, :]] for e in range(2)]
    mxq = EA(nc.sbuf_tensor("mxq", [128, 1], F32))
    mxk = EA(nc.sbuf_tensor("mxk", [128, 1], F32))
    mxt = EA(nc.sbuf_tensor("mxt", [128, 1], F32))
    sc1 = EA(nc.sbuf_tensor("sc1", [1, 4], F32))
    rdf = EA(nc.sbuf_tensor("rdf", [65, 1024], BF16))
    yev = EA(nc.sbuf_tensor("yev", [65, 2, 512], BF16))
    rb_t = EA(nc.sbuf_tensor("rb_t", [64, 512], F32))
    rb_sb = rb_t[:, :]
    yst = [EA(nc.sbuf_tensor("yst0", [64, 512], BF16))]
    p_sq = EA(nc.psum_tensor("p_sq", [128, 2048], F32))
    p_s = [p_sq[:, i * 512:(i + 1) * 512] for i in range(4)]
    p_y = [EA(nc.psum_tensor(f"p_y{e}", [65, 512], F32)) for e in range(2)]
    p_rb = EA(nc.psum_tensor("p_rb", [64, 512], F32))

    def emit_A(w: Walk):
        w.g(lambda g: g.dma_start(out=c_id_b[:, :], in_=d_ident),
            [("CONSTG", 16)])
        w.g(lambda g: g.dma_start(out=c_ltri_b[:, :], in_=d_ltri),
            [("CONSTG", 16)])
        w.g(lambda g: g.dma_start(out=c_o64_b[:, :], in_=d_ones64),
            [("CONSTG", 16)])
        w.g(lambda g: g.dma_start(out=c_o2w_b[:, :], in_=d_ones2w),
            [("CONSTG", 16)])
        gv_consts = w.c["CONSTG"]
        w.y(lambda y: y.dma_start(out=c_id_f[:, :], in_=d_ident),
            [("CONSTS", 16)])
        w.y(lambda y: y.dma_start(out=c_ones_f[:, :], in_=d_ones),
            [("CONSTS", 16)])
        w.y(lambda y: y.dma_start(out=c_o64_f[:, :], in_=d_ones64),
            [("CONSTS", 16)])
        sv_consts = w.c["CONSTS"]
        w.v(lambda v: v.memset(Vb[:, :, :, 64:65], 1.0), [("VE", 1)])
        ve_ones = w.c["VE"]

        wT_loaded = [0, 0]
        xtb_rd2 = [0, 0]
        wslice_rd = 0
        vb_rd_prev = 0
        qt_rd_prev = 0
        sq2_rd4 = [0, 0, 0, 0]
        ps_wr = [0] * 4
        ps_rd = [("SC", 0)] * 4
        py_rd = [0, 0]
        ptb_rd = [[0, 0, 0], [0, 0, 0]]
        ptb_ve = [[0, 0, 0], [0, 0, 0]]
        rb_rd = 0
        ti_rd = 0
        yst_st = [0, 0]
        yev_rd = 0

        for g in range(2):
            gsl = slice(g * 512, (g + 1) * 512)
            w.wait("gpsimd", "MM", wslice_rd)
            for dsrc, dst in ((d_wqT, wq_b), (d_wkT, wk_b), (d_wvT, wv_b)):
                w.g(lambda gg, dsrc=dsrc, dst=dst: gg.dma_start(
                    out=dst[:, :, :],
                    in_=dsrc.rearrange("(cc p) n -> p cc n", p=128)[:, :, gsl]),
                    [("WQKV", 16)])
            wT_loaded[g] = w.c["WQKV"]

            for ch in range(NCH):
                cb = ch % 2
                w.wait("gpsimd", "MM", xtb_rd2[cb])
                xts = "XTB" if cb == 0 else "XTB2"
                w.g(lambda gg, ch=ch, cb=cb, xts=xts: gg.dma_start(
                    out=xtbs[cb],
                    in_=d_xT.rearrange("(cc p) n -> p cc n", p=128)
                    [:, :, ch * 512:(ch + 1) * 512]), [(xts, 16)])
                gv_x = w.c[xts]
                w.wait("tensor", xts, gv_x)
                w.wait("tensor", "WQKV", wT_loaded[g])
                chs = slice(ch * 512, (ch + 1) * 512)
                for wi, (wbuf, dstT) in enumerate(((wq_b, QT), (wk_b, KT))):
                    for dt in range(4):
                        pb = (wi * 4 + dt) % 4
                        sem, cnt = ps_rd[pb]
                        w.wait("tensor", sem, cnt)
                        for cc in range(8):
                            w.t(lambda t, pb=pb, wbuf=wbuf, dt=dt, cc=cc,
                                cb=cb: t.matmul(
                                    p_s[pb],
                                    wbuf[:, cc, dt * 128:(dt + 1) * 128],
                                    xtbs[cb][:, cc, :],
                                    start=(cc == 0), stop=(cc == 7)),
                                [("MM", 1)] if cc == 7 else ())
                        mmv = w.c["MM"]
                        ps_wr[pb] = mmv
                        w.wait("vector", "MM", max(mmv, qt_rd_prev))
                        w.v(lambda v, pb=pb, dstT=dstT, dt=dt, chs=chs:
                            v.tensor_copy(dstT[:, dt, chs], p_s[pb]),
                            [("VE", 1)])
                        ps_rd[pb] = ("VE", w.c["VE"])
                for pt in range(4):
                    pb = pt % 4
                    sem, cnt = ps_rd[pb]
                    w.wait("tensor", sem, cnt)
                    for cc in range(8):
                        w.t(lambda t, pb=pb, cc=cc, pt=pt, cb=cb:
                            t.matmul(p_s[pb],
                                     xtbs[cb][:, cc, pt * 128:(pt + 1) * 128],
                                     wv_b[:, cc, :],
                                     start=(cc == 0), stop=(cc == 7)),
                            [("MM", 1)] if cc == 7 else ())
                    mmv = w.c["MM"]
                    ps_wr[pb] = mmv
                    kt_i = ch * 4 + pt
                    w.wait("vector", "VE", ve_ones)
                    w.wait("vector", "MM", max(mmv, vb_rd_prev))
                    w.v(lambda v, pb=pb, kt_i=kt_i:
                        v.tensor_copy(Vb[:, kt_i, 0:8, 0:64],
                                      p_s[pb].rearrange(
                                          "p (h d) -> p h d", d=64)),
                        [("VE", 1)])
                    ps_rd[pb] = ("VE", w.c["VE"])
                ptmax = max(ptb_rd[0] + ptb_rd[1])
                w.wait("tensor", "CONSTG", gv_consts)

                def norm_mm(w, item):
                    sqsl, nb, scv, mx, first = item
                    sem, cnt = ps_rd[nb]
                    w.wait("tensor", sem, cnt)
                    w.wait("tensor", "SC", scv)
                    w.t(lambda t, sqsl=sqsl, nb=nb: t.matmul(
                        p_s[nb], c_o2w_b[:, :], sq2[:, sqsl],
                        start=True, stop=True), [("MM", 1)])
                    ps_wr[nb] = w.c["MM"]
                    sq2_rd4[nb] = w.c["MM"]
                    w.wait("vector", "MM", ps_wr[nb])
                    w.v(lambda v, nb=nb: v.tensor_reduce(
                        mxt[:, :], p_s[nb], AX.X, ALU.max), [("VE", 1)])
                    ps_rd[nb] = ("VE", w.c["VE"])
                    w.drain("vector")
                    if first:
                        w.v(lambda v, mx=mx: v.tensor_copy(
                            mx[:, :], mxt[:, :]), [("VE", 1)])
                    else:
                        w.v(lambda v, mx=mx: v.tensor_tensor(
                            mx[:, :], mx[:, :], mxt[:, :], ALU.max),
                            [("VE", 1)])
                    w.drain("vector")

                pend_nm = None
                for si, (nsrc, mx) in enumerate(((QT, mxq), (KT, mxk))):
                    for pr in range(4):
                        nb = (si * 4 + pr) % 4
                        sqsl = slice(nb * 512, nb * 512 + 512)
                        w.wait("scalar", "MM", max(sq2_rd4[nb], ptmax))
                        w.wait("scalar", "VE", w.c["VE"])
                        w.s(lambda s, nsrc=nsrc, pr=pr, chs=chs, sqsl=sqsl:
                            s.activation(
                                sq2[:, sqsl], nsrc[:, pr, chs],
                                AF.Square), [("SC", 1)])
                        scv = w.c["SC"]
                        if pend_nm is not None:
                            norm_mm(w, pend_nm)
                        pend_nm = (sqsl, nb, scv, mx, ch == 0 and pr == 0)
                norm_mm(w, pend_nm)
                xtb_rd2[cb] = w.c["MM"]
            wslice_rd = w.c["MM"]
            ve_a1 = w.c["VE"]
            w.v(lambda v: v.tensor_tensor(sc1[:, 2:3], mxq[0:1, 0:1],
                                          mxk[0:1, 0:1], ALU.mult),
                [("VE", 1)])
            ve_m = w.c["VE"]
            w.wait("scalar", "VE", ve_m)
            w.s(lambda s: s.activation(sc1[:, 3:4], sc1[:, 2:3], AF.Ln),
                [("SC", 1)])
            w.drain("scalar")
            w.s(lambda s: s.activation(sc1[:, 2:3], sc1[:, 3:4], AF.Exp,
                                       scale=0.5), [("SC", 1)])
            scv_m = w.c["SC"]
            w.wait("vector", "SC", scv_m)
            w.drain("vector")
            w.v(lambda v: v.tensor_scalar_mul(sc1[:, 3:4], sc1[:, 2:3],
                                              -0.125), [("VE", 1)])
            ve_negm = w.c["VE"]
            sem0, cnt0 = ps_rd[0]
            w.wait("tensor", sem0, cnt0)
            w.wait("tensor", "VE", max(ve_negm, ti_rd))
            w.wait("tensor", "CONSTS", sv_consts)
            w.t(lambda t: t.matmul(p_sq[:, 0:1], c_ones_f[:, :], sc1[:, 3:4],
                                   start=True, stop=True), [("MM", 1)])
            mm_ti = w.c["MM"]
            w.wait("vector", "MM", mm_ti)
            w.v(lambda v, g=g: v.tensor_copy(negm_b[:, g:g + 1], p_sq[:, 0:1]),
                [("VE", 1)])
            ti_rd = w.c["VE"]
            ps_rd[0] = ("VE", w.c["VE"])
            ve_negm_done = w.c["VE"]

            w.wait("tensor", "CONSTG", gv_consts)
            pending_tail = [None]

            def flush_tail(w):
                if pending_tail[0] is None:
                    return
                fn = pending_tail[0]
                pending_tail[0] = None
                fn(w)

            for pr in range(4):
                for qb in range(NQB):
                    kts = 4 * qb + 4
                    qsl = slice(qb * 512, (qb + 1) * 512)
                    exp_v = [[("SC", 0)] * 3, [("SC", 0)] * 3]
                    for kt in range(kts + 2):
                        if kt == 1:
                            flush_tail(w)
                        if kt < kts:
                            j = kt - 4 * qb
                            i2 = kt % 2
                            i3 = kt % 3
                            qs = 128 * j if j > 0 else 0
                            for e in range(2):
                                sem, cnt = ps_rd[2 * i2 + e]
                                w.wait("tensor", sem, cnt)
                            w.wait("tensor", "VE", ve_a1)
                            for e in range(2):
                                hp = 64 * e
                                w.t(lambda t, i2=i2, e=e, hp=hp, pr=pr,
                                    kt=kt, qb=qb, qs=qs: t.matmul(
                                        p_s[2 * i2 + e][:, qs:512],
                                        KT[hp:hp + 64, pr,
                                           kt * 128:(kt + 1) * 128],
                                        QT[hp:hp + 64, pr,
                                           qb * 512 + qs:(qb + 1) * 512],
                                        start=True, stop=True,
                                        tile_position=(hp, 0)), [("MM", 1)])
                                ps_wr[2 * i2 + e] = w.c["MM"]
                            w.wait("scalar", "MM",
                                   max(ps_wr[2 * i2], ps_wr[2 * i2 + 1],
                                       ptb_rd[0][i3], ptb_rd[1][i3],
                                       max(sq2_rd4)))
                            w.wait("scalar", "VE",
                                   max(ve_negm_done, ptb_ve[0][i3],
                                       ptb_ve[1][i3]))
                            if qs == 0 and i3 < 2:
                                w.s(lambda s, i3=i3, i2=i2, g=g:
                                    s.activation(
                                        sq2[:, 2 * i3 * 512:
                                            (2 * i3 + 2) * 512],
                                        p_sq[:, 2 * i2 * 512:
                                             (2 * i2 + 2) * 512],
                                        AF.Exp, bias=negm_b[:, g:g + 1],
                                        scale=0.125), [("SC", 1)])
                            elif qs == 0:
                                w.s(lambda s, i2=i2, g=g: s.activation(
                                    xTb[:, 0:2, :].rearrange(
                                        "p a b -> p (a b)"),
                                    p_sq[:, 2 * i2 * 512:
                                         (2 * i2 + 2) * 512],
                                    AF.Exp, bias=negm_b[:, g:g + 1],
                                    scale=0.125), [("SC", 1)])
                            elif i3 < 2:
                                w.s(lambda s, i3=i3, i2=i2, g=g, qs=qs:
                                    s.activation(
                                        sq2[:, 2 * i3 * 512 + qs:
                                            (2 * i3 + 2) * 512],
                                        p_sq[:, 2 * i2 * 512 + qs:
                                             (2 * i2 + 2) * 512],
                                        AF.Exp, bias=negm_b[:, g:g + 1],
                                        scale=0.125), [("SC", 1)])
                            else:
                                w.s(lambda s, i2=i2, g=g, qs=qs:
                                    s.activation(
                                        xTb[:, 0:2, :].rearrange(
                                            "p a b -> p (a b)")[:, qs:1024],
                                        p_sq[:, 2 * i2 * 512 + qs:
                                             (2 * i2 + 2) * 512],
                                        AF.Exp, bias=negm_b[:, g:g + 1],
                                        scale=0.125), [("SC", 1)])
                            sc_exp = w.c["SC"]
                            ps_rd[2 * i2] = ("SC", sc_exp)
                            ps_rd[2 * i2 + 1] = ("SC", sc_exp)
                            if j >= 0:
                                w.wait("vector", "SC", sc_exp)
                                w.wait("vector", "CONSTG", gv_consts)
                                for e in range(2):
                                    w.v(lambda v, i3=i3, e=e, qs=qs:
                                        v.tensor_tensor(
                                            ptb3[e][i3][:, qs:qs + 128],
                                            ptb3[e][i3][:, qs:qs + 128],
                                            c_ltri_b[:, :], ALU.mult),
                                        [("VE", 1)])
                                    ptb_ve[e][i3] = w.c["VE"]
                                exp_v[0][i3] = ("VE", w.c["VE"])
                                exp_v[1][i3] = ("VE", w.c["VE"])
                            else:
                                exp_v[0][i3] = ("SC", sc_exp)
                                exp_v[1][i3] = ("SC", sc_exp)
                        if kt > 1:
                            pkt = kt - 2
                            pj = pkt - 4 * qb
                            pqs = 128 * pj if pj > 0 else 0
                            for e in range(2):
                                hg = 2 * pr + e
                                esem, ecnt = exp_v[e][pkt % 3]
                                w.wait("tensor", esem, ecnt)
                                if pkt == 0:
                                    w.wait("tensor", "VE", py_rd[e])
                                w.t(lambda t, e=e, pkt=pkt, hg=hg, kts=kts,
                                    pqs=pqs: t.matmul(
                                        p_y[e][:, pqs:512],
                                        Vb[:, pkt, hg, 0:65],
                                        ptb3[e][pkt % 3][:, pqs:512],
                                        start=(pkt == 0),
                                        stop=(pkt == kts - 1)),
                                    [("MM", 1)])
                                ptb_rd[e][pkt % 3] = w.c["MM"]
                    mm_pv = w.c["MM"]
                    w.wait("vector", "MM", mm_pv)
                    w.wait("vector", "VE", yev_rd)
                    w.drain("vector")
                    for e in range(2):
                        def _evac(v, e=e):
                            with nc.allow_low_precision(
                                    reason="bf16 attention-out operand"):
                                return v.tensor_copy(yev[:, e, :],
                                                     p_y[e][:, :])
                        w.v(_evac, [("VE", 1)])
                        py_rd[e] = w.c["VE"]
                    w.drain("vector")
                    for e in range(2):
                        def _recip(v, e=e):
                            with nc.allow_low_precision(
                                    reason="bf16 denom bcast operand"):
                                return v.reciprocal(
                                    rdf[64:65, e * 512:(e + 1) * 512],
                                    yev[64:65, e, :])
                        w.v(_recip, [("VE", 1)])
                    ve_rd2 = w.c["VE"]

                    def tail(w, g=g, pr=pr, qsl=qsl, ve_rd2=ve_rd2):
                        nonlocal rb_rd, yev_rd
                        ysl = 0
                        for e in range(2):
                            w.wait("tensor", "VE", max(ve_rd2, rb_rd))
                            w.wait("tensor", "CONSTG", gv_consts)
                            w.t(lambda t, e=e: t.matmul(
                                p_rb[:, :], c_o64_b[64:65, :],
                                rdf[64:65, e * 512:(e + 1) * 512],
                                start=True, stop=True), [("MM", 1)])
                            mm_rb = w.c["MM"]
                            w.wait("vector", "MM", mm_rb)
                            w.drain("vector")
                            w.v(lambda v: v.tensor_copy(rb_sb, p_rb[:, :]),
                                [("VE", 1)])
                            rb_rd = w.c["VE"]
                            w.drain("vector")
                            if e == 0:
                                def _m0(v, g=g, pr=pr, qsl=qsl):
                                    with nc.allow_low_precision(
                                            reason="bf16 yT operand"):
                                        return v.tensor_tensor(
                                            yTn[g][0:64, pr, qsl],
                                            yev[0:64, 0, :], rb_sb,
                                            ALU.mult)
                                w.v(_m0, [("VE", 1)])
                                yev_rd = w.c["VE"]
                            else:
                                yss = "YST0" if ysl == 0 else "YST1"
                                w.wait("vector", yss, yst_st[ysl])

                                def _m1(v, ysl=ysl):
                                    with nc.allow_low_precision(
                                            reason="bf16 yT operand"):
                                        return v.tensor_tensor(
                                            yst[ysl][:, :],
                                            yev[0:64, 1, :], rb_sb,
                                            ALU.mult)
                                w.v(_m1, [("VE", 1)])
                                yev_rd = w.c["VE"]
                                ve_yst = w.c["VE"]
                                w.wait("sync", "VE", ve_yst)
                                w.y(lambda y, g=g, pr=pr, qsl=qsl, ysl=ysl,
                                    yss=yss: y.dma_start(
                                        out=yTn[g][64:128, pr, qsl],
                                        in_=yst[ysl][:, :]), [(yss, 16)])
                                yst_st[ysl] = w.c[yss]
                    pending_tail[0] = tail
                flush_tail(w)
            flush_tail(w)
            vb_rd_prev = w.c["MM"]
            qt_rd_prev = w.c["MM"]
            xtb_rd2[0] = w.c["MM"]
            xtb_rd2[1] = w.c["MM"]

        if tap_t:
            w.wait("sync", "VE", w.c["VE"])
            w.wait("sync", "MM", w.c["MM"])
            w.wait("sync", "CONSTG", gv_consts)
            w.wait("sync", "YST0", w.c["YST0"])
            w.wait("sync", "YST1", w.c["YST1"])
        if "tap_qt" in tap_t:
            w.y(lambda y: y.dma_start(out=tap_t["tap_qt"], in_=QT[:, :, :]),
                [("TAP", 16)])
        if "tap_kt" in tap_t:
            w.y(lambda y: y.dma_start(out=tap_t["tap_kt"], in_=KT[:, :, :]),
                [("TAP", 16)])
        if "tap_v" in tap_t:
            w.y(lambda y: y.dma_start(out=tap_t["tap_v"], in_=Vb[:, :, :, :]),
                [("TAP", 16)])
        if "tap_negm" in tap_t:
            w.y(lambda y: y.dma_start(out=tap_t["tap_negm"], in_=negm_b[:, :]),
                [("TAP", 16)])
        if "tap_ytn" in tap_t:
            w.y(lambda y: y.dma_start(out=tap_t["tap_ytn"][0],
                                      in_=yTn[0][:, :, :]), [("TAP", 16)])
            w.y(lambda y: y.dma_start(out=tap_t["tap_ytn"][1],
                                      in_=yTn[1][:, :, :]), [("TAP", 16)])
        if w.name == "sync":
            w.eng.wait_ge(w.sems["YST0"], w.c["YST0"])
            w.eng.wait_ge(w.sems["YST1"], w.c["YST1"])
            w.eng.wait_ge(w.sems["TAP"], w.c["TAP"])

    def run_block(blk, emit):
        for en in ENGS:
            def cb(e, en=en):
                walks[en].eng = e
                emit(walks[en])
            getattr(blk, en)(cb)

    with nc.Block("blkA") as blkA:
        run_block(blkA, emit_A)

    st_a.close()

    st_x1t = ExitStack()
    x1T = st_x1t.enter_context(
        nc.sbuf_tensor("x1T", [128, 8, S], BF16, side="right"))
    st_a3 = ExitStack()
    E3 = st_a3.enter_context
    woT_b = E3(nc.sbuf_tensor("woT_b", [128, 8, 1024], BF16))
    xres = [E3(nc.sbuf_tensor(f"xres{i}", [128, D], F32)) for i in range(3)]
    x1pre = [E3(nc.sbuf_tensor(f"x1pre{i}", [128, D], F32)) for i in range(2)]
    x1f = [E3(nc.sbuf_tensor(f"x1f{i}", [128, D], F32)) for i in range(2)]
    x1bb = [E3(nc.sbuf_tensor(f"x1bb{i}", [128, D], BF16)) for i in range(2)]
    junk_b = E3(nc.sbuf_tensor("junk_b", [128, D], BF16))
    stt = E3(nc.sbuf_tensor("stt", [128, 16], F32))
    p_z = [E3(nc.psum_tensor(f"p_z{i}", [128, 512], F32)) for i in range(2)]
    p_t = [E3(nc.psum_tensor(f"p_t{i}", [128, 128], BF16)) for i in range(2)]

    def emit_A3(w: Walk):
        w.g(lambda g: g.dma_start(
            out=woT_b[:, :, :],
            in_=d_woT.rearrange("(cc p) n -> p cc n", p=128)), [("WO", 16)])
        gv_wo = w.c["WO"]
        xres_rd = [0, 0, 0]
        x1f_rd = [0, 0]
        x1b_rd = [0, 0]
        x1pre_rd = [0, 0]
        stt_rd = 0
        pz_rd = [0, 0]
        pt_rd = [0, 0]
        pend_tp = [None, None]
        for pt in range(NPT):
            bi = pt % 2
            ri = pt % 3
            psl = slice(pt * 128, (pt + 1) * 128)
            w.wait("sync", "VE", xres_rd[ri])
            xrs = ("XRES0", "XRES1", "XRES2")[ri]
            w.y(lambda y, psl=psl, ri=ri: y.dma_start(out=xres[ri][:, :],
                                                      in_=d_x[psl, :]),
                [(xrs, 16)])
            sv_x = w.c[xrs]
            w.wait("tensor", "WO", gv_wo)
            for half in range(2):
                w.wait("tensor", "VE", pz_rd[half])
                for cc in range(8):
                    g2, p2 = divmod(cc, 4)
                    w.t(lambda t, half=half, g2=g2, p2=p2, cc=cc, psl=psl:
                        t.matmul(p_z[half][:, :],
                                 yTn[g2][:, p2, psl],
                                 woT_b[:, cc, half * 512:(half + 1) * 512],
                                 start=(cc == 0), stop=(cc == 7)),
                        [("MM", 1)] if cc == 7 else ())
            mm_z = w.c["MM"]
            w.wait("vector", "MM", mm_z)
            w.wait("vector", xrs, sv_x)
            for half in range(2):
                w.v(lambda v, half=half, bi=bi, ri=ri: v.tensor_tensor(
                    x1pre[bi][:, half * 512:(half + 1) * 512],
                    p_z[half][:, :],
                    xres[ri][:, half * 512:(half + 1) * 512], ALU.add),
                    [("VE", 1)])
                pz_rd[half] = w.c["VE"]
            xres_rd[ri] = w.c["VE"]
            ve_pre = w.c["VE"]
            w.drain("vector")
            w.v(lambda v, bi=bi: v.bn_stats(stt[:, 0:6],
                                            x1pre[bi][:, 0:512]),
                [("VE", 1)])
            w.drain("vector")
            w.v(lambda v, bi=bi: v.bn_stats(stt[:, 6:12],
                                            x1pre[bi][:, 512:1024]),
                [("VE", 1)])
            w.drain("vector")
            w.wait("vector", "SC", stt_rd)
            w.v(lambda v: v.bn_aggr(stt[:, 12:14], stt[:, 0:12]),
                [("VE", 1)])
            w.drain("vector")
            w.v(lambda v: v.tensor_scalar(stt[:, 14:15], stt[:, 13:14],
                                          1.0, EPS, ALU.mult, ALU.add),
                [("VE", 1)])
            ve_var = w.c["VE"]
            w.wait("scalar", "VE", ve_var)
            w.s(lambda s: s.activation(stt[:, 15:16], stt[:, 14:15], AF.Ln),
                [("SC", 1)])
            w.drain("scalar")
            w.s(lambda s: s.activation(stt[:, 14:15], stt[:, 15:16], AF.Exp,
                                       scale=-0.5), [("SC", 1)])
            sc_rstd = w.c["SC"]
            stt_rd = sc_rstd
            w.wait("vector", "SC", sc_rstd)
            w.wait("vector", "X1F0" if bi == 0 else "X1F1", x1f_rd[bi])
            w.drain("vector")
            w.v(lambda v, bi=bi: v.tensor_scalar(x1f[bi][:, :],
                                                 x1pre[bi][:, :],
                                                 stt[:, 12:13], stt[:, 14:15],
                                                 ALU.subtract, ALU.mult),
                [("VE", 1)])
            ve_x1 = w.c["VE"]
            w.drain("vector")
            w.wait("vector", "MM", x1b_rd[bi])
            w.v(lambda v, bi=bi: v.tensor_copy(x1bb[bi][:, :], x1f[bi][:, :]),
                [("VE", 1)])
            ve_x1b = w.c["VE"]
            w.wait("sync", "VE", ve_x1)
            x1s = "X1F0" if bi == 0 else "X1F1"
            w.y(lambda y, bi=bi, psl=psl, x1s=x1s: y.dma_start(
                out=d_x1h[psl, :], in_=x1f[bi][:, :]), [(x1s, 16)])
            x1f_rd[bi] = w.c[x1s]
            pend_tp[bi] = (pt, ve_x1b)
            ppt = pend_tp[1 - bi]
            if ppt is not None:
                tpt, tve = ppt
                tbi = 1 - bi
                for cc in range(8):
                    ti = cc % 2
                    w.wait("tensor", "VE", max(tve, pt_rd[ti]))
                    w.t(lambda t, ti=ti, tbi=tbi, cc=cc: t.transpose(
                        p_t[ti][:, :], x1bb[tbi][:, cc * 128:(cc + 1) * 128],
                        c_id_b[:, :]), [("MM", 1)])
                    mm_t = w.c["MM"]
                    if cc == 7:
                        x1b_rd[tbi] = mm_t
                    w.wait("vector", "MM", mm_t)
                    w.v(lambda v, cc=cc, tpt=tpt, ti=ti: v.tensor_copy(
                        x1T[:, cc, tpt * 128:(tpt + 1) * 128], p_t[ti][:, :]),
                        [("VE", 1)])
                    pt_rd[ti] = w.c["VE"]
                pend_tp[1 - bi] = None
        for tbi in range(2):
            ppt = pend_tp[tbi]
            if ppt is None:
                continue
            tpt, tve = ppt
            for cc in range(8):
                ti = cc % 2
                w.wait("tensor", "VE", max(tve, pt_rd[ti]))
                w.t(lambda t, ti=ti, tbi=tbi, cc=cc: t.transpose(
                    p_t[ti][:, :], x1bb[tbi][:, cc * 128:(cc + 1) * 128],
                    c_id_b[:, :]), [("MM", 1)])
                mm_t = w.c["MM"]
                if cc == 7:
                    x1b_rd[tbi] = mm_t
                w.wait("vector", "MM", mm_t)
                w.v(lambda v, cc=cc, tpt=tpt, ti=ti: v.tensor_copy(
                    x1T[:, cc, tpt * 128:(tpt + 1) * 128], p_t[ti][:, :]),
                    [("VE", 1)])
                pt_rd[ti] = w.c["VE"]
            pend_tp[tbi] = None
        if "tap_x1" in tap_t:
            w.wait("sync", "X1F0", x1f_rd[0])
            w.wait("sync", "X1F1", x1f_rd[1])
            w.y(lambda y: y.dma_start(out=tap_t["tap_x1"], in_=d_x1h),
                [("TAP", 16)])
        if "tap_x1t" in tap_t:
            w.wait("sync", "VE", w.c["VE"])
            w.y(lambda y: y.dma_start(out=tap_t["tap_x1t"], in_=x1T[:, :, :]),
                [("TAP", 16)])
        if w.name == "sync":
            w.eng.wait_ge(w.sems["X1F0"], w.c["X1F0"])
            w.eng.wait_ge(w.sems["X1F1"], w.c["X1F1"])
            w.eng.wait_ge(w.sems["TAP"], w.c["TAP"])

    with nc.Block("blkA3") as blkA3:
        run_block(blkA3, emit_A3)

    st_a3.close()
    st_ytn.close()

    st_b = ExitStack()
    EB = st_b.enter_context
    y_acc = EB(nc.sbuf_tensor("y_acc", [128, NQT, D], F32))
    w13_b = [EB(nc.sbuf_tensor(f"w13_b{i}", [128, 8, 1024], BF16))
             for i in range(2)]
    w2_b = [EB(nc.sbuf_tensor(f"w2_b{i}", [128, 4, 1024], BF16))
            for i in range(2)]
    hT = [EB(nc.sbuf_tensor(f"hT{i}", [128, 4, BCH], BF16)) for i in range(2)]
    sg_b = EB(nc.sbuf_tensor("sg_b", [128, 512], BF16))
    gw_b = EB(nc.sbuf_tensor("gw_b", [128, 8, NE], BF16))
    gb_sb = EB(nc.sbuf_tensor("gb_sb", [NE, 1], F32))
    gbT_sb = EB(nc.sbuf_tensor("gbT_sb", [1, NE], F32))
    biT = EB(nc.sbuf_tensor("biT", [NE, S], F32))
    bi_pm = EB(nc.sbuf_tensor("bi_pm", [128, NPT, NE], F32))
    gbb = EB(nc.sbuf_tensor("gbb", [128, NE], F32))
    sp_pm = EB(nc.sbuf_tensor("sp_pm", [128, NE], F32))
    top8 = EB(nc.sbuf_tensor("top8", [128, 8], F32))
    msk_b = EB(nc.sbuf_tensor("msk_b", [128, NE], F32))
    wsel = EB(nc.sbuf_tensor("wsel", [128, NE], F32))
    dnr = EB(nc.sbuf_tensor("dnr", [128, 1], F32))
    comb = EB(nc.sbuf_tensor("comb", [128, NPT, NE], F32))
    x2res = [EB(nc.sbuf_tensor(f"x2res{i}", [128, D], F32)) for i in range(2)]
    x2pre = EB(nc.sbuf_tensor("x2pre", [128, D], F32))
    outf = [EB(nc.sbuf_tensor(f"outf{i}", [128, D], F32)) for i in range(2)]
    junk2 = EB(nc.sbuf_tensor("junk2", [128, D], BF16))
    st2 = EB(nc.sbuf_tensor("st2", [128, 16], F32))
    p_h = [EB(nc.psum_tensor(f"p_h{i}", [128, 512], F32)) for i in range(4)]
    p_zb = [EB(nc.psum_tensor(f"p_zb{i}", [128, 512], F32)) for i in range(2)]
    p_r16 = EB(nc.psum_tensor("p_r16", [NE, 512], F32))
    p_tb = EB(nc.psum_tensor("p_tb", [128, NE], F32))

    NEXP = NE + 1
    NGI = 4 * NEXP

    def exp_dma(w, e, bslot):
        ws = "W13A" if bslot == 0 else "W13B"
        if e < NE:
            w.g(lambda g, e=e, bslot=bslot: g.dma_start(
                out=w13_b[bslot][:, :, :],
                in_=d_w13T.rearrange("e (cc p) n -> e p cc n", p=128)[e]),
                [(ws, 16)])
            w.g(lambda g, e=e, bslot=bslot: g.dma_start(
                out=w2_b[bslot][:, :, :],
                in_=d_w2T.rearrange("e (cc p) n -> e p cc n", p=128)[e]),
                [(ws, 16)])
        else:
            w.g(lambda g, bslot=bslot: g.dma_start(
                out=w13_b[bslot][:, :, :],
                in_=d_w13sT.rearrange("(cc p) n -> p cc n", p=128)),
                [(ws, 16)])
            w.g(lambda g, bslot=bslot: g.dma_start(
                out=w2_b[bslot][:, :, :],
                in_=d_w2sT.rearrange("(cc p) n -> p cc n", p=128)),
                [(ws, 16)])
        return w.c[ws]

    def emit_B(w: Walk):
        w.g(lambda g: g.dma_start(
            out=gw_b[:, :, :],
            in_=d_gwT.rearrange("(cc p) n -> p cc n", p=128)), [("GW", 16)])
        gv_gw = w.c["GW"]
        w.y(lambda y: y.dma_start(out=gb_sb[:, :], in_=d_gb), [("GBS", 16)])
        w.y(lambda y: y.dma_start(out=gbT_sb[:, :], in_=d_gbT), [("GBS", 16)])
        sv_gb = w.c["GBS"]
        w.wait("tensor", "GBS", sv_gb)
        w.t(lambda t: t.matmul(p_tb[:, :], c_ones_f[:, :], gbT_sb[:, :],
                               start=True, stop=True), [("MM", 1)])
        mm_gbb = w.c["MM"]
        w.wait("vector", "MM", mm_gbb)
        w.v(lambda v: v.tensor_copy(gbb[:, :], p_tb[:, :]), [("VE", 1)])
        ptb_rd = w.c["VE"]
        r16_rd = 0
        for ch in range(NCH):
            chs = slice(ch * 512, (ch + 1) * 512)
            w.wait("tensor", "GW", gv_gw)
            w.wait("tensor", "SC", r16_rd)
            for cc in range(8):
                w.t(lambda t, cc=cc, chs=chs: t.matmul(
                    p_r16[:, :], gw_b[:, cc, :], x1T[:, cc, chs],
                    start=(cc == 0), stop=(cc == 7)),
                    [("MM", 1)] if cc == 7 else ())
            mm_r = w.c["MM"]
            w.wait("scalar", "MM", mm_r)
            w.s(lambda s, chs=chs: s.activation(biT[:, chs], p_r16[:, :],
                                                AF.Sigmoid), [("SC", 1)])
            r16_rd = w.c["SC"]
        sc_router = w.c["SC"]
        w.wait("vector", "SC", sc_router)
        w.wait("vector", "GBS", sv_gb)
        w.v(lambda v: v.tensor_scalar(biT[:, :], biT[:, :], gb_sb[:, :], None,
                                      ALU.add), [("VE", 1)])
        ve_bi = w.c["VE"]

        def route_tiles(w, pts):
            nonlocal ptb_rd
            for pt in pts:
                w.wait("tensor", "VE", max(ve_bi, ptb_rd))
                w.t(lambda t, pt=pt: t.transpose(
                    p_tb[:, :], biT[:, pt * 128:(pt + 1) * 128],
                    c_id_f[0:16, 0:16]), [("MM", 1)])
                mm_t = w.c["MM"]
                w.wait("vector", "MM", mm_t)
                w.drain("vector")
                w.v(lambda v, pt=pt: v.tensor_copy(bi_pm[:, pt, :],
                                                   p_tb[:, :]), [("VE", 1)])
                ptb_rd = w.c["VE"]
                w.drain("vector")
                w.v(lambda v, pt=pt: v.tensor_tensor(
                    sp_pm[:, :], bi_pm[:, pt, :], gbb[:, :], ALU.subtract),
                    [("VE", 1)])
                w.drain("vector")
                w.v(lambda v, pt=pt: v.max(out=top8[:, :],
                                           in_=bi_pm[:, pt, :]), [("VE", 1)])
                w.drain("vector")
                w.v(lambda v, pt=pt: v.tensor_scalar(
                    msk_b[:, :], bi_pm[:, pt, :], top8[:, 3:4], None,
                    ALU.is_ge), [("VE", 1)])
                w.drain("vector")
                w.v(lambda v: v.tensor_tensor(wsel[:, :], sp_pm[:, :],
                                              msk_b[:, :], ALU.mult),
                    [("VE", 1)])
                w.drain("vector")
                w.v(lambda v: v.tensor_reduce(dnr[:, :], wsel[:, :], AX.X,
                                              ALU.add), [("VE", 1)])
                w.drain("vector")
                w.v(lambda v: v.reciprocal(dnr[:, :], dnr[:, :]),
                    [("VE", 1)])
                w.drain("vector")
                w.v(lambda v, pt=pt: v.tensor_scalar(
                    comb[:, pt, :], wsel[:, :], dnr[:, :], ROUTE_SCALE,
                    ALU.mult, ALU.mult), [("VE", 1)])

        route_tiles(w, range(0, NQT))

        wload = [0, 0]
        w13_rd = [0, 0]
        hT_rd = [0, 0]
        ph_rd = [("SC", 0), ("VE", 0), ("SC", 0), ("VE", 0)]
        pz_rd = [0, 0]
        sg_rd = 0
        x2res_rd = [0, 0]
        outf_rd = [0, 0]
        x2pre_rd = 0
        st2_rd = 0
        tap_yg = [0]
        wload[0] = exp_dma(w, 0, 0)
        for gi in range(NGI):
            qt_i, e = divmod(gi, NEXP)
            bslot = gi % 2
            if e == 0 and qt_i + 1 < 4:
                route_tiles(w, range((qt_i + 1) * NQT, (qt_i + 2) * NQT))
            if gi + 1 < NGI:
                nslot = (gi + 1) % 2
                w.wait("gpsimd", "MM", w13_rd[nslot])
                wload[nslot] = exp_dma(w, (gi + 1) % NEXP, nslot)
            def emit_z(w, ch, ve_h, bslot=bslot, qt_i=qt_i, e=e):
                nonlocal pz_rd, hT_rd, w13_rd
                w.drain("vector")
                for pst in range(NST):
                    t_i = ch * NST + pst
                    pt_g = qt_i * NQT + t_i
                    for co in range(2):
                        zb = co
                        w.wait("tensor", "VE", max(ve_h, pz_rd[zb]))
                        for iop in range(4):
                            w.t(lambda t, zb=zb, iop=iop, ch=ch, pst=pst,
                                bslot=bslot, co=co: t.matmul(
                                    p_zb[zb][:, :],
                                    hT[ch % 2][:, iop,
                                               pst * 128:(pst + 1) * 128],
                                    w2_b[bslot][:, iop,
                                                co * 512:(co + 1) * 512],
                                    start=(iop == 0), stop=(iop == 3)),
                                [("MM", 1)] if iop == 3 else ())
                        mm_zb = w.c["MM"]
                        hT_rd[ch % 2] = mm_zb
                        w13_rd[bslot] = mm_zb
                        w.wait("vector", "MM", mm_zb)
                        w.wait("vector", "TAP", tap_yg[0])
                        csl = slice(co * 512, (co + 1) * 512)
                        if e == 0:
                            w.v(lambda v, t_i=t_i, csl=csl, zb=zb, pt_g=pt_g:
                                v.tensor_scalar(y_acc[:, t_i, csl],
                                                p_zb[zb][:, :],
                                                comb[:, pt_g, 0:1], None,
                                                ALU.mult), [("VE", 1)])
                        elif e < NE:
                            w.v(lambda v, t_i=t_i, csl=csl, zb=zb, pt_g=pt_g,
                                e=e: v.scalar_tensor_tensor(
                                    y_acc[:, t_i, csl], p_zb[zb][:, :],
                                    comb[:, pt_g, e:e + 1],
                                    y_acc[:, t_i, csl],
                                    ALU.mult, ALU.add), [("VE", 1)])
                        else:
                            w.v(lambda v, t_i=t_i, csl=csl, zb=zb:
                                v.scalar_tensor_tensor(
                                    y_acc[:, t_i, csl], p_zb[zb][:, :], 1.0,
                                    y_acc[:, t_i, csl],
                                    ALU.mult, ALU.add), [("VE", 1)])
                        pz_rd[zb] = w.c["VE"]

            pend_z = None
            for ch in range(NBCH):
                gpos = qt_i * QTR + ch * BCH
                gsl = slice(gpos, gpos + BCH)
                w.wait("tensor", "W13A" if bslot == 0 else "W13B",
                       wload[bslot])
                if ch > 0:
                    w.wait("tensor", "MM", hT_rd[ch % 2])
                for iop in range(4):
                    pa = 2 * (iop % 2)
                    pb_ = pa + 1
                    for pslot, iotile in ((pa, iop), (pb_, 4 + iop)):
                        sem, cnt = ph_rd[pslot]
                        w.wait("tensor", sem, cnt)
                        for cc in range(8):
                            w.t(lambda t, pslot=pslot, cc=cc, iotile=iotile,
                                bslot=bslot, gsl=gsl: t.matmul(
                                    p_h[pslot][:, 0:BCH],
                                    w13_b[bslot][:, cc,
                                                 iotile * 128:
                                                 (iotile + 1) * 128],
                                    x1T[:, cc, gsl],
                                    start=(cc == 0), stop=(cc == 7)),
                                [("MM", 1)] if cc == 7 else ())
                    mm_gu = w.c["MM"]
                    w.wait("scalar", "MM", mm_gu)
                    w.wait("scalar", "VE", sg_rd)
                    w.s(lambda s, pa=pa: s.activation(sg_b[:, 0:BCH],
                                                      p_h[pa][:, 0:BCH],
                                                      AF.Silu), [("SC", 1)])
                    sc_silu = w.c["SC"]
                    ph_rd[pa] = ("SC", sc_silu)
                    w.wait("vector", "SC", sc_silu)
                    w.wait("vector", "MM", hT_rd[ch % 2])
                    w.v(lambda v, iop=iop, ch=ch, pb_=pb_: v.tensor_tensor(
                        hT[ch % 2][:, iop, :], sg_b[:, 0:BCH],
                        p_h[pb_][:, 0:BCH], ALU.mult), [("VE", 1)])
                    sg_rd = w.c["VE"]
                    ph_rd[pb_] = ("VE", w.c["VE"])
                ve_h = w.c["VE"]
                if pend_z is not None:
                    emit_z(w, pend_z[0], pend_z[1])
                pend_z = (ch, ve_h)
            emit_z(w, pend_z[0], pend_z[1])
            if e != NE:
                continue
            if "tap_comb" in tap_t and qt_i == 3 and e == NE:
                w.wait("sync", "VE", w.c["VE"])
                w.y(lambda y: y.dma_start(out=tap_t["tap_comb"],
                                          in_=comb[:, :, :]), [("TAP", 16)])
            if "tap_yacc" in tap_t and qt_i == 0:
                w.wait("sync", "VE", w.c["VE"])
                w.y(lambda y: y.dma_start(out=tap_t["tap_yacc"],
                                          in_=y_acc[:, :, :]), [("TAP", 16)])
                tap_yg[0] = w.c["TAP"]
            w.drain("vector")
            for t_i in range(NQT):
                bi = t_i % 2
                pt_g = qt_i * NQT + t_i
                psl = slice(pt_g * 128, (pt_g + 1) * 128)
                w.wait("sync", "VE", x2res_rd[bi])
                x2s = "X2RES0" if bi == 0 else "X2RES1"
                w.y(lambda y, psl=psl, bi=bi, x2s=x2s: y.dma_start(
                    out=x2res[bi][:, :], in_=d_x1h[psl, :]), [(x2s, 16)])
                sv_r = w.c[x2s]
                w.wait("vector", x2s, sv_r)
                w.wait("vector", "SC", x2pre_rd)
                w.v(lambda v, t_i=t_i, bi=bi: v.tensor_tensor(
                    x2pre[:, :], y_acc[:, t_i, :], x2res[bi][:, :], ALU.add),
                    [("VE", 1)])
                x2res_rd[bi] = w.c["VE"]
                ve_pre = w.c["VE"]
                w.drain("vector")
                w.v(lambda v: v.bn_stats(st2[:, 0:6], x2pre[:, 0:512]),
                    [("VE", 1)])
                w.drain("vector")
                w.v(lambda v: v.bn_stats(st2[:, 6:12], x2pre[:, 512:1024]),
                    [("VE", 1)])
                w.drain("vector")
                w.wait("vector", "SC", st2_rd)
                w.v(lambda v: v.bn_aggr(st2[:, 12:14], st2[:, 0:12]),
                    [("VE", 1)])
                w.drain("vector")
                w.v(lambda v: v.tensor_scalar(st2[:, 14:15], st2[:, 13:14],
                                              1.0, EPS, ALU.mult, ALU.add),
                    [("VE", 1)])
                ve_var = w.c["VE"]
                w.wait("scalar", "VE", ve_var)
                w.s(lambda s: s.activation(st2[:, 15:16], st2[:, 14:15],
                                           AF.Ln), [("SC", 1)])
                w.drain("scalar")
                w.s(lambda s: s.activation(st2[:, 14:15], st2[:, 15:16],
                                           AF.Exp, scale=-0.5), [("SC", 1)])
                sc_rstd = w.c["SC"]
                st2_rd = sc_rstd
                x2pre_rd = sc_rstd
                w.wait("vector", "SC", sc_rstd)
                w.wait("vector", "OUT0" if bi == 0 else "OUT1", outf_rd[bi])
                w.drain("vector")
                w.v(lambda v, bi=bi: v.tensor_scalar(outf[bi][:, :],
                                                     x2pre[:, :],
                                                     st2[:, 12:13],
                                                     st2[:, 14:15],
                                                     ALU.subtract, ALU.mult),
                    [("VE", 1)])
                ve_out = w.c["VE"]
                w.wait("sync", "VE", ve_out)
                ots = "OUT0" if bi == 0 else "OUT1"
                w.y(lambda y, bi=bi, psl=psl, ots=ots: y.dma_start(
                    out=d_out[psl, :], in_=outf[bi][:, :]), [(ots, 16)])
                outf_rd[bi] = w.c[ots]
        if w.name == "sync":
            w.eng.wait_ge(w.sems["OUT0"], w.c["OUT0"])
            w.eng.wait_ge(w.sems["OUT1"], w.c["OUT1"])
            w.eng.wait_ge(w.sems["TAP"], w.c["TAP"])

    with nc.Block("blkB") as blkB:
        run_block(blkB, emit_B)

    st_b.close()
    st_x1t.close()
    st_persist.close()

    base = walks["tensor"].c
    for n, wk in walks.items():
        assert wk.c == base, (n, wk.c, base)
    return nc


def _host_consts():
    ident = np.eye(128, dtype=np.float32)
    k = np.arange(128)[:, None]
    q = np.arange(128)[None, :]
    cmask = np.ascontiguousarray((q >= k).astype(np.float32))
    ones = np.ones((1, 128), dtype=np.float32)
    o2w = np.zeros((128, 128), dtype=np.float32)
    o2w[:, 0] = 1.0
    o64 = np.ones((128, 64), dtype=np.float32)
    return ident, cmask, ones, o2w, o64


def make_shared_inputs(L0_attn_W_Q, L0_attn_W_K, L0_attn_W_V, L0_attn_W_O,
                       L0_ffn_gate_w, L0_ffn_gate_b, L0_ffn_w13, L0_ffn_w2,
                       L0_ffn_w13_s, L0_ffn_w2_s):
    ident, cmask, ones, o2w, o64 = _host_consts()
    f32 = np.float32
    A = lambda a: np.ascontiguousarray(np.asarray(a, f32))
    return {
        "wqT": A(np.asarray(L0_attn_W_Q, f32).T),
        "wkT": A(np.asarray(L0_attn_W_K, f32).T),
        "wvT": A(np.asarray(L0_attn_W_V, f32).T),
        "woT": A(np.asarray(L0_attn_W_O, f32).T),
        "gwT": A(np.asarray(L0_ffn_gate_w, f32).T),
        "gb": A(np.asarray(L0_ffn_gate_b, f32)[:, None]),
        "gbT": A(np.asarray(L0_ffn_gate_b, f32)[None, :]),
        "w13T": A(np.asarray(L0_ffn_w13, f32).transpose(0, 2, 1)),
        "w2T": A(np.asarray(L0_ffn_w2, f32).transpose(0, 2, 1)),
        "w13sT": A(np.asarray(L0_ffn_w13_s, f32).T),
        "w2sT": A(np.asarray(L0_ffn_w2_s, f32).T),
        "ident": ident, "ltri": cmask, "ones": ones, "ones2w": o2w,
        "ones64": o64,
    }


def kernel(L0_attn_x, L0_attn_W_Q, L0_attn_W_K, L0_attn_W_V, L0_attn_W_O,
           L0_ffn_input_ids, L0_ffn_gate_w, L0_ffn_gate_b, L0_ffn_w13,
           L0_ffn_w2, L0_ffn_w13_s, L0_ffn_w2_s):
    shared = make_shared_inputs(L0_attn_W_Q, L0_attn_W_K, L0_attn_W_V,
                                L0_attn_W_O, L0_ffn_gate_w, L0_ffn_gate_b,
                                L0_ffn_w13, L0_ffn_w2, L0_ffn_w13_s,
                                L0_ffn_w2_s)
    xs = np.asarray(L0_attn_x, np.float32)
    in_maps = []
    for c in range(8):
        x = np.ascontiguousarray(xs[c])
        xT = np.ascontiguousarray(x.T)
        m = {"x": x, "xT": xT}
        m.update(shared)
        in_maps.append(m)
    nc = build_program(S=4096)
    res = run_bass_kernel_spmd(nc, in_maps, core_ids=list(range(8)))
    out = np.stack([np.asarray(r["out"], dtype=np.float32)
                    for r in res.results], axis=0)
    return out


# version 15
# speedup vs baseline: 1.0954x; 1.0954x over previous
import numpy as np
from contextlib import ExitStack

import concourse.bass as bass
import concourse.mybir as mybir
from concourse.bass_utils import run_bass_kernel_spmd

F32 = mybir.dt.float32
BF16 = mybir.dt.bfloat16
AF = mybir.ActivationFunctionType
ALU = mybir.AluOpType
AX = mybir.AxisListType

D = 1024
H = 16
NE = 16
INTER = 512
ROUTE_SCALE = 2.5
EPS = 1e-5
NEG = -1.0e9

ENGS = ("tensor", "scalar", "vector", "gpsimd", "sync")


class Walk:

    def __init__(self, name, sem_names):
        self.name = name
        self.eng = None
        self.sems = None
        self.c = {k: 0 for k in sem_names}
        self._lastw = {}

    def _emit(self, owner, fn, incs):
        inst = fn(self.eng) if self.name == owner else None
        for sem, n in incs:
            self.c[sem] += n
            if inst is not None:
                inst.then_inc(self.sems[sem], n)

    def t(self, fn, incs=()):
        self._emit("tensor", fn, incs)

    def s(self, fn, incs=()):
        self._emit("scalar", fn, incs)

    def v(self, fn, incs=()):
        self._emit("vector", fn, incs)

    def g(self, fn, incs=()):
        self._emit("gpsimd", fn, incs)

    def y(self, fn, incs=()):
        self._emit("sync", fn, incs)

    def wait(self, eng_name, sem, val):
        if val <= 0 or self.name != eng_name:
            return
        if self._lastw.get(sem, 0) >= val:
            return
        self._lastw[sem] = val
        self.eng.wait_ge(self.sems[sem], val)

    def drain(self, eng_name):
        if self.name == eng_name:
            self.eng.drain()


def build_program(S=4096, taps=()):
    taps = set(taps)
    NQB = S // 512
    NKT = S // 128
    NCH = S // 512
    NPT = S // 128
    QTR = S // 4
    NQT = NPT // 4
    BCH = min(512, QTR)
    NBCH = QTR // BCH
    NST = BCH // 128
    SQW = 2048

    nc = bass.Bass("TRN2", target_bir_lowering=False)

    d_x = nc.dram_tensor("x", [S, D], F32, kind="ExternalInput").ap()
    d_xT = nc.dram_tensor("xT", [D, S], F32, kind="ExternalInput").ap()
    d_wqT = nc.dram_tensor("wqT", [D, D], F32, kind="ExternalInput").ap()
    d_wkT = nc.dram_tensor("wkT", [D, D], F32, kind="ExternalInput").ap()
    d_wvT = nc.dram_tensor("wvT", [D, D], F32, kind="ExternalInput").ap()
    d_woT = nc.dram_tensor("woT", [D, D], F32, kind="ExternalInput").ap()
    d_gwT = nc.dram_tensor("gwT", [D, NE], F32, kind="ExternalInput").ap()
    d_gb = nc.dram_tensor("gb", [NE, 1], F32, kind="ExternalInput").ap()
    d_gbT = nc.dram_tensor("gbT", [1, NE], F32, kind="ExternalInput").ap()
    d_w13T = nc.dram_tensor("w13T", [NE, D, D], F32, kind="ExternalInput").ap()
    d_w2T = nc.dram_tensor("w2T", [NE, INTER, D], F32, kind="ExternalInput").ap()
    d_w13sT = nc.dram_tensor("w13sT", [D, D], F32, kind="ExternalInput").ap()
    d_w2sT = nc.dram_tensor("w2sT", [INTER, D], F32, kind="ExternalInput").ap()
    d_ident = nc.dram_tensor("ident", [128, 128], F32, kind="ExternalInput").ap()
    d_ltri = nc.dram_tensor("ltri", [128, 128], F32, kind="ExternalInput").ap()
    d_ones = nc.dram_tensor("ones", [1, 128], F32, kind="ExternalInput").ap()
    d_ones2w = nc.dram_tensor("ones2w", [128, 128], F32, kind="ExternalInput").ap()
    d_ones64 = nc.dram_tensor("ones64", [128, 64], F32, kind="ExternalInput").ap()

    d_out = nc.dram_tensor("out", [S, D], F32, kind="ExternalOutput").ap()
    d_x1h = nc.dram_tensor("x1h", [S, D], F32, kind="Internal").ap()

    tap_t = {}
    for name, shape, dt in (
        ("tap_qt", [128, 4, S], BF16), ("tap_kt", [128, 4, S], BF16),
        ("tap_v", [128, NKT, 8, 65], BF16), ("tap_ytn", [2, 128, 4, S], BF16),
        ("tap_x1", [S, D], F32), ("tap_comb", [128, NPT, NE], F32),
        ("tap_x1t", [128, 8, S], BF16), ("tap_negm", [128, 2], F32),
        ("tap_yacc", [128, NQT, D], F32),
    ):
        if name in taps:
            tap_t[name] = nc.dram_tensor(name, shape, dt,
                                         kind="ExternalOutput").ap()

    sem_names = ("MM", "SC", "VE", "CONSTG", "CONSTS", "WQKV", "XTB",
                 "WO", "XRES0", "XRES1", "XRES2", "X1F0", "X1F1", "GW",
                 "GBS", "XTB2",
                 "W13A", "W13B", "X2RES0", "X2RES1", "OUT0", "OUT1",
                 "YST0", "YST1", "TAP")
    walks = {n: Walk(n, sem_names) for n in ENGS}

    st_persist = ExitStack()
    E = st_persist.enter_context
    semd = {n: E(nc.semaphore(n)) for n in sem_names}
    for wk in walks.values():
        wk.sems = semd

    c_id_b = E(nc.sbuf_tensor("c_id_b", [128, 128], BF16))
    c_id_f = E(nc.sbuf_tensor("c_id_f", [128, 128], F32))
    c_ones_f = E(nc.sbuf_tensor("c_ones_f", [1, 128], F32))
    c_o64_f = E(nc.sbuf_tensor("c_o64_f", [128, 64], F32))
    negm_b = E(nc.sbuf_tensor("negm_b", [128, 2], F32))

    st_ytn = ExitStack()
    yTn = [st_ytn.enter_context(nc.sbuf_tensor(f"yTn{g}", [128, 4, S], BF16))
           for g in range(2)]

    st_a = ExitStack()
    EA = st_a.enter_context
    c_ltri_b = EA(nc.sbuf_tensor("c_ltri_b", [128, 128], BF16))
    c_o64_b = EA(nc.sbuf_tensor("c_o64_b", [128, 64], BF16))
    c_o2w_b = EA(nc.sbuf_tensor("c_o2w_b", [128, 128], BF16))
    xTb = EA(nc.sbuf_tensor("xTb", [128, 8, 512], BF16))
    xTb2 = yTn[1][:, 0, :].rearrange("p (a b) -> p a b", a=8)
    xtbs = [xTb[:, :, :], xTb2]
    wq_b = EA(nc.sbuf_tensor("wq_b", [128, 8, 512], BF16))
    wk_b = EA(nc.sbuf_tensor("wk_b", [128, 8, 512], BF16))
    wv_b = EA(nc.sbuf_tensor("wv_b", [128, 8, 512], BF16))
    QT = EA(nc.sbuf_tensor("QT", [128, 4, S], BF16))
    KT = EA(nc.sbuf_tensor("KT", [128, 4, S], BF16))
    Vb = EA(nc.sbuf_tensor("Vb", [128, NKT, 8, 65], BF16))
    sq2 = EA(nc.sbuf_tensor("sq2", [128, SQW], BF16))
    ptb = [[sq2[:, (2 * i + e) * 512:(2 * i + e + 1) * 512]
            for i in range(2)] for e in range(2)]
    ptb3 = [[ptb[e][0], ptb[e][1], xTb[:, e, :]] for e in range(2)]
    mxq = EA(nc.sbuf_tensor("mxq", [128, 1], F32))
    mxk = EA(nc.sbuf_tensor("mxk", [128, 1], F32))
    mxt = EA(nc.sbuf_tensor("mxt", [128, 1], F32))
    sc1 = EA(nc.sbuf_tensor("sc1", [1, 4], F32))
    rdf = EA(nc.sbuf_tensor("rdf", [65, 1024], BF16))
    yev = EA(nc.sbuf_tensor("yev", [65, 2, 512], BF16))
    rb_t = EA(nc.sbuf_tensor("rb_t", [64, 512], F32))
    rb_sb = rb_t[:, :]
    yst = [EA(nc.sbuf_tensor("yst0", [64, 512], BF16))]
    p_sq = EA(nc.psum_tensor("p_sq", [128, 2048], F32))
    p_s = [p_sq[:, i * 512:(i + 1) * 512] for i in range(4)]
    p_y = [EA(nc.psum_tensor(f"p_y{e}", [65, 512], F32)) for e in range(2)]
    p_rb = EA(nc.psum_tensor("p_rb", [64, 512], F32))

    def emit_A(w: Walk):
        w.g(lambda g: g.dma_start(out=c_id_b[:, :], in_=d_ident),
            [("CONSTG", 16)])
        w.g(lambda g: g.dma_start(out=c_ltri_b[:, :], in_=d_ltri),
            [("CONSTG", 16)])
        w.g(lambda g: g.dma_start(out=c_o64_b[:, :], in_=d_ones64),
            [("CONSTG", 16)])
        w.g(lambda g: g.dma_start(out=c_o2w_b[:, :], in_=d_ones2w),
            [("CONSTG", 16)])
        gv_consts = w.c["CONSTG"]
        w.y(lambda y: y.dma_start(out=c_id_f[:, :], in_=d_ident),
            [("CONSTS", 16)])
        w.y(lambda y: y.dma_start(out=c_ones_f[:, :], in_=d_ones),
            [("CONSTS", 16)])
        w.y(lambda y: y.dma_start(out=c_o64_f[:, :], in_=d_ones64),
            [("CONSTS", 16)])
        sv_consts = w.c["CONSTS"]
        w.v(lambda v: v.memset(Vb[:, :, :, 64:65], 1.0), [("VE", 1)])
        ve_ones = w.c["VE"]

        wT_loaded = [0, 0]
        xtb_rd2 = [0, 0]
        wslice_rd = 0
        vb_rd_prev = 0
        qt_rd_prev = 0
        sq2_rd4 = [0, 0, 0, 0]
        ps_wr = [0] * 4
        ps_rd = [("SC", 0)] * 4
        py_rd = [0, 0]
        ptb_rd = [[0, 0, 0], [0, 0, 0]]
        ptb_ve = [[0, 0, 0], [0, 0, 0]]
        rb_rd = 0
        ti_rd = 0
        yst_st = [0, 0]
        yev_rd = 0

        for g in range(2):
            gsl = slice(g * 512, (g + 1) * 512)
            w.wait("gpsimd", "MM", wslice_rd)
            for dsrc, dst in ((d_wqT, wq_b), (d_wkT, wk_b), (d_wvT, wv_b)):
                w.g(lambda gg, dsrc=dsrc, dst=dst: gg.dma_start(
                    out=dst[:, :, :],
                    in_=dsrc.rearrange("(cc p) n -> p cc n", p=128)[:, :, gsl]),
                    [("WQKV", 16)])
            wT_loaded[g] = w.c["WQKV"]

            for ch in range(NCH):
                cb = ch % 2
                w.wait("gpsimd", "MM", xtb_rd2[cb])
                xts = "XTB" if cb == 0 else "XTB2"
                w.g(lambda gg, ch=ch, cb=cb, xts=xts: gg.dma_start(
                    out=xtbs[cb],
                    in_=d_xT.rearrange("(cc p) n -> p cc n", p=128)
                    [:, :, ch * 512:(ch + 1) * 512]), [(xts, 16)])
                gv_x = w.c[xts]
                w.wait("tensor", xts, gv_x)
                w.wait("tensor", "WQKV", wT_loaded[g])
                chs = slice(ch * 512, (ch + 1) * 512)
                for wi, (wbuf, dstT) in enumerate(((wq_b, QT), (wk_b, KT))):
                    for dt in range(4):
                        pb = (wi * 4 + dt) % 4
                        sem, cnt = ps_rd[pb]
                        w.wait("tensor", sem, cnt)
                        for cc in range(8):
                            w.t(lambda t, pb=pb, wbuf=wbuf, dt=dt, cc=cc,
                                cb=cb: t.matmul(
                                    p_s[pb],
                                    wbuf[:, cc, dt * 128:(dt + 1) * 128],
                                    xtbs[cb][:, cc, :],
                                    start=(cc == 0), stop=(cc == 7)),
                                [("MM", 1)] if cc == 7 else ())
                        mmv = w.c["MM"]
                        ps_wr[pb] = mmv
                        w.wait("vector", "MM", max(mmv, qt_rd_prev))
                        w.v(lambda v, pb=pb, dstT=dstT, dt=dt, chs=chs:
                            v.tensor_copy(dstT[:, dt, chs], p_s[pb]),
                            [("VE", 1)])
                        ps_rd[pb] = ("VE", w.c["VE"])
                for pt in range(4):
                    pb = pt % 4
                    sem, cnt = ps_rd[pb]
                    w.wait("tensor", sem, cnt)
                    for cc in range(8):
                        w.t(lambda t, pb=pb, cc=cc, pt=pt, cb=cb:
                            t.matmul(p_s[pb],
                                     xtbs[cb][:, cc, pt * 128:(pt + 1) * 128],
                                     wv_b[:, cc, :],
                                     start=(cc == 0), stop=(cc == 7)),
                            [("MM", 1)] if cc == 7 else ())
                    mmv = w.c["MM"]
                    ps_wr[pb] = mmv
                    kt_i = ch * 4 + pt
                    w.wait("vector", "VE", ve_ones)
                    w.wait("vector", "MM", max(mmv, vb_rd_prev))
                    w.v(lambda v, pb=pb, kt_i=kt_i:
                        v.tensor_copy(Vb[:, kt_i, 0:8, 0:64],
                                      p_s[pb].rearrange(
                                          "p (h d) -> p h d", d=64)),
                        [("VE", 1)])
                    ps_rd[pb] = ("VE", w.c["VE"])
                ptmax = max(ptb_rd[0] + ptb_rd[1])
                w.wait("tensor", "CONSTG", gv_consts)

                def norm_mm(w, item):
                    sqsl, nb, scv, mx, first = item
                    sem, cnt = ps_rd[nb]
                    w.wait("tensor", sem, cnt)
                    w.wait("tensor", "SC", scv)
                    w.t(lambda t, sqsl=sqsl, nb=nb: t.matmul(
                        p_s[nb], c_o2w_b[:, :], sq2[:, sqsl],
                        start=True, stop=True), [("MM", 1)])
                    ps_wr[nb] = w.c["MM"]
                    sq2_rd4[nb] = w.c["MM"]
                    w.wait("vector", "MM", ps_wr[nb])
                    w.v(lambda v, nb=nb: v.tensor_reduce(
                        mxt[:, :], p_s[nb], AX.X, ALU.max), [("VE", 1)])
                    ps_rd[nb] = ("VE", w.c["VE"])
                    w.drain("vector")
                    if first:
                        w.v(lambda v, mx=mx: v.tensor_copy(
                            mx[:, :], mxt[:, :]), [("VE", 1)])
                    else:
                        w.v(lambda v, mx=mx: v.tensor_tensor(
                            mx[:, :], mx[:, :], mxt[:, :], ALU.max),
                            [("VE", 1)])
                    w.drain("vector")

                pend_nm = None
                for si, (nsrc, mx) in enumerate(((QT, mxq), (KT, mxk))):
                    for pr in range(4):
                        nb = (si * 4 + pr) % 4
                        sqsl = slice(nb * 512, nb * 512 + 512)
                        w.wait("scalar", "MM", max(sq2_rd4[nb], ptmax))
                        w.wait("scalar", "VE", w.c["VE"])
                        w.s(lambda s, nsrc=nsrc, pr=pr, chs=chs, sqsl=sqsl:
                            s.activation(
                                sq2[:, sqsl], nsrc[:, pr, chs],
                                AF.Square), [("SC", 1)])
                        scv = w.c["SC"]
                        if pend_nm is not None:
                            norm_mm(w, pend_nm)
                        pend_nm = (sqsl, nb, scv, mx, ch == 0 and pr == 0)
                norm_mm(w, pend_nm)
                xtb_rd2[cb] = w.c["MM"]
            wslice_rd = w.c["MM"]
            ve_a1 = w.c["VE"]
            w.v(lambda v: v.tensor_tensor(sc1[:, 2:3], mxq[0:1, 0:1],
                                          mxk[0:1, 0:1], ALU.mult),
                [("VE", 1)])
            ve_m = w.c["VE"]
            w.wait("scalar", "VE", ve_m)
            w.s(lambda s: s.activation(sc1[:, 3:4], sc1[:, 2:3], AF.Ln),
                [("SC", 1)])
            w.drain("scalar")
            w.s(lambda s: s.activation(sc1[:, 2:3], sc1[:, 3:4], AF.Exp,
                                       scale=0.5), [("SC", 1)])
            scv_m = w.c["SC"]
            w.wait("vector", "SC", scv_m)
            w.drain("vector")
            w.v(lambda v: v.tensor_scalar_mul(sc1[:, 3:4], sc1[:, 2:3],
                                              -0.125), [("VE", 1)])
            ve_negm = w.c["VE"]
            sem0, cnt0 = ps_rd[0]
            w.wait("tensor", sem0, cnt0)
            w.wait("tensor", "VE", max(ve_negm, ti_rd))
            w.wait("tensor", "CONSTS", sv_consts)
            w.t(lambda t: t.matmul(p_sq[:, 0:1], c_ones_f[:, :], sc1[:, 3:4],
                                   start=True, stop=True), [("MM", 1)])
            mm_ti = w.c["MM"]
            w.wait("vector", "MM", mm_ti)
            w.v(lambda v, g=g: v.tensor_copy(negm_b[:, g:g + 1], p_sq[:, 0:1]),
                [("VE", 1)])
            ti_rd = w.c["VE"]
            ps_rd[0] = ("VE", w.c["VE"])
            ve_negm_done = w.c["VE"]

            w.wait("tensor", "CONSTG", gv_consts)
            pending_tail = [None]

            def flush_tail(w):
                if pending_tail[0] is None:
                    return
                fn = pending_tail[0]
                pending_tail[0] = None
                fn(w)

            for pr in range(4):
                for qb in range(NQB):
                    kts = 4 * qb + 4
                    qsl = slice(qb * 512, (qb + 1) * 512)
                    exp_v = [[("SC", 0)] * 3, [("SC", 0)] * 3]
                    for kt in range(kts + 2):
                        if kt == 4:
                            flush_tail(w)
                        if kt < kts:
                            j = kt - 4 * qb
                            i2 = kt % 2
                            i3 = kt % 3
                            qs = 128 * j if j > 0 else 0
                            for e in range(2):
                                sem, cnt = ps_rd[2 * i2 + e]
                                w.wait("tensor", sem, cnt)
                            w.wait("tensor", "VE", ve_a1)
                            for e in range(2):
                                hp = 64 * e
                                w.t(lambda t, i2=i2, e=e, hp=hp, pr=pr,
                                    kt=kt, qb=qb, qs=qs: t.matmul(
                                        p_s[2 * i2 + e][:, qs:512],
                                        KT[hp:hp + 64, pr,
                                           kt * 128:(kt + 1) * 128],
                                        QT[hp:hp + 64, pr,
                                           qb * 512 + qs:(qb + 1) * 512],
                                        start=True, stop=True,
                                        tile_position=(hp, 0)), [("MM", 1)])
                                ps_wr[2 * i2 + e] = w.c["MM"]
                            w.wait("scalar", "MM",
                                   max(ps_wr[2 * i2], ps_wr[2 * i2 + 1],
                                       ptb_rd[0][i3], ptb_rd[1][i3],
                                       max(sq2_rd4)))
                            w.wait("scalar", "VE",
                                   max(ve_negm_done, ptb_ve[0][i3],
                                       ptb_ve[1][i3]))
                            if qs == 0 and i3 < 2:
                                w.s(lambda s, i3=i3, i2=i2, g=g:
                                    s.activation(
                                        sq2[:, 2 * i3 * 512:
                                            (2 * i3 + 2) * 512],
                                        p_sq[:, 2 * i2 * 512:
                                             (2 * i2 + 2) * 512],
                                        AF.Exp, bias=negm_b[:, g:g + 1],
                                        scale=0.125), [("SC", 1)])
                            elif qs == 0:
                                w.s(lambda s, i2=i2, g=g: s.activation(
                                    xTb[:, 0:2, :].rearrange(
                                        "p a b -> p (a b)"),
                                    p_sq[:, 2 * i2 * 512:
                                         (2 * i2 + 2) * 512],
                                    AF.Exp, bias=negm_b[:, g:g + 1],
                                    scale=0.125), [("SC", 1)])
                            elif i3 < 2:
                                w.s(lambda s, i3=i3, i2=i2, g=g, qs=qs:
                                    s.activation(
                                        sq2[:, 2 * i3 * 512 + qs:
                                            (2 * i3 + 2) * 512],
                                        p_sq[:, 2 * i2 * 512 + qs:
                                             (2 * i2 + 2) * 512],
                                        AF.Exp, bias=negm_b[:, g:g + 1],
                                        scale=0.125), [("SC", 1)])
                            else:
                                w.s(lambda s, i2=i2, g=g, qs=qs:
                                    s.activation(
                                        xTb[:, 0:2, :].rearrange(
                                            "p a b -> p (a b)")[:, qs:1024],
                                        p_sq[:, 2 * i2 * 512 + qs:
                                             (2 * i2 + 2) * 512],
                                        AF.Exp, bias=negm_b[:, g:g + 1],
                                        scale=0.125), [("SC", 1)])
                            sc_exp = w.c["SC"]
                            ps_rd[2 * i2] = ("SC", sc_exp)
                            ps_rd[2 * i2 + 1] = ("SC", sc_exp)
                            if j >= 0:
                                w.wait("vector", "SC", sc_exp)
                                w.wait("vector", "CONSTG", gv_consts)
                                for e in range(2):
                                    w.v(lambda v, i3=i3, e=e, qs=qs:
                                        v.tensor_tensor(
                                            ptb3[e][i3][:, qs:qs + 128],
                                            ptb3[e][i3][:, qs:qs + 128],
                                            c_ltri_b[:, :], ALU.mult),
                                        [("VE", 1)])
                                    ptb_ve[e][i3] = w.c["VE"]
                                exp_v[0][i3] = ("VE", w.c["VE"])
                                exp_v[1][i3] = ("VE", w.c["VE"])
                            else:
                                exp_v[0][i3] = ("SC", sc_exp)
                                exp_v[1][i3] = ("SC", sc_exp)
                        if kt > 1:
                            pkt = kt - 2
                            pj = pkt - 4 * qb
                            pqs = 128 * pj if pj > 0 else 0
                            for e in range(2):
                                hg = 2 * pr + e
                                esem, ecnt = exp_v[e][pkt % 3]
                                w.wait("tensor", esem, ecnt)
                                if pkt == 0:
                                    w.wait("tensor", "VE", py_rd[e])
                                w.t(lambda t, e=e, pkt=pkt, hg=hg, kts=kts,
                                    pqs=pqs: t.matmul(
                                        p_y[e][:, pqs:512],
                                        Vb[:, pkt, hg, 0:65],
                                        ptb3[e][pkt % 3][:, pqs:512],
                                        start=(pkt == 0),
                                        stop=(pkt == kts - 1)),
                                    [("MM", 1)])
                                ptb_rd[e][pkt % 3] = w.c["MM"]
                    mm_pv = w.c["MM"]
                    w.wait("vector", "MM", mm_pv)
                    w.wait("vector", "VE", yev_rd)
                    w.drain("vector")
                    for e in range(2):
                        def _evac(v, e=e):
                            with nc.allow_low_precision(
                                    reason="bf16 attention-out operand"):
                                return v.tensor_copy(yev[:, e, :],
                                                     p_y[e][:, :])
                        w.v(_evac, [("VE", 1)])
                        py_rd[e] = w.c["VE"]
                    w.drain("vector")
                    for e in range(2):
                        def _recip(v, e=e):
                            with nc.allow_low_precision(
                                    reason="bf16 denom bcast operand"):
                                return v.reciprocal(
                                    rdf[64:65, e * 512:(e + 1) * 512],
                                    yev[64:65, e, :])
                        w.v(_recip, [("VE", 1)])
                    ve_rd2 = w.c["VE"]

                    def tail(w, g=g, pr=pr, qsl=qsl, ve_rd2=ve_rd2):
                        nonlocal rb_rd, yev_rd
                        ysl = 0
                        for e in range(2):
                            w.wait("tensor", "VE", max(ve_rd2, rb_rd))
                            w.wait("tensor", "CONSTG", gv_consts)
                            w.t(lambda t, e=e: t.matmul(
                                p_rb[:, :], c_o64_b[64:65, :],
                                rdf[64:65, e * 512:(e + 1) * 512],
                                start=True, stop=True), [("MM", 1)])
                            mm_rb = w.c["MM"]
                            w.wait("vector", "MM", mm_rb)
                            w.drain("vector")
                            w.v(lambda v: v.tensor_copy(rb_sb, p_rb[:, :]),
                                [("VE", 1)])
                            rb_rd = w.c["VE"]
                            w.drain("vector")
                            if e == 0:
                                def _m0(v, g=g, pr=pr, qsl=qsl):
                                    with nc.allow_low_precision(
                                            reason="bf16 yT operand"):
                                        return v.tensor_tensor(
                                            yTn[g][0:64, pr, qsl],
                                            yev[0:64, 0, :], rb_sb,
                                            ALU.mult)
                                w.v(_m0, [("VE", 1)])
                                yev_rd = w.c["VE"]
                            else:
                                yss = "YST0" if ysl == 0 else "YST1"
                                w.wait("vector", yss, yst_st[ysl])

                                def _m1(v, ysl=ysl):
                                    with nc.allow_low_precision(
                                            reason="bf16 yT operand"):
                                        return v.tensor_tensor(
                                            yst[ysl][:, :],
                                            yev[0:64, 1, :], rb_sb,
                                            ALU.mult)
                                w.v(_m1, [("VE", 1)])
                                yev_rd = w.c["VE"]
                                ve_yst = w.c["VE"]
                                w.wait("sync", "VE", ve_yst)
                                w.y(lambda y, g=g, pr=pr, qsl=qsl, ysl=ysl,
                                    yss=yss: y.dma_start(
                                        out=yTn[g][64:128, pr, qsl],
                                        in_=yst[ysl][:, :]), [(yss, 16)])
                                yst_st[ysl] = w.c[yss]
                    pending_tail[0] = tail
                flush_tail(w)
            flush_tail(w)
            vb_rd_prev = w.c["MM"]
            qt_rd_prev = w.c["MM"]
            xtb_rd2[0] = w.c["MM"]
            xtb_rd2[1] = w.c["MM"]

        if tap_t:
            w.wait("sync", "VE", w.c["VE"])
            w.wait("sync", "MM", w.c["MM"])
            w.wait("sync", "CONSTG", gv_consts)
            w.wait("sync", "YST0", w.c["YST0"])
            w.wait("sync", "YST1", w.c["YST1"])
        if "tap_qt" in tap_t:
            w.y(lambda y: y.dma_start(out=tap_t["tap_qt"], in_=QT[:, :, :]),
                [("TAP", 16)])
        if "tap_kt" in tap_t:
            w.y(lambda y: y.dma_start(out=tap_t["tap_kt"], in_=KT[:, :, :]),
                [("TAP", 16)])
        if "tap_v" in tap_t:
            w.y(lambda y: y.dma_start(out=tap_t["tap_v"], in_=Vb[:, :, :, :]),
                [("TAP", 16)])
        if "tap_negm" in tap_t:
            w.y(lambda y: y.dma_start(out=tap_t["tap_negm"], in_=negm_b[:, :]),
                [("TAP", 16)])
        if "tap_ytn" in tap_t:
            w.y(lambda y: y.dma_start(out=tap_t["tap_ytn"][0],
                                      in_=yTn[0][:, :, :]), [("TAP", 16)])
            w.y(lambda y: y.dma_start(out=tap_t["tap_ytn"][1],
                                      in_=yTn[1][:, :, :]), [("TAP", 16)])
        if w.name == "sync":
            w.eng.wait_ge(w.sems["YST0"], w.c["YST0"])
            w.eng.wait_ge(w.sems["YST1"], w.c["YST1"])
            w.eng.wait_ge(w.sems["TAP"], w.c["TAP"])

    def run_block(blk, emit):
        for en in ENGS:
            def cb(e, en=en):
                walks[en].eng = e
                emit(walks[en])
            getattr(blk, en)(cb)

    with nc.Block("blkA") as blkA:
        run_block(blkA, emit_A)

    st_a.close()

    st_x1t = ExitStack()
    x1T = st_x1t.enter_context(
        nc.sbuf_tensor("x1T", [128, 8, S], BF16, side="right"))
    st_a3 = ExitStack()
    E3 = st_a3.enter_context
    woT_b = E3(nc.sbuf_tensor("woT_b", [128, 8, 1024], BF16))
    xres = [E3(nc.sbuf_tensor(f"xres{i}", [128, D], F32)) for i in range(3)]
    x1pre = [E3(nc.sbuf_tensor(f"x1pre{i}", [128, D], F32)) for i in range(2)]
    x1f = [E3(nc.sbuf_tensor(f"x1f{i}", [128, D], F32)) for i in range(2)]
    x1bb = [E3(nc.sbuf_tensor(f"x1bb{i}", [128, D], BF16)) for i in range(3)]
    junk_b = E3(nc.sbuf_tensor("junk_b", [128, D], BF16))
    stt = E3(nc.sbuf_tensor("stt", [128, 16], F32))
    p_z = [E3(nc.psum_tensor(f"p_z{i}", [128, 512], F32)) for i in range(2)]
    p_t = [E3(nc.psum_tensor(f"p_t{i}", [128, 128], BF16)) for i in range(2)]

    def emit_A3(w: Walk):
        w.g(lambda g: g.dma_start(
            out=woT_b[:, :, :],
            in_=d_woT.rearrange("(cc p) n -> p cc n", p=128)), [("WO", 16)])
        gv_wo = w.c["WO"]
        xres_rd = [0, 0, 0]
        x1f_rd = [0, 0]
        x1b_rd = [0, 0]
        x1pre_rd = [0, 0]
        stt_rd = 0
        pz_rd = [0, 0]
        pt_rd = [0, 0]
        pend_tp = []
        x1b_rd3 = [0, 0, 0]
        for pt in range(NPT):
            bi = pt % 2
            ri = pt % 3
            psl = slice(pt * 128, (pt + 1) * 128)
            w.wait("sync", "VE", xres_rd[ri])
            xrs = ("XRES0", "XRES1", "XRES2")[ri]
            w.y(lambda y, psl=psl, ri=ri: y.dma_start(out=xres[ri][:, :],
                                                      in_=d_x[psl, :]),
                [(xrs, 16)])
            sv_x = w.c[xrs]
            w.wait("tensor", "WO", gv_wo)
            for half in range(2):
                w.wait("tensor", "VE", pz_rd[half])
                for cc in range(8):
                    g2, p2 = divmod(cc, 4)
                    w.t(lambda t, half=half, g2=g2, p2=p2, cc=cc, psl=psl:
                        t.matmul(p_z[half][:, :],
                                 yTn[g2][:, p2, psl],
                                 woT_b[:, cc, half * 512:(half + 1) * 512],
                                 start=(cc == 0), stop=(cc == 7)),
                        [("MM", 1)] if cc == 7 else ())
            mm_z = w.c["MM"]
            w.wait("vector", "MM", mm_z)
            w.wait("vector", xrs, sv_x)
            for half in range(2):
                w.v(lambda v, half=half, bi=bi, ri=ri: v.tensor_tensor(
                    x1pre[bi][:, half * 512:(half + 1) * 512],
                    p_z[half][:, :],
                    xres[ri][:, half * 512:(half + 1) * 512], ALU.add),
                    [("VE", 1)])
                pz_rd[half] = w.c["VE"]
            xres_rd[ri] = w.c["VE"]
            ve_pre = w.c["VE"]
            w.drain("vector")
            w.v(lambda v, bi=bi: v.bn_stats(stt[:, 0:6],
                                            x1pre[bi][:, 0:512]),
                [("VE", 1)])
            w.drain("vector")
            w.v(lambda v, bi=bi: v.bn_stats(stt[:, 6:12],
                                            x1pre[bi][:, 512:1024]),
                [("VE", 1)])
            w.drain("vector")
            w.wait("vector", "SC", stt_rd)
            w.v(lambda v: v.bn_aggr(stt[:, 12:14], stt[:, 0:12]),
                [("VE", 1)])
            w.drain("vector")
            w.v(lambda v: v.tensor_scalar(stt[:, 14:15], stt[:, 13:14],
                                          1.0, EPS, ALU.mult, ALU.add),
                [("VE", 1)])
            ve_var = w.c["VE"]
            w.wait("scalar", "VE", ve_var)
            w.s(lambda s: s.activation(stt[:, 15:16], stt[:, 14:15], AF.Ln),
                [("SC", 1)])
            w.drain("scalar")
            w.s(lambda s: s.activation(stt[:, 14:15], stt[:, 15:16], AF.Exp,
                                       scale=-0.5), [("SC", 1)])
            sc_rstd = w.c["SC"]
            stt_rd = sc_rstd
            w.wait("vector", "SC", sc_rstd)
            w.wait("vector", "X1F0" if bi == 0 else "X1F1", x1f_rd[bi])
            w.drain("vector")
            w.v(lambda v, bi=bi: v.tensor_scalar(x1f[bi][:, :],
                                                 x1pre[bi][:, :],
                                                 stt[:, 12:13], stt[:, 14:15],
                                                 ALU.subtract, ALU.mult),
                [("VE", 1)])
            ve_x1 = w.c["VE"]
            ti3 = pt % 3
            w.drain("vector")
            w.wait("vector", "MM", x1b_rd3[ti3])
            w.v(lambda v, ti3=ti3, bi=bi: v.tensor_copy(x1bb[ti3][:, :],
                                                        x1f[bi][:, :]),
                [("VE", 1)])
            ve_x1b = w.c["VE"]
            w.wait("sync", "VE", ve_x1)
            x1s = "X1F0" if bi == 0 else "X1F1"
            w.y(lambda y, bi=bi, psl=psl, x1s=x1s: y.dma_start(
                out=d_x1h[psl, :], in_=x1f[bi][:, :]), [(x1s, 16)])
            x1f_rd[bi] = w.c[x1s]
            pend_tp.append((pt, ve_x1b))
            if len(pend_tp) > 2:
                tpt, tve = pend_tp.pop(0)
                tb3 = tpt % 3
                for cc in range(8):
                    ti = cc % 2
                    w.wait("tensor", "VE", max(tve, pt_rd[ti]))
                    w.t(lambda t, ti=ti, tb3=tb3, cc=cc: t.transpose(
                        p_t[ti][:, :], x1bb[tb3][:, cc * 128:(cc + 1) * 128],
                        c_id_b[:, :]), [("MM", 1)])
                    mm_t = w.c["MM"]
                    if cc == 7:
                        x1b_rd3[tb3] = mm_t
                    w.wait("vector", "MM", mm_t)
                    w.v(lambda v, cc=cc, tpt=tpt, ti=ti: v.tensor_copy(
                        x1T[:, cc, tpt * 128:(tpt + 1) * 128], p_t[ti][:, :]),
                        [("VE", 1)])
                    pt_rd[ti] = w.c["VE"]
        while pend_tp:
            tpt, tve = pend_tp.pop(0)
            tb3 = tpt % 3
            for cc in range(8):
                ti = cc % 2
                w.wait("tensor", "VE", max(tve, pt_rd[ti]))
                w.t(lambda t, ti=ti, tb3=tb3, cc=cc: t.transpose(
                    p_t[ti][:, :], x1bb[tb3][:, cc * 128:(cc + 1) * 128],
                    c_id_b[:, :]), [("MM", 1)])
                mm_t = w.c["MM"]
                if cc == 7:
                    x1b_rd3[tb3] = mm_t
                w.wait("vector", "MM", mm_t)
                w.v(lambda v, cc=cc, tpt=tpt, ti=ti: v.tensor_copy(
                    x1T[:, cc, tpt * 128:(tpt + 1) * 128], p_t[ti][:, :]),
                    [("VE", 1)])
                pt_rd[ti] = w.c["VE"]
        if "tap_x1" in tap_t:
            w.wait("sync", "X1F0", x1f_rd[0])
            w.wait("sync", "X1F1", x1f_rd[1])
            w.y(lambda y: y.dma_start(out=tap_t["tap_x1"], in_=d_x1h),
                [("TAP", 16)])
        if "tap_x1t" in tap_t:
            w.wait("sync", "VE", w.c["VE"])
            w.y(lambda y: y.dma_start(out=tap_t["tap_x1t"], in_=x1T[:, :, :]),
                [("TAP", 16)])
        if w.name == "sync":
            w.eng.wait_ge(w.sems["X1F0"], w.c["X1F0"])
            w.eng.wait_ge(w.sems["X1F1"], w.c["X1F1"])
            w.eng.wait_ge(w.sems["TAP"], w.c["TAP"])

    with nc.Block("blkA3") as blkA3:
        run_block(blkA3, emit_A3)

    st_a3.close()
    st_ytn.close()

    st_b = ExitStack()
    EB = st_b.enter_context
    y_acc = EB(nc.sbuf_tensor("y_acc", [128, NQT, D], F32))
    w13_b = [EB(nc.sbuf_tensor(f"w13_b{i}", [128, 8, 1024], BF16))
             for i in range(2)]
    w2_b = [EB(nc.sbuf_tensor(f"w2_b{i}", [128, 4, 1024], BF16))
            for i in range(2)]
    hT = [EB(nc.sbuf_tensor(f"hT{i}", [128, 4, BCH], BF16)) for i in range(2)]
    sg_b = EB(nc.sbuf_tensor("sg_b", [128, 512], BF16))
    gw_b = EB(nc.sbuf_tensor("gw_b", [128, 8, NE], BF16))
    gb_sb = EB(nc.sbuf_tensor("gb_sb", [NE, 1], F32))
    gbT_sb = EB(nc.sbuf_tensor("gbT_sb", [1, NE], F32))
    biT = EB(nc.sbuf_tensor("biT", [NE, S], F32))
    bi_pm = EB(nc.sbuf_tensor("bi_pm", [128, NPT, NE], F32))
    gbb = EB(nc.sbuf_tensor("gbb", [128, NE], F32))
    sp_pm = EB(nc.sbuf_tensor("sp_pm", [128, NE], F32))
    top8 = EB(nc.sbuf_tensor("top8", [128, 8], F32))
    msk_b = EB(nc.sbuf_tensor("msk_b", [128, NE], F32))
    wsel = EB(nc.sbuf_tensor("wsel", [128, NE], F32))
    dnr = EB(nc.sbuf_tensor("dnr", [128, 1], F32))
    comb = EB(nc.sbuf_tensor("comb", [128, NPT, NE], F32))
    x2res = [EB(nc.sbuf_tensor(f"x2res{i}", [128, D], F32)) for i in range(2)]
    x2pre = EB(nc.sbuf_tensor("x2pre", [128, D], F32))
    outf = [EB(nc.sbuf_tensor(f"outf{i}", [128, D], F32)) for i in range(2)]
    junk2 = EB(nc.sbuf_tensor("junk2", [128, D], BF16))
    st2 = EB(nc.sbuf_tensor("st2", [128, 16], F32))
    p_h = [EB(nc.psum_tensor(f"p_h{i}", [128, 512], F32)) for i in range(4)]
    p_zb = [EB(nc.psum_tensor(f"p_zb{i}", [128, 512], F32)) for i in range(2)]
    p_r16 = EB(nc.psum_tensor("p_r16", [128, 512], F32))
    p_tb = p_r16[:, 32:48]
    p_tb2 = [p_r16[:, 64:80], p_r16[:, 80:96]]

    NEXP = NE + 1
    NGI = 4 * NEXP

    def exp_dma(w, e, bslot):
        ws = "W13A" if bslot == 0 else "W13B"
        if e < NE:
            w.g(lambda g, e=e, bslot=bslot: g.dma_start(
                out=w13_b[bslot][:, :, :],
                in_=d_w13T.rearrange("e (cc p) n -> e p cc n", p=128)[e]),
                [(ws, 16)])
            w.g(lambda g, e=e, bslot=bslot: g.dma_start(
                out=w2_b[bslot][:, :, :],
                in_=d_w2T.rearrange("e (cc p) n -> e p cc n", p=128)[e]),
                [(ws, 16)])
        else:
            w.g(lambda g, bslot=bslot: g.dma_start(
                out=w13_b[bslot][:, :, :],
                in_=d_w13sT.rearrange("(cc p) n -> p cc n", p=128)),
                [(ws, 16)])
            w.g(lambda g, bslot=bslot: g.dma_start(
                out=w2_b[bslot][:, :, :],
                in_=d_w2sT.rearrange("(cc p) n -> p cc n", p=128)),
                [(ws, 16)])
        return w.c[ws]

    def emit_B(w: Walk):
        w.g(lambda g: g.dma_start(
            out=gw_b[:, :, :],
            in_=d_gwT.rearrange("(cc p) n -> p cc n", p=128)), [("GW", 16)])
        gv_gw = w.c["GW"]
        w.y(lambda y: y.dma_start(out=gb_sb[:, :], in_=d_gb), [("GBS", 16)])
        w.y(lambda y: y.dma_start(out=gbT_sb[:, :], in_=d_gbT), [("GBS", 16)])
        sv_gb = w.c["GBS"]
        r16_rd = 0
        for ch in range(NCH):
            chs = slice(ch * 512, (ch + 1) * 512)
            w.wait("tensor", "GW", gv_gw)
            w.wait("tensor", "SC", r16_rd)
            for cc in range(8):
                w.t(lambda t, cc=cc, chs=chs: t.matmul(
                    p_r16[0:NE, :], gw_b[:, cc, :], x1T[:, cc, chs],
                    start=(cc == 0), stop=(cc == 7)),
                    [("MM", 1)] if cc == 7 else ())
            mm_r = w.c["MM"]
            w.wait("scalar", "MM", mm_r)
            w.s(lambda s, chs=chs: s.activation(biT[:, chs], p_r16[0:NE, :],
                                                AF.Sigmoid), [("SC", 1)])
            r16_rd = w.c["SC"]
        sc_router = w.c["SC"]
        w.wait("tensor", "GBS", sv_gb)
        w.wait("tensor", "SC", sc_router)
        w.t(lambda t: t.matmul(p_tb, c_ones_f[:, :], gbT_sb[:, :],
                               start=True, stop=True), [("MM", 1)])
        mm_gbb = w.c["MM"]
        w.wait("vector", "MM", mm_gbb)
        w.v(lambda v: v.tensor_copy(gbb[:, :], p_tb), [("VE", 1)])
        ptb_rd = w.c["VE"]
        ptb_rd2 = [ptb_rd, ptb_rd]
        w.wait("vector", "SC", sc_router)
        w.wait("vector", "GBS", sv_gb)
        w.v(lambda v: v.tensor_scalar(biT[:, :], biT[:, :], gb_sb[:, :], None,
                                      ALU.add), [("VE", 1)])
        ve_bi = w.c["VE"]

        def route_tiles(w, pts):
            nonlocal ptb_rd
            for pt in pts:
                tb = pt % 2
                w.wait("tensor", "VE", max(ve_bi, ptb_rd2[tb]))
                w.t(lambda t, pt=pt, tb=tb: t.transpose(
                    p_tb2[tb], biT[:, pt * 128:(pt + 1) * 128],
                    c_id_f[0:16, 0:16]), [("MM", 1)])
                mm_t = w.c["MM"]
                w.wait("vector", "MM", mm_t)
                w.drain("vector")
                w.v(lambda v, pt=pt, tb=tb: v.tensor_copy(bi_pm[:, pt, :],
                                                          p_tb2[tb]),
                    [("VE", 1)])
                ptb_rd2[tb] = w.c["VE"]
                ptb_rd = w.c["VE"]
                w.drain("vector")
                w.v(lambda v, pt=pt: v.tensor_tensor(
                    sp_pm[:, :], bi_pm[:, pt, :], gbb[:, :], ALU.subtract),
                    [("VE", 1)])
                w.drain("vector")
                w.v(lambda v, pt=pt: v.max(out=top8[:, :],
                                           in_=bi_pm[:, pt, :]), [("VE", 1)])
                w.drain("vector")
                w.v(lambda v, pt=pt: v.tensor_scalar(
                    msk_b[:, :], bi_pm[:, pt, :], top8[:, 3:4], None,
                    ALU.is_ge), [("VE", 1)])
                w.drain("vector")
                w.v(lambda v: v.tensor_tensor(wsel[:, :], sp_pm[:, :],
                                              msk_b[:, :], ALU.mult),
                    [("VE", 1)])
                w.drain("vector")
                w.v(lambda v: v.tensor_reduce(dnr[:, :], wsel[:, :], AX.X,
                                              ALU.add), [("VE", 1)])
                w.drain("vector")
                w.v(lambda v: v.reciprocal(dnr[:, :], dnr[:, :]),
                    [("VE", 1)])
                w.drain("vector")
                w.v(lambda v, pt=pt: v.tensor_scalar(
                    comb[:, pt, :], wsel[:, :], dnr[:, :], ROUTE_SCALE,
                    ALU.mult, ALU.mult), [("VE", 1)])

        route_tiles(w, range(0, NQT))

        wload = [0, 0]
        w13_rd = [0, 0]
        hT_rd = [0, 0]
        ph_rd = [("SC", 0), ("VE", 0), ("SC", 0), ("VE", 0)]
        pz_rd = [0, 0]
        sg_rd = 0
        x2res_rd = [0, 0]
        outf_rd = [0, 0]
        x2pre_rd = 0
        st2_rd = 0
        tap_yg = [0]
        wload[0] = exp_dma(w, 0, 0)
        for gi in range(NGI):
            qt_i, e = divmod(gi, NEXP)
            bslot = gi % 2
            if e == 0 and qt_i + 1 < 4:
                route_tiles(w, range((qt_i + 1) * NQT, (qt_i + 2) * NQT))
            if gi + 1 < NGI:
                nslot = (gi + 1) % 2
                w.wait("gpsimd", "MM", w13_rd[nslot])
                wload[nslot] = exp_dma(w, (gi + 1) % NEXP, nslot)
            def emit_z(w, ch, ve_h, bslot=bslot, qt_i=qt_i, e=e):
                nonlocal pz_rd, hT_rd, w13_rd
                w.drain("vector")
                for pst in range(NST):
                    t_i = ch * NST + pst
                    pt_g = qt_i * NQT + t_i
                    for co in range(2):
                        zb = co
                        w.wait("tensor", "VE", max(ve_h, pz_rd[zb]))
                        for iop in range(4):
                            w.t(lambda t, zb=zb, iop=iop, ch=ch, pst=pst,
                                bslot=bslot, co=co: t.matmul(
                                    p_zb[zb][:, :],
                                    hT[ch % 2][:, iop,
                                               pst * 128:(pst + 1) * 128],
                                    w2_b[bslot][:, iop,
                                                co * 512:(co + 1) * 512],
                                    start=(iop == 0), stop=(iop == 3)),
                                [("MM", 1)] if iop == 3 else ())
                        mm_zb = w.c["MM"]
                        hT_rd[ch % 2] = mm_zb
                        w13_rd[bslot] = mm_zb
                        w.wait("vector", "MM", mm_zb)
                        w.wait("vector", "TAP", tap_yg[0])
                        csl = slice(co * 512, (co + 1) * 512)
                        if e == 0:
                            w.v(lambda v, t_i=t_i, csl=csl, zb=zb, pt_g=pt_g:
                                v.tensor_scalar(y_acc[:, t_i, csl],
                                                p_zb[zb][:, :],
                                                comb[:, pt_g, 0:1], None,
                                                ALU.mult), [("VE", 1)])
                        elif e < NE:
                            w.v(lambda v, t_i=t_i, csl=csl, zb=zb, pt_g=pt_g,
                                e=e: v.scalar_tensor_tensor(
                                    y_acc[:, t_i, csl], p_zb[zb][:, :],
                                    comb[:, pt_g, e:e + 1],
                                    y_acc[:, t_i, csl],
                                    ALU.mult, ALU.add), [("VE", 1)])
                        else:
                            w.v(lambda v, t_i=t_i, csl=csl, zb=zb:
                                v.scalar_tensor_tensor(
                                    y_acc[:, t_i, csl], p_zb[zb][:, :], 1.0,
                                    y_acc[:, t_i, csl],
                                    ALU.mult, ALU.add), [("VE", 1)])
                        pz_rd[zb] = w.c["VE"]

            pend_z = None
            for ch in range(NBCH):
                gpos = qt_i * QTR + ch * BCH
                gsl = slice(gpos, gpos + BCH)
                w.wait("tensor", "W13A" if bslot == 0 else "W13B",
                       wload[bslot])
                if ch > 0:
                    w.wait("tensor", "MM", hT_rd[ch % 2])
                for iop in range(4):
                    pa = 2 * (iop % 2)
                    pb_ = pa + 1
                    for pslot, iotile in ((pa, iop), (pb_, 4 + iop)):
                        sem, cnt = ph_rd[pslot]
                        w.wait("tensor", sem, cnt)
                        for cc in range(8):
                            w.t(lambda t, pslot=pslot, cc=cc, iotile=iotile,
                                bslot=bslot, gsl=gsl: t.matmul(
                                    p_h[pslot][:, 0:BCH],
                                    w13_b[bslot][:, cc,
                                                 iotile * 128:
                                                 (iotile + 1) * 128],
                                    x1T[:, cc, gsl],
                                    start=(cc == 0), stop=(cc == 7)),
                                [("MM", 1)] if cc == 7 else ())
                    mm_gu = w.c["MM"]
                    w.wait("scalar", "MM", mm_gu)
                    w.wait("scalar", "VE", sg_rd)
                    w.s(lambda s, pa=pa: s.activation(sg_b[:, 0:BCH],
                                                      p_h[pa][:, 0:BCH],
                                                      AF.Silu), [("SC", 1)])
                    sc_silu = w.c["SC"]
                    ph_rd[pa] = ("SC", sc_silu)
                    w.wait("vector", "SC", sc_silu)
                    w.wait("vector", "MM", hT_rd[ch % 2])
                    w.v(lambda v, iop=iop, ch=ch, pb_=pb_: v.tensor_tensor(
                        hT[ch % 2][:, iop, :], sg_b[:, 0:BCH],
                        p_h[pb_][:, 0:BCH], ALU.mult), [("VE", 1)])
                    sg_rd = w.c["VE"]
                    ph_rd[pb_] = ("VE", w.c["VE"])
                ve_h = w.c["VE"]
                if pend_z is not None:
                    emit_z(w, pend_z[0], pend_z[1])
                pend_z = (ch, ve_h)
            emit_z(w, pend_z[0], pend_z[1])
            if e != NE:
                continue
            if "tap_comb" in tap_t and qt_i == 3 and e == NE:
                w.wait("sync", "VE", w.c["VE"])
                w.y(lambda y: y.dma_start(out=tap_t["tap_comb"],
                                          in_=comb[:, :, :]), [("TAP", 16)])
            if "tap_yacc" in tap_t and qt_i == 0:
                w.wait("sync", "VE", w.c["VE"])
                w.y(lambda y: y.dma_start(out=tap_t["tap_yacc"],
                                          in_=y_acc[:, :, :]), [("TAP", 16)])
                tap_yg[0] = w.c["TAP"]
            w.drain("vector")
            for t_i in range(NQT):
                bi = t_i % 2
                pt_g = qt_i * NQT + t_i
                psl = slice(pt_g * 128, (pt_g + 1) * 128)
                w.wait("sync", "VE", x2res_rd[bi])
                x2s = "X2RES0" if bi == 0 else "X2RES1"
                w.y(lambda y, psl=psl, bi=bi, x2s=x2s: y.dma_start(
                    out=x2res[bi][:, :], in_=d_x1h[psl, :]), [(x2s, 16)])
                sv_r = w.c[x2s]
                w.wait("vector", x2s, sv_r)
                w.wait("vector", "SC", x2pre_rd)
                w.v(lambda v, t_i=t_i, bi=bi: v.tensor_tensor(
                    x2pre[:, :], y_acc[:, t_i, :], x2res[bi][:, :], ALU.add),
                    [("VE", 1)])
                x2res_rd[bi] = w.c["VE"]
                ve_pre = w.c["VE"]
                w.drain("vector")
                w.v(lambda v: v.bn_stats(st2[:, 0:6], x2pre[:, 0:512]),
                    [("VE", 1)])
                w.drain("vector")
                w.v(lambda v: v.bn_stats(st2[:, 6:12], x2pre[:, 512:1024]),
                    [("VE", 1)])
                w.drain("vector")
                w.wait("vector", "SC", st2_rd)
                w.v(lambda v: v.bn_aggr(st2[:, 12:14], st2[:, 0:12]),
                    [("VE", 1)])
                w.drain("vector")
                w.v(lambda v: v.tensor_scalar(st2[:, 14:15], st2[:, 13:14],
                                              1.0, EPS, ALU.mult, ALU.add),
                    [("VE", 1)])
                ve_var = w.c["VE"]
                w.wait("scalar", "VE", ve_var)
                w.s(lambda s: s.activation(st2[:, 15:16], st2[:, 14:15],
                                           AF.Ln), [("SC", 1)])
                w.drain("scalar")
                w.s(lambda s: s.activation(st2[:, 14:15], st2[:, 15:16],
                                           AF.Exp, scale=-0.5), [("SC", 1)])
                sc_rstd = w.c["SC"]
                st2_rd = sc_rstd
                x2pre_rd = sc_rstd
                w.wait("vector", "SC", sc_rstd)
                w.wait("vector", "OUT0" if bi == 0 else "OUT1", outf_rd[bi])
                w.drain("vector")
                w.v(lambda v, bi=bi: v.tensor_scalar(outf[bi][:, :],
                                                     x2pre[:, :],
                                                     st2[:, 12:13],
                                                     st2[:, 14:15],
                                                     ALU.subtract, ALU.mult),
                    [("VE", 1)])
                ve_out = w.c["VE"]
                w.wait("sync", "VE", ve_out)
                ots = "OUT0" if bi == 0 else "OUT1"
                w.y(lambda y, bi=bi, psl=psl, ots=ots: y.dma_start(
                    out=d_out[psl, :], in_=outf[bi][:, :]), [(ots, 16)])
                outf_rd[bi] = w.c[ots]
        if w.name == "sync":
            w.eng.wait_ge(w.sems["OUT0"], w.c["OUT0"])
            w.eng.wait_ge(w.sems["OUT1"], w.c["OUT1"])
            w.eng.wait_ge(w.sems["TAP"], w.c["TAP"])

    with nc.Block("blkB") as blkB:
        run_block(blkB, emit_B)

    st_b.close()
    st_x1t.close()
    st_persist.close()

    base = walks["tensor"].c
    for n, wk in walks.items():
        assert wk.c == base, (n, wk.c, base)
    return nc


def _host_consts():
    ident = np.eye(128, dtype=np.float32)
    k = np.arange(128)[:, None]
    q = np.arange(128)[None, :]
    cmask = np.ascontiguousarray((q >= k).astype(np.float32))
    ones = np.ones((1, 128), dtype=np.float32)
    o2w = np.zeros((128, 128), dtype=np.float32)
    o2w[:, 0] = 1.0
    o64 = np.ones((128, 64), dtype=np.float32)
    return ident, cmask, ones, o2w, o64


def make_shared_inputs(L0_attn_W_Q, L0_attn_W_K, L0_attn_W_V, L0_attn_W_O,
                       L0_ffn_gate_w, L0_ffn_gate_b, L0_ffn_w13, L0_ffn_w2,
                       L0_ffn_w13_s, L0_ffn_w2_s):
    ident, cmask, ones, o2w, o64 = _host_consts()
    f32 = np.float32
    A = lambda a: np.ascontiguousarray(np.asarray(a, f32))
    return {
        "wqT": A(np.asarray(L0_attn_W_Q, f32).T),
        "wkT": A(np.asarray(L0_attn_W_K, f32).T),
        "wvT": A(np.asarray(L0_attn_W_V, f32).T),
        "woT": A(np.asarray(L0_attn_W_O, f32).T),
        "gwT": A(np.asarray(L0_ffn_gate_w, f32).T),
        "gb": A(np.asarray(L0_ffn_gate_b, f32)[:, None]),
        "gbT": A(np.asarray(L0_ffn_gate_b, f32)[None, :]),
        "w13T": A(np.asarray(L0_ffn_w13, f32).transpose(0, 2, 1)),
        "w2T": A(np.asarray(L0_ffn_w2, f32).transpose(0, 2, 1)),
        "w13sT": A(np.asarray(L0_ffn_w13_s, f32).T),
        "w2sT": A(np.asarray(L0_ffn_w2_s, f32).T),
        "ident": ident, "ltri": cmask, "ones": ones, "ones2w": o2w,
        "ones64": o64,
    }


def kernel(L0_attn_x, L0_attn_W_Q, L0_attn_W_K, L0_attn_W_V, L0_attn_W_O,
           L0_ffn_input_ids, L0_ffn_gate_w, L0_ffn_gate_b, L0_ffn_w13,
           L0_ffn_w2, L0_ffn_w13_s, L0_ffn_w2_s):
    shared = make_shared_inputs(L0_attn_W_Q, L0_attn_W_K, L0_attn_W_V,
                                L0_attn_W_O, L0_ffn_gate_w, L0_ffn_gate_b,
                                L0_ffn_w13, L0_ffn_w2, L0_ffn_w13_s,
                                L0_ffn_w2_s)
    xs = np.asarray(L0_attn_x, np.float32)
    in_maps = []
    for c in range(8):
        x = np.ascontiguousarray(xs[c])
        xT = np.ascontiguousarray(x.T)
        m = {"x": x, "xT": xT}
        m.update(shared)
        in_maps.append(m)
    nc = build_program(S=4096)
    res = run_bass_kernel_spmd(nc, in_maps, core_ids=list(range(8)))
    out = np.stack([np.asarray(r["out"], dtype=np.float32)
                    for r in res.results], axis=0)
    return out


# version 16
# speedup vs baseline: 1.1251x; 1.0271x over previous
import numpy as np
from contextlib import ExitStack

import concourse.bass as bass
import concourse.mybir as mybir
from concourse.bass_utils import run_bass_kernel_spmd

F32 = mybir.dt.float32
BF16 = mybir.dt.bfloat16
AF = mybir.ActivationFunctionType
ALU = mybir.AluOpType
AX = mybir.AxisListType

D = 1024
H = 16
NE = 16
INTER = 512
ROUTE_SCALE = 2.5
EPS = 1e-5
NEG = -1.0e9

ENGS = ("tensor", "scalar", "vector", "gpsimd", "sync")


class Walk:

    def __init__(self, name, sem_names):
        self.name = name
        self.eng = None
        self.sems = None
        self.c = {k: 0 for k in sem_names}
        self._lastw = {}

    def _emit(self, owner, fn, incs):
        inst = fn(self.eng) if self.name == owner else None
        for sem, n in incs:
            self.c[sem] += n
            if inst is not None:
                inst.then_inc(self.sems[sem], n)

    def t(self, fn, incs=()):
        self._emit("tensor", fn, incs)

    def s(self, fn, incs=()):
        self._emit("scalar", fn, incs)

    def v(self, fn, incs=()):
        self._emit("vector", fn, incs)

    def g(self, fn, incs=()):
        self._emit("gpsimd", fn, incs)

    def y(self, fn, incs=()):
        self._emit("sync", fn, incs)

    def wait(self, eng_name, sem, val):
        if val <= 0 or self.name != eng_name:
            return
        if self._lastw.get(sem, 0) >= val:
            return
        self._lastw[sem] = val
        self.eng.wait_ge(self.sems[sem], val)

    def drain(self, eng_name):
        if self.name == eng_name:
            self.eng.drain()


def build_program(S=4096, taps=()):
    taps = set(taps)
    NQB = S // 512
    NKT = S // 128
    NCH = S // 512
    NPT = S // 128
    QTR = S // 4
    NQT = NPT // 4
    BCH = min(512, QTR)
    NBCH = QTR // BCH
    NST = BCH // 128
    SQW = 2048

    nc = bass.Bass("TRN2", target_bir_lowering=False)

    d_x = nc.dram_tensor("x", [S, D], F32, kind="ExternalInput").ap()
    d_xT = nc.dram_tensor("xT", [D, S], F32, kind="ExternalInput").ap()
    d_wqT = nc.dram_tensor("wqT", [D, D], F32, kind="ExternalInput").ap()
    d_wkT = nc.dram_tensor("wkT", [D, D], F32, kind="ExternalInput").ap()
    d_wvT = nc.dram_tensor("wvT", [D, D], F32, kind="ExternalInput").ap()
    d_woT = nc.dram_tensor("woT", [D, D], F32, kind="ExternalInput").ap()
    d_gwT = nc.dram_tensor("gwT", [D, NE], F32, kind="ExternalInput").ap()
    d_gb = nc.dram_tensor("gb", [NE, 1], F32, kind="ExternalInput").ap()
    d_gbT = nc.dram_tensor("gbT", [1, NE], F32, kind="ExternalInput").ap()
    d_w13T = nc.dram_tensor("w13T", [NE, D, D], F32, kind="ExternalInput").ap()
    d_w2T = nc.dram_tensor("w2T", [NE, INTER, D], F32, kind="ExternalInput").ap()
    d_w13sT = nc.dram_tensor("w13sT", [D, D], F32, kind="ExternalInput").ap()
    d_w2sT = nc.dram_tensor("w2sT", [INTER, D], F32, kind="ExternalInput").ap()
    d_ident = nc.dram_tensor("ident", [128, 128], F32, kind="ExternalInput").ap()
    d_ltri = nc.dram_tensor("ltri", [128, 128], F32, kind="ExternalInput").ap()
    d_ones = nc.dram_tensor("ones", [1, 128], F32, kind="ExternalInput").ap()
    d_ones2w = nc.dram_tensor("ones2w", [128, 128], F32, kind="ExternalInput").ap()
    d_ones64 = nc.dram_tensor("ones64", [128, 64], F32, kind="ExternalInput").ap()

    d_out = nc.dram_tensor("out", [S, D], F32, kind="ExternalOutput").ap()
    d_x1h = nc.dram_tensor("x1h", [S, D], F32, kind="Internal").ap()

    tap_t = {}
    for name, shape, dt in (
        ("tap_qt", [128, 4, S], BF16), ("tap_kt", [128, 4, S], BF16),
        ("tap_v", [128, NKT, 8, 65], BF16), ("tap_ytn", [2, 128, 4, S], BF16),
        ("tap_x1", [S, D], F32), ("tap_comb", [128, NPT, NE], F32),
        ("tap_x1t", [128, 8, S], BF16), ("tap_negm", [128, 2], F32),
        ("tap_yacc", [128, NQT, D], F32),
    ):
        if name in taps:
            tap_t[name] = nc.dram_tensor(name, shape, dt,
                                         kind="ExternalOutput").ap()

    sem_names = ("MM", "SC", "VE", "CONSTG", "CONSTS", "WQKV", "XTB",
                 "WO", "XRES0", "XRES1", "XRES2", "X1F0", "X1F1", "GW",
                 "GBS", "XTB2",
                 "W13A", "W13B", "X2RES0", "X2RES1", "OUT0", "OUT1",
                 "YST0", "YST1", "TAP")
    walks = {n: Walk(n, sem_names) for n in ENGS}

    st_persist = ExitStack()
    E = st_persist.enter_context
    semd = {n: E(nc.semaphore(n)) for n in sem_names}
    for wk in walks.values():
        wk.sems = semd

    c_id_b = E(nc.sbuf_tensor("c_id_b", [128, 128], BF16))
    c_id_f = E(nc.sbuf_tensor("c_id_f", [128, 128], F32))
    c_ones_f = E(nc.sbuf_tensor("c_ones_f", [1, 128], F32))
    c_o64_f = E(nc.sbuf_tensor("c_o64_f", [128, 64], F32))
    negm_b = E(nc.sbuf_tensor("negm_b", [128, 2], F32))

    st_ytn = ExitStack()
    yTn = [st_ytn.enter_context(nc.sbuf_tensor(f"yTn{g}", [128, 4, S], BF16))
           for g in range(2)]

    st_a = ExitStack()
    EA = st_a.enter_context
    c_ltri_b = EA(nc.sbuf_tensor("c_ltri_b", [128, 128], BF16))
    c_o64_b = EA(nc.sbuf_tensor("c_o64_b", [128, 64], BF16))
    c_o2w_b = EA(nc.sbuf_tensor("c_o2w_b", [128, 128], BF16))
    xTb = EA(nc.sbuf_tensor("xTb", [128, 8, 512], BF16))
    xTb2 = yTn[1][:, 0, :].rearrange("p (a b) -> p a b", a=8)
    xtbs = [xTb[:, :, :], xTb2]
    wq_b = EA(nc.sbuf_tensor("wq_b", [128, 8, 512], BF16))
    wk_b = EA(nc.sbuf_tensor("wk_b", [128, 8, 512], BF16))
    wv_b = EA(nc.sbuf_tensor("wv_b", [128, 8, 512], BF16))
    QT = EA(nc.sbuf_tensor("QT", [128, 4, S], BF16))
    KT = EA(nc.sbuf_tensor("KT", [128, 4, S], BF16))
    Vb = EA(nc.sbuf_tensor("Vb", [128, NKT, 8, 65], BF16))
    sq2 = EA(nc.sbuf_tensor("sq2", [128, SQW], BF16))
    ptb = [[sq2[:, (2 * i + e) * 512:(2 * i + e + 1) * 512]
            for i in range(2)] for e in range(2)]
    ptb3 = [[ptb[e][0], ptb[e][1], xTb[:, e, :]] for e in range(2)]
    mxq = EA(nc.sbuf_tensor("mxq", [128, 1], F32))
    mxk = EA(nc.sbuf_tensor("mxk", [128, 1], F32))
    mxt = EA(nc.sbuf_tensor("mxt", [128, 1], F32))
    sc1 = EA(nc.sbuf_tensor("sc1", [1, 4], F32))
    rdf = EA(nc.sbuf_tensor("rdf", [65, 1024], BF16))
    yev = EA(nc.sbuf_tensor("yev", [65, 2, 512], BF16))
    rb_t = EA(nc.sbuf_tensor("rb_t", [64, 512], F32))
    rb_sb = rb_t[:, :]
    yst = [EA(nc.sbuf_tensor("yst0", [64, 512], BF16))]
    p_sq = EA(nc.psum_tensor("p_sq", [128, 2048], F32))
    p_s = [p_sq[:, i * 512:(i + 1) * 512] for i in range(4)]
    p_y = [EA(nc.psum_tensor(f"p_y{e}", [65, 512], F32)) for e in range(2)]
    p_rb = EA(nc.psum_tensor("p_rb", [64, 512], F32))

    def emit_A(w: Walk):
        w.g(lambda g: g.dma_start(out=c_id_b[:, :], in_=d_ident),
            [("CONSTG", 16)])
        w.g(lambda g: g.dma_start(out=c_ltri_b[:, :], in_=d_ltri),
            [("CONSTG", 16)])
        w.g(lambda g: g.dma_start(out=c_o64_b[:, :], in_=d_ones64),
            [("CONSTG", 16)])
        w.g(lambda g: g.dma_start(out=c_o2w_b[:, :], in_=d_ones2w),
            [("CONSTG", 16)])
        gv_consts = w.c["CONSTG"]
        w.y(lambda y: y.dma_start(out=c_id_f[:, :], in_=d_ident),
            [("CONSTS", 16)])
        w.y(lambda y: y.dma_start(out=c_ones_f[:, :], in_=d_ones),
            [("CONSTS", 16)])
        w.y(lambda y: y.dma_start(out=c_o64_f[:, :], in_=d_ones64),
            [("CONSTS", 16)])
        sv_consts = w.c["CONSTS"]
        w.v(lambda v: v.memset(Vb[:, :, :, 64:65], 1.0), [("VE", 1)])
        ve_ones = w.c["VE"]

        wT_loaded = [0, 0]
        xtb_rd2 = [0, 0]
        wslice_rd = 0
        vb_rd_prev = 0
        qt_rd_prev = 0
        sq2_rd4 = [0, 0, 0, 0]
        ps_wr = [0] * 4
        ps_rd = [("SC", 0)] * 4
        py_rd = [0, 0]
        ptb_rd = [[0, 0, 0], [0, 0, 0]]
        ptb_ve = [[0, 0, 0], [0, 0, 0]]
        rb_rd = 0
        ti_rd = 0
        yst_st = [0, 0]
        yev_rd = 0

        for g in range(2):
            gsl = slice(g * 512, (g + 1) * 512)
            w.wait("gpsimd", "MM", wslice_rd)
            for dsrc, dst in ((d_wqT, wq_b), (d_wkT, wk_b), (d_wvT, wv_b)):
                w.g(lambda gg, dsrc=dsrc, dst=dst: gg.dma_start(
                    out=dst[:, :, :],
                    in_=dsrc.rearrange("(cc p) n -> p cc n", p=128)[:, :, gsl]),
                    [("WQKV", 16)])
            wT_loaded[g] = w.c["WQKV"]

            for ch in range(NCH):
                cb = ch % 2
                w.wait("gpsimd", "MM", xtb_rd2[cb])
                xts = "XTB" if cb == 0 else "XTB2"
                w.g(lambda gg, ch=ch, cb=cb, xts=xts: gg.dma_start(
                    out=xtbs[cb],
                    in_=d_xT.rearrange("(cc p) n -> p cc n", p=128)
                    [:, :, ch * 512:(ch + 1) * 512]), [(xts, 16)])
                gv_x = w.c[xts]
                w.wait("tensor", xts, gv_x)
                w.wait("tensor", "WQKV", wT_loaded[g])
                chs = slice(ch * 512, (ch + 1) * 512)
                for wi, (wbuf, dstT) in enumerate(((wq_b, QT), (wk_b, KT))):
                    for dt in range(4):
                        pb = (wi * 4 + dt) % 4
                        sem, cnt = ps_rd[pb]
                        w.wait("tensor", sem, cnt)
                        for cc in range(8):
                            w.t(lambda t, pb=pb, wbuf=wbuf, dt=dt, cc=cc,
                                cb=cb: t.matmul(
                                    p_s[pb],
                                    wbuf[:, cc, dt * 128:(dt + 1) * 128],
                                    xtbs[cb][:, cc, :],
                                    start=(cc == 0), stop=(cc == 7)),
                                [("MM", 1)] if cc == 7 else ())
                        mmv = w.c["MM"]
                        ps_wr[pb] = mmv
                        w.wait("vector", "MM", max(mmv, qt_rd_prev))
                        w.v(lambda v, pb=pb, dstT=dstT, dt=dt, chs=chs:
                            v.tensor_copy(dstT[:, dt, chs], p_s[pb]),
                            [("VE", 1)])
                        ps_rd[pb] = ("VE", w.c["VE"])
                for pt in range(4):
                    pb = pt % 4
                    sem, cnt = ps_rd[pb]
                    w.wait("tensor", sem, cnt)
                    for cc in range(8):
                        w.t(lambda t, pb=pb, cc=cc, pt=pt, cb=cb:
                            t.matmul(p_s[pb],
                                     xtbs[cb][:, cc, pt * 128:(pt + 1) * 128],
                                     wv_b[:, cc, :],
                                     start=(cc == 0), stop=(cc == 7)),
                            [("MM", 1)] if cc == 7 else ())
                    mmv = w.c["MM"]
                    ps_wr[pb] = mmv
                    kt_i = ch * 4 + pt
                    w.wait("vector", "VE", ve_ones)
                    w.wait("vector", "MM", max(mmv, vb_rd_prev))
                    w.v(lambda v, pb=pb, kt_i=kt_i:
                        v.tensor_copy(Vb[:, kt_i, 0:8, 0:64],
                                      p_s[pb].rearrange(
                                          "p (h d) -> p h d", d=64)),
                        [("VE", 1)])
                    ps_rd[pb] = ("VE", w.c["VE"])
                ptmax = max(ptb_rd[0] + ptb_rd[1])
                w.wait("tensor", "CONSTG", gv_consts)

                def norm_mm(w, item):
                    sqsl, nb, scv, mx, first = item
                    sem, cnt = ps_rd[nb]
                    w.wait("tensor", sem, cnt)
                    w.wait("tensor", "SC", scv)
                    w.t(lambda t, sqsl=sqsl, nb=nb: t.matmul(
                        p_s[nb], c_o2w_b[:, :], sq2[:, sqsl],
                        start=True, stop=True), [("MM", 1)])
                    ps_wr[nb] = w.c["MM"]
                    sq2_rd4[nb] = w.c["MM"]
                    w.wait("vector", "MM", ps_wr[nb])
                    w.v(lambda v, nb=nb: v.tensor_reduce(
                        mxt[:, :], p_s[nb], AX.X, ALU.max), [("VE", 1)])
                    ps_rd[nb] = ("VE", w.c["VE"])
                    w.drain("vector")
                    if first:
                        w.v(lambda v, mx=mx: v.tensor_copy(
                            mx[:, :], mxt[:, :]), [("VE", 1)])
                    else:
                        w.v(lambda v, mx=mx: v.tensor_tensor(
                            mx[:, :], mx[:, :], mxt[:, :], ALU.max),
                            [("VE", 1)])
                    w.drain("vector")

                pend_nm = None
                for si, (nsrc, mx) in enumerate(((QT, mxq), (KT, mxk))):
                    for pr in range(4):
                        nb = (si * 4 + pr) % 4
                        sqsl = slice(nb * 512, nb * 512 + 512)
                        w.wait("scalar", "MM", max(sq2_rd4[nb], ptmax))
                        w.wait("scalar", "VE", w.c["VE"])
                        w.s(lambda s, nsrc=nsrc, pr=pr, chs=chs, sqsl=sqsl:
                            s.activation(
                                sq2[:, sqsl], nsrc[:, pr, chs],
                                AF.Square), [("SC", 1)])
                        scv = w.c["SC"]
                        if pend_nm is not None:
                            norm_mm(w, pend_nm)
                        pend_nm = (sqsl, nb, scv, mx, ch == 0 and pr == 0)
                norm_mm(w, pend_nm)
                xtb_rd2[cb] = w.c["MM"]
            wslice_rd = w.c["MM"]
            ve_a1 = w.c["VE"]
            w.v(lambda v: v.tensor_tensor(sc1[:, 2:3], mxq[0:1, 0:1],
                                          mxk[0:1, 0:1], ALU.mult),
                [("VE", 1)])
            ve_m = w.c["VE"]
            w.wait("scalar", "VE", ve_m)
            w.s(lambda s: s.activation(sc1[:, 3:4], sc1[:, 2:3], AF.Ln),
                [("SC", 1)])
            w.drain("scalar")
            w.s(lambda s: s.activation(sc1[:, 2:3], sc1[:, 3:4], AF.Exp,
                                       scale=0.5), [("SC", 1)])
            scv_m = w.c["SC"]
            w.wait("vector", "SC", scv_m)
            w.drain("vector")
            w.v(lambda v: v.tensor_scalar_mul(sc1[:, 3:4], sc1[:, 2:3],
                                              -0.125), [("VE", 1)])
            ve_negm = w.c["VE"]
            sem0, cnt0 = ps_rd[0]
            w.wait("tensor", sem0, cnt0)
            w.wait("tensor", "VE", max(ve_negm, ti_rd))
            w.wait("tensor", "CONSTS", sv_consts)
            w.t(lambda t: t.matmul(p_sq[:, 0:1], c_ones_f[:, :], sc1[:, 3:4],
                                   start=True, stop=True), [("MM", 1)])
            mm_ti = w.c["MM"]
            w.wait("vector", "MM", mm_ti)
            w.v(lambda v, g=g: v.tensor_copy(negm_b[:, g:g + 1], p_sq[:, 0:1]),
                [("VE", 1)])
            ti_rd = w.c["VE"]
            ps_rd[0] = ("VE", w.c["VE"])
            ve_negm_done = w.c["VE"]

            w.wait("tensor", "CONSTG", gv_consts)
            pending_tail = [None]

            def flush_tail(w):
                if pending_tail[0] is None:
                    return
                fn = pending_tail[0]
                pending_tail[0] = None
                fn(w)

            for pr in range(4):
                for qb in range(NQB):
                    kts = 4 * qb + 4
                    qsl = slice(qb * 512, (qb + 1) * 512)
                    exp_v = [[("SC", 0)] * 3, [("SC", 0)] * 3]
                    for kt in range(kts + 2):
                        if kt == min(6, kts):
                            flush_tail(w)
                        if kt < kts:
                            j = kt - 4 * qb
                            i2 = kt % 2
                            i3 = kt % 3
                            qs = 128 * j if j > 0 else 0
                            for e in range(2):
                                sem, cnt = ps_rd[2 * i2 + e]
                                w.wait("tensor", sem, cnt)
                            w.wait("tensor", "VE", ve_a1)
                            for e in range(2):
                                hp = 64 * e
                                w.t(lambda t, i2=i2, e=e, hp=hp, pr=pr,
                                    kt=kt, qb=qb, qs=qs: t.matmul(
                                        p_s[2 * i2 + e][:, qs:512],
                                        KT[hp:hp + 64, pr,
                                           kt * 128:(kt + 1) * 128],
                                        QT[hp:hp + 64, pr,
                                           qb * 512 + qs:(qb + 1) * 512],
                                        start=True, stop=True,
                                        tile_position=(hp, 0)), [("MM", 1)])
                                ps_wr[2 * i2 + e] = w.c["MM"]
                            w.wait("scalar", "MM",
                                   max(ps_wr[2 * i2], ps_wr[2 * i2 + 1],
                                       ptb_rd[0][i3], ptb_rd[1][i3],
                                       max(sq2_rd4)))
                            w.wait("scalar", "VE",
                                   max(ve_negm_done, ptb_ve[0][i3],
                                       ptb_ve[1][i3]))
                            if qs == 0 and i3 < 2:
                                w.s(lambda s, i3=i3, i2=i2, g=g:
                                    s.activation(
                                        sq2[:, 2 * i3 * 512:
                                            (2 * i3 + 2) * 512],
                                        p_sq[:, 2 * i2 * 512:
                                             (2 * i2 + 2) * 512],
                                        AF.Exp, bias=negm_b[:, g:g + 1],
                                        scale=0.125), [("SC", 1)])
                            elif qs == 0:
                                w.s(lambda s, i2=i2, g=g: s.activation(
                                    xTb[:, 0:2, :].rearrange(
                                        "p a b -> p (a b)"),
                                    p_sq[:, 2 * i2 * 512:
                                         (2 * i2 + 2) * 512],
                                    AF.Exp, bias=negm_b[:, g:g + 1],
                                    scale=0.125), [("SC", 1)])
                            elif i3 < 2:
                                w.s(lambda s, i3=i3, i2=i2, g=g, qs=qs:
                                    s.activation(
                                        sq2[:, 2 * i3 * 512 + qs:
                                            (2 * i3 + 2) * 512],
                                        p_sq[:, 2 * i2 * 512 + qs:
                                             (2 * i2 + 2) * 512],
                                        AF.Exp, bias=negm_b[:, g:g + 1],
                                        scale=0.125), [("SC", 1)])
                            else:
                                w.s(lambda s, i2=i2, g=g, qs=qs:
                                    s.activation(
                                        xTb[:, 0:2, :].rearrange(
                                            "p a b -> p (a b)")[:, qs:1024],
                                        p_sq[:, 2 * i2 * 512 + qs:
                                             (2 * i2 + 2) * 512],
                                        AF.Exp, bias=negm_b[:, g:g + 1],
                                        scale=0.125), [("SC", 1)])
                            sc_exp = w.c["SC"]
                            ps_rd[2 * i2] = ("SC", sc_exp)
                            ps_rd[2 * i2 + 1] = ("SC", sc_exp)
                            if j >= 0:
                                w.wait("vector", "SC", sc_exp)
                                w.wait("vector", "CONSTG", gv_consts)
                                for e in range(2):
                                    w.v(lambda v, i3=i3, e=e, qs=qs:
                                        v.tensor_tensor(
                                            ptb3[e][i3][:, qs:qs + 128],
                                            ptb3[e][i3][:, qs:qs + 128],
                                            c_ltri_b[:, :], ALU.mult),
                                        [("VE", 1)])
                                    ptb_ve[e][i3] = w.c["VE"]
                                exp_v[0][i3] = ("VE", w.c["VE"])
                                exp_v[1][i3] = ("VE", w.c["VE"])
                            else:
                                exp_v[0][i3] = ("SC", sc_exp)
                                exp_v[1][i3] = ("SC", sc_exp)
                        if kt > 1:
                            pkt = kt - 2
                            pj = pkt - 4 * qb
                            pqs = 128 * pj if pj > 0 else 0
                            for e in range(2):
                                hg = 2 * pr + e
                                esem, ecnt = exp_v[e][pkt % 3]
                                w.wait("tensor", esem, ecnt)
                                if pkt == 0:
                                    w.wait("tensor", "VE", py_rd[e])
                                w.t(lambda t, e=e, pkt=pkt, hg=hg, kts=kts,
                                    pqs=pqs: t.matmul(
                                        p_y[e][:, pqs:512],
                                        Vb[:, pkt, hg, 0:65],
                                        ptb3[e][pkt % 3][:, pqs:512],
                                        start=(pkt == 0),
                                        stop=(pkt == kts - 1)),
                                    [("MM", 1)])
                                ptb_rd[e][pkt % 3] = w.c["MM"]
                    mm_pv = w.c["MM"]
                    w.wait("vector", "MM", mm_pv)
                    w.wait("vector", "VE", yev_rd)
                    w.drain("vector")
                    for e in range(2):
                        def _evac(v, e=e):
                            with nc.allow_low_precision(
                                    reason="bf16 attention-out operand"):
                                return v.tensor_copy(yev[:, e, :],
                                                     p_y[e][:, :])
                        w.v(_evac, [("VE", 1)])
                        py_rd[e] = w.c["VE"]
                    w.drain("vector")
                    for e in range(2):
                        def _recip(v, e=e):
                            with nc.allow_low_precision(
                                    reason="bf16 denom bcast operand"):
                                return v.reciprocal(
                                    rdf[64:65, e * 512:(e + 1) * 512],
                                    yev[64:65, e, :])
                        w.v(_recip, [("VE", 1)])
                    ve_rd2 = w.c["VE"]

                    def tail(w, g=g, pr=pr, qsl=qsl, ve_rd2=ve_rd2):
                        nonlocal rb_rd, yev_rd
                        ysl = 0
                        for e in range(2):
                            w.wait("tensor", "VE", max(ve_rd2, rb_rd))
                            w.wait("tensor", "CONSTG", gv_consts)
                            w.t(lambda t, e=e: t.matmul(
                                p_rb[:, :], c_o64_b[64:65, :],
                                rdf[64:65, e * 512:(e + 1) * 512],
                                start=True, stop=True), [("MM", 1)])
                            mm_rb = w.c["MM"]
                            w.wait("vector", "MM", mm_rb)
                            w.drain("vector")
                            w.v(lambda v: v.tensor_copy(rb_sb, p_rb[:, :]),
                                [("VE", 1)])
                            rb_rd = w.c["VE"]
                            w.drain("vector")
                            if e == 0:
                                def _m0(v, g=g, pr=pr, qsl=qsl):
                                    with nc.allow_low_precision(
                                            reason="bf16 yT operand"):
                                        return v.tensor_tensor(
                                            yTn[g][0:64, pr, qsl],
                                            yev[0:64, 0, :], rb_sb,
                                            ALU.mult)
                                w.v(_m0, [("VE", 1)])
                                yev_rd = w.c["VE"]
                            else:
                                yss = "YST0" if ysl == 0 else "YST1"
                                w.wait("vector", yss, yst_st[ysl])

                                def _m1(v, ysl=ysl):
                                    with nc.allow_low_precision(
                                            reason="bf16 yT operand"):
                                        return v.tensor_tensor(
                                            yst[ysl][:, :],
                                            yev[0:64, 1, :], rb_sb,
                                            ALU.mult)
                                w.v(_m1, [("VE", 1)])
                                yev_rd = w.c["VE"]
                                ve_yst = w.c["VE"]
                                w.wait("sync", "VE", ve_yst)
                                w.y(lambda y, g=g, pr=pr, qsl=qsl, ysl=ysl,
                                    yss=yss: y.dma_start(
                                        out=yTn[g][64:128, pr, qsl],
                                        in_=yst[ysl][:, :]), [(yss, 16)])
                                yst_st[ysl] = w.c[yss]
                    pending_tail[0] = tail
                flush_tail(w)
            flush_tail(w)
            vb_rd_prev = w.c["MM"]
            qt_rd_prev = w.c["MM"]
            xtb_rd2[0] = w.c["MM"]
            xtb_rd2[1] = w.c["MM"]

        if tap_t:
            w.wait("sync", "VE", w.c["VE"])
            w.wait("sync", "MM", w.c["MM"])
            w.wait("sync", "CONSTG", gv_consts)
            w.wait("sync", "YST0", w.c["YST0"])
            w.wait("sync", "YST1", w.c["YST1"])
        if "tap_qt" in tap_t:
            w.y(lambda y: y.dma_start(out=tap_t["tap_qt"], in_=QT[:, :, :]),
                [("TAP", 16)])
        if "tap_kt" in tap_t:
            w.y(lambda y: y.dma_start(out=tap_t["tap_kt"], in_=KT[:, :, :]),
                [("TAP", 16)])
        if "tap_v" in tap_t:
            w.y(lambda y: y.dma_start(out=tap_t["tap_v"], in_=Vb[:, :, :, :]),
                [("TAP", 16)])
        if "tap_negm" in tap_t:
            w.y(lambda y: y.dma_start(out=tap_t["tap_negm"], in_=negm_b[:, :]),
                [("TAP", 16)])
        if "tap_ytn" in tap_t:
            w.y(lambda y: y.dma_start(out=tap_t["tap_ytn"][0],
                                      in_=yTn[0][:, :, :]), [("TAP", 16)])
            w.y(lambda y: y.dma_start(out=tap_t["tap_ytn"][1],
                                      in_=yTn[1][:, :, :]), [("TAP", 16)])
        if w.name == "sync":
            w.eng.wait_ge(w.sems["YST0"], w.c["YST0"])
            w.eng.wait_ge(w.sems["YST1"], w.c["YST1"])
            w.eng.wait_ge(w.sems["TAP"], w.c["TAP"])

    def run_block(blk, emit):
        for en in ENGS:
            def cb(e, en=en):
                walks[en].eng = e
                emit(walks[en])
            getattr(blk, en)(cb)

    with nc.Block("blkA") as blkA:
        run_block(blkA, emit_A)

    st_a.close()

    st_x1t = ExitStack()
    x1T = st_x1t.enter_context(
        nc.sbuf_tensor("x1T", [128, 8, S], BF16, side="right"))
    st_a3 = ExitStack()
    E3 = st_a3.enter_context
    woT_b = E3(nc.sbuf_tensor("woT_b", [128, 8, 1024], BF16))
    xres = [E3(nc.sbuf_tensor(f"xres{i}", [128, D], F32)) for i in range(3)]
    x1pre = [E3(nc.sbuf_tensor(f"x1pre{i}", [128, D], F32)) for i in range(2)]
    x1f = [E3(nc.sbuf_tensor(f"x1f{i}", [128, D], F32)) for i in range(2)]
    x1bb = [E3(nc.sbuf_tensor(f"x1bb{i}", [128, D], BF16)) for i in range(3)]
    junk_b = E3(nc.sbuf_tensor("junk_b", [128, D], BF16))
    stt = E3(nc.sbuf_tensor("stt", [128, 16], F32))
    p_z = [E3(nc.psum_tensor(f"p_z{i}", [128, 512], F32)) for i in range(2)]
    p_t = [E3(nc.psum_tensor(f"p_t{i}", [128, 128], BF16)) for i in range(4)]

    def emit_A3(w: Walk):
        w.g(lambda g: g.dma_start(
            out=woT_b[:, :, :],
            in_=d_woT.rearrange("(cc p) n -> p cc n", p=128)), [("WO", 16)])
        gv_wo = w.c["WO"]
        xres_rd = [0, 0, 0]
        x1f_rd = [0, 0]
        x1b_rd = [0, 0]
        x1pre_rd = [0, 0]
        stt_rd = 0
        pz_rd = [0, 0]
        pt_rd = [0, 0, 0, 0]
        pend_tp = []
        x1b_rd3 = [0, 0, 0]
        for pt in range(NPT):
            bi = pt % 2
            ri = pt % 3
            psl = slice(pt * 128, (pt + 1) * 128)
            w.wait("sync", "VE", xres_rd[ri])
            xrs = ("XRES0", "XRES1", "XRES2")[ri]
            w.y(lambda y, psl=psl, ri=ri: y.dma_start(out=xres[ri][:, :],
                                                      in_=d_x[psl, :]),
                [(xrs, 16)])
            sv_x = w.c[xrs]
            w.wait("tensor", "WO", gv_wo)
            for half in range(2):
                w.wait("tensor", "VE", pz_rd[half])
                for cc in range(8):
                    g2, p2 = divmod(cc, 4)
                    w.t(lambda t, half=half, g2=g2, p2=p2, cc=cc, psl=psl:
                        t.matmul(p_z[half][:, :],
                                 yTn[g2][:, p2, psl],
                                 woT_b[:, cc, half * 512:(half + 1) * 512],
                                 start=(cc == 0), stop=(cc == 7)),
                        [("MM", 1)] if cc == 7 else ())
            mm_z = w.c["MM"]
            w.wait("vector", "MM", mm_z)
            w.wait("vector", xrs, sv_x)
            for half in range(2):
                w.v(lambda v, half=half, bi=bi, ri=ri: v.tensor_tensor(
                    x1pre[bi][:, half * 512:(half + 1) * 512],
                    p_z[half][:, :],
                    xres[ri][:, half * 512:(half + 1) * 512], ALU.add),
                    [("VE", 1)])
                pz_rd[half] = w.c["VE"]
            xres_rd[ri] = w.c["VE"]
            ve_pre = w.c["VE"]
            w.drain("vector")
            w.v(lambda v, bi=bi: v.bn_stats(stt[:, 0:6],
                                            x1pre[bi][:, 0:512]),
                [("VE", 1)])
            w.drain("vector")
            w.v(lambda v, bi=bi: v.bn_stats(stt[:, 6:12],
                                            x1pre[bi][:, 512:1024]),
                [("VE", 1)])
            w.drain("vector")
            w.wait("vector", "SC", stt_rd)
            w.v(lambda v: v.bn_aggr(stt[:, 12:14], stt[:, 0:12]),
                [("VE", 1)])
            w.drain("vector")
            w.v(lambda v: v.tensor_scalar(stt[:, 14:15], stt[:, 13:14],
                                          1.0, EPS, ALU.mult, ALU.add),
                [("VE", 1)])
            ve_var = w.c["VE"]
            w.wait("scalar", "VE", ve_var)
            w.s(lambda s: s.activation(stt[:, 15:16], stt[:, 14:15], AF.Ln),
                [("SC", 1)])
            w.drain("scalar")
            w.s(lambda s: s.activation(stt[:, 14:15], stt[:, 15:16], AF.Exp,
                                       scale=-0.5), [("SC", 1)])
            sc_rstd = w.c["SC"]
            stt_rd = sc_rstd
            w.wait("vector", "SC", sc_rstd)
            w.wait("vector", "X1F0" if bi == 0 else "X1F1", x1f_rd[bi])
            w.drain("vector")
            w.v(lambda v, bi=bi: v.tensor_scalar(x1f[bi][:, :],
                                                 x1pre[bi][:, :],
                                                 stt[:, 12:13], stt[:, 14:15],
                                                 ALU.subtract, ALU.mult),
                [("VE", 1)])
            ve_x1 = w.c["VE"]
            ti3 = pt % 3
            w.drain("vector")
            w.wait("vector", "MM", x1b_rd3[ti3])
            w.v(lambda v, ti3=ti3, bi=bi: v.tensor_copy(x1bb[ti3][:, :],
                                                        x1f[bi][:, :]),
                [("VE", 1)])
            ve_x1b = w.c["VE"]
            w.wait("sync", "VE", ve_x1)
            x1s = "X1F0" if bi == 0 else "X1F1"
            w.y(lambda y, bi=bi, psl=psl, x1s=x1s: y.dma_start(
                out=d_x1h[psl, :], in_=x1f[bi][:, :]), [(x1s, 16)])
            x1f_rd[bi] = w.c[x1s]
            pend_tp.append((pt, ve_x1b))
            if len(pend_tp) > 2:
                tpt, tve = pend_tp.pop(0)
                tb3 = tpt % 3
                for cc in range(8):
                    ti = cc % 4
                    w.wait("tensor", "VE", max(tve, pt_rd[ti]))
                    w.t(lambda t, ti=ti, tb3=tb3, cc=cc: t.transpose(
                        p_t[ti][:, :], x1bb[tb3][:, cc * 128:(cc + 1) * 128],
                        c_id_b[:, :]), [("MM", 1)])
                    mm_t = w.c["MM"]
                    if cc == 7:
                        x1b_rd3[tb3] = mm_t
                    w.wait("vector", "MM", mm_t)
                    w.v(lambda v, cc=cc, tpt=tpt, ti=ti: v.tensor_copy(
                        x1T[:, cc, tpt * 128:(tpt + 1) * 128], p_t[ti][:, :]),
                        [("VE", 1)])
                    pt_rd[ti] = w.c["VE"]
        while pend_tp:
            tpt, tve = pend_tp.pop(0)
            tb3 = tpt % 3
            for cc in range(8):
                ti = cc % 2
                w.wait("tensor", "VE", max(tve, pt_rd[ti]))
                w.t(lambda t, ti=ti, tb3=tb3, cc=cc: t.transpose(
                    p_t[ti][:, :], x1bb[tb3][:, cc * 128:(cc + 1) * 128],
                    c_id_b[:, :]), [("MM", 1)])
                mm_t = w.c["MM"]
                if cc == 7:
                    x1b_rd3[tb3] = mm_t
                w.wait("vector", "MM", mm_t)
                w.v(lambda v, cc=cc, tpt=tpt, ti=ti: v.tensor_copy(
                    x1T[:, cc, tpt * 128:(tpt + 1) * 128], p_t[ti][:, :]),
                    [("VE", 1)])
                pt_rd[ti] = w.c["VE"]
        if "tap_x1" in tap_t:
            w.wait("sync", "X1F0", x1f_rd[0])
            w.wait("sync", "X1F1", x1f_rd[1])
            w.y(lambda y: y.dma_start(out=tap_t["tap_x1"], in_=d_x1h),
                [("TAP", 16)])
        if "tap_x1t" in tap_t:
            w.wait("sync", "VE", w.c["VE"])
            w.y(lambda y: y.dma_start(out=tap_t["tap_x1t"], in_=x1T[:, :, :]),
                [("TAP", 16)])
        if w.name == "sync":
            w.eng.wait_ge(w.sems["X1F0"], w.c["X1F0"])
            w.eng.wait_ge(w.sems["X1F1"], w.c["X1F1"])
            w.eng.wait_ge(w.sems["TAP"], w.c["TAP"])

    with nc.Block("blkA3") as blkA3:
        run_block(blkA3, emit_A3)

    st_a3.close()
    st_ytn.close()

    st_b = ExitStack()
    EB = st_b.enter_context
    y_acc = EB(nc.sbuf_tensor("y_acc", [128, NQT, D], F32))
    w13_b = [EB(nc.sbuf_tensor(f"w13_b{i}", [128, 8, 1024], BF16))
             for i in range(2)]
    w2_b = [EB(nc.sbuf_tensor(f"w2_b{i}", [128, 4, 1024], BF16))
            for i in range(2)]
    hT = [EB(nc.sbuf_tensor(f"hT{i}", [128, 4, BCH], BF16)) for i in range(2)]
    sg_b = EB(nc.sbuf_tensor("sg_b", [128, 512], BF16))
    gw_b = EB(nc.sbuf_tensor("gw_b", [128, 8, NE], BF16))
    gb_sb = EB(nc.sbuf_tensor("gb_sb", [NE, 1], F32))
    gbT_sb = EB(nc.sbuf_tensor("gbT_sb", [1, NE], F32))
    biT = EB(nc.sbuf_tensor("biT", [NE, S], F32))
    bi_pm = EB(nc.sbuf_tensor("bi_pm", [128, NPT, NE], F32))
    gbb = EB(nc.sbuf_tensor("gbb", [128, NE], F32))
    sp_pm = EB(nc.sbuf_tensor("sp_pm", [128, NE], F32))
    top8 = EB(nc.sbuf_tensor("top8", [128, 8], F32))
    msk_b = EB(nc.sbuf_tensor("msk_b", [128, NE], F32))
    wsel = EB(nc.sbuf_tensor("wsel", [128, NE], F32))
    dnr = EB(nc.sbuf_tensor("dnr", [128, 1], F32))
    comb = EB(nc.sbuf_tensor("comb", [128, NPT, NE], F32))
    x2res = [EB(nc.sbuf_tensor(f"x2res{i}", [128, D], F32)) for i in range(2)]
    x2pre = EB(nc.sbuf_tensor("x2pre", [128, D], F32))
    outf = [EB(nc.sbuf_tensor(f"outf{i}", [128, D], F32)) for i in range(2)]
    junk2 = EB(nc.sbuf_tensor("junk2", [128, D], BF16))
    st2 = EB(nc.sbuf_tensor("st2", [128, 16], F32))
    p_h = [EB(nc.psum_tensor(f"p_h{i}", [128, 512], F32)) for i in range(4)]
    p_zb = [EB(nc.psum_tensor(f"p_zb{i}", [128, 512], F32)) for i in range(2)]
    p_r16 = EB(nc.psum_tensor("p_r16", [128, 512], F32))
    p_rt2 = EB(nc.psum_tensor("p_rt2", [128, 16], F32))
    p_tb = p_r16[:, 32:48]
    p_tb2 = [p_r16[:, 64:80], p_rt2[:, :]]

    NEXP = NE + 1
    NGI = 4 * NEXP

    def exp_dma(w, e, bslot):
        ws = "W13A" if bslot == 0 else "W13B"
        if e < NE:
            w.g(lambda g, e=e, bslot=bslot: g.dma_start(
                out=w13_b[bslot][:, :, :],
                in_=d_w13T.rearrange("e (cc p) n -> e p cc n", p=128)[e]),
                [(ws, 16)])
            w.g(lambda g, e=e, bslot=bslot: g.dma_start(
                out=w2_b[bslot][:, :, :],
                in_=d_w2T.rearrange("e (cc p) n -> e p cc n", p=128)[e]),
                [(ws, 16)])
        else:
            w.g(lambda g, bslot=bslot: g.dma_start(
                out=w13_b[bslot][:, :, :],
                in_=d_w13sT.rearrange("(cc p) n -> p cc n", p=128)),
                [(ws, 16)])
            w.g(lambda g, bslot=bslot: g.dma_start(
                out=w2_b[bslot][:, :, :],
                in_=d_w2sT.rearrange("(cc p) n -> p cc n", p=128)),
                [(ws, 16)])
        return w.c[ws]

    def emit_B(w: Walk):
        w.g(lambda g: g.dma_start(
            out=gw_b[:, :, :],
            in_=d_gwT.rearrange("(cc p) n -> p cc n", p=128)), [("GW", 16)])
        gv_gw = w.c["GW"]
        w.y(lambda y: y.dma_start(out=gb_sb[:, :], in_=d_gb), [("GBS", 16)])
        w.y(lambda y: y.dma_start(out=gbT_sb[:, :], in_=d_gbT), [("GBS", 16)])
        sv_gb = w.c["GBS"]
        r16_rd = 0
        for ch in range(NCH):
            chs = slice(ch * 512, (ch + 1) * 512)
            w.wait("tensor", "GW", gv_gw)
            w.wait("tensor", "SC", r16_rd)
            for cc in range(8):
                w.t(lambda t, cc=cc, chs=chs: t.matmul(
                    p_r16[0:NE, :], gw_b[:, cc, :], x1T[:, cc, chs],
                    start=(cc == 0), stop=(cc == 7)),
                    [("MM", 1)] if cc == 7 else ())
            mm_r = w.c["MM"]
            w.wait("scalar", "MM", mm_r)
            w.s(lambda s, chs=chs: s.activation(biT[:, chs], p_r16[0:NE, :],
                                                AF.Sigmoid), [("SC", 1)])
            r16_rd = w.c["SC"]
        sc_router = w.c["SC"]
        w.wait("tensor", "GBS", sv_gb)
        w.wait("tensor", "SC", sc_router)
        w.t(lambda t: t.matmul(p_tb, c_ones_f[:, :], gbT_sb[:, :],
                               start=True, stop=True), [("MM", 1)])
        mm_gbb = w.c["MM"]
        w.wait("vector", "MM", mm_gbb)
        w.v(lambda v: v.tensor_copy(gbb[:, :], p_tb), [("VE", 1)])
        ptb_rd = w.c["VE"]
        ptb_rd2 = [ptb_rd, ptb_rd]
        w.wait("vector", "SC", sc_router)
        w.wait("vector", "GBS", sv_gb)
        w.v(lambda v: v.tensor_scalar(biT[:, :], biT[:, :], gb_sb[:, :], None,
                                      ALU.add), [("VE", 1)])
        ve_bi = w.c["VE"]

        def route_tiles(w, pts):
            nonlocal ptb_rd
            for pt in pts:
                tb = pt % 2
                w.wait("tensor", "VE", max(ve_bi, ptb_rd2[tb]))
                w.t(lambda t, pt=pt, tb=tb: t.transpose(
                    p_tb2[tb], biT[:, pt * 128:(pt + 1) * 128],
                    c_id_f[0:16, 0:16]), [("MM", 1)])
                mm_t = w.c["MM"]
                w.wait("vector", "MM", mm_t)
                w.drain("vector")
                w.v(lambda v, pt=pt, tb=tb: v.tensor_copy(bi_pm[:, pt, :],
                                                          p_tb2[tb]),
                    [("VE", 1)])
                ptb_rd2[tb] = w.c["VE"]
                ptb_rd = w.c["VE"]
                w.drain("vector")
                w.v(lambda v, pt=pt: v.tensor_tensor(
                    sp_pm[:, :], bi_pm[:, pt, :], gbb[:, :], ALU.subtract),
                    [("VE", 1)])
                w.drain("vector")
                w.v(lambda v, pt=pt: v.max(out=top8[:, :],
                                           in_=bi_pm[:, pt, :]), [("VE", 1)])
                w.drain("vector")
                w.v(lambda v, pt=pt: v.tensor_scalar(
                    msk_b[:, :], bi_pm[:, pt, :], top8[:, 3:4], None,
                    ALU.is_ge), [("VE", 1)])
                w.drain("vector")
                w.v(lambda v: v.tensor_tensor(wsel[:, :], sp_pm[:, :],
                                              msk_b[:, :], ALU.mult),
                    [("VE", 1)])
                w.drain("vector")
                w.v(lambda v: v.tensor_reduce(dnr[:, :], wsel[:, :], AX.X,
                                              ALU.add), [("VE", 1)])
                w.drain("vector")
                w.v(lambda v: v.reciprocal(dnr[:, :], dnr[:, :]),
                    [("VE", 1)])
                w.drain("vector")
                w.v(lambda v, pt=pt: v.tensor_scalar(
                    comb[:, pt, :], wsel[:, :], dnr[:, :], ROUTE_SCALE,
                    ALU.mult, ALU.mult), [("VE", 1)])

        route_tiles(w, range(0, NQT))

        wload = [0, 0]
        w13_rd = [0, 0]
        hT_rd = [0, 0]
        ph_rd = [("SC", 0), ("VE", 0), ("SC", 0), ("VE", 0)]
        pz_rd = [0, 0]
        sg_rd = 0
        x2res_rd = [0, 0]
        outf_rd = [0, 0]
        x2pre_rd = 0
        st2_rd = 0
        tap_yg = [0]
        wload[0] = exp_dma(w, 0, 0)
        for gi in range(NGI):
            qt_i, e = divmod(gi, NEXP)
            bslot = gi % 2
            if e == NEXP // 2 and qt_i + 1 < 4:
                route_tiles(w, range((qt_i + 1) * NQT, (qt_i + 2) * NQT))
            if gi + 1 < NGI:
                nslot = (gi + 1) % 2
                w.wait("gpsimd", "MM", w13_rd[nslot])
                wload[nslot] = exp_dma(w, (gi + 1) % NEXP, nslot)
            def emit_z(w, ch, ve_h, bslot=bslot, qt_i=qt_i, e=e):
                nonlocal pz_rd, hT_rd, w13_rd
                w.drain("vector")
                for pst in range(NST):
                    t_i = ch * NST + pst
                    pt_g = qt_i * NQT + t_i
                    for co in range(2):
                        zb = co
                        w.wait("tensor", "VE", max(ve_h, pz_rd[zb]))
                        for iop in range(4):
                            w.t(lambda t, zb=zb, iop=iop, ch=ch, pst=pst,
                                bslot=bslot, co=co: t.matmul(
                                    p_zb[zb][:, :],
                                    hT[ch % 2][:, iop,
                                               pst * 128:(pst + 1) * 128],
                                    w2_b[bslot][:, iop,
                                                co * 512:(co + 1) * 512],
                                    start=(iop == 0), stop=(iop == 3)),
                                [("MM", 1)] if iop == 3 else ())
                        mm_zb = w.c["MM"]
                        hT_rd[ch % 2] = mm_zb
                        w13_rd[bslot] = mm_zb
                        w.wait("vector", "MM", mm_zb)
                        w.wait("vector", "TAP", tap_yg[0])
                        csl = slice(co * 512, (co + 1) * 512)
                        if e == 0:
                            w.v(lambda v, t_i=t_i, csl=csl, zb=zb, pt_g=pt_g:
                                v.tensor_scalar(y_acc[:, t_i, csl],
                                                p_zb[zb][:, :],
                                                comb[:, pt_g, 0:1], None,
                                                ALU.mult), [("VE", 1)])
                        elif e < NE:
                            w.v(lambda v, t_i=t_i, csl=csl, zb=zb, pt_g=pt_g,
                                e=e: v.scalar_tensor_tensor(
                                    y_acc[:, t_i, csl], p_zb[zb][:, :],
                                    comb[:, pt_g, e:e + 1],
                                    y_acc[:, t_i, csl],
                                    ALU.mult, ALU.add), [("VE", 1)])
                        else:
                            w.v(lambda v, t_i=t_i, csl=csl, zb=zb:
                                v.scalar_tensor_tensor(
                                    y_acc[:, t_i, csl], p_zb[zb][:, :], 1.0,
                                    y_acc[:, t_i, csl],
                                    ALU.mult, ALU.add), [("VE", 1)])
                        pz_rd[zb] = w.c["VE"]

            pend_z = None
            for ch in range(NBCH):
                gpos = qt_i * QTR + ch * BCH
                gsl = slice(gpos, gpos + BCH)
                w.wait("tensor", "W13A" if bslot == 0 else "W13B",
                       wload[bslot])
                if ch > 0:
                    w.wait("tensor", "MM", hT_rd[ch % 2])
                for iop in range(4):
                    pa = 2 * (iop % 2)
                    pb_ = pa + 1
                    for pslot, iotile in ((pa, iop), (pb_, 4 + iop)):
                        sem, cnt = ph_rd[pslot]
                        w.wait("tensor", sem, cnt)
                        for cc in range(8):
                            w.t(lambda t, pslot=pslot, cc=cc, iotile=iotile,
                                bslot=bslot, gsl=gsl: t.matmul(
                                    p_h[pslot][:, 0:BCH],
                                    w13_b[bslot][:, cc,
                                                 iotile * 128:
                                                 (iotile + 1) * 128],
                                    x1T[:, cc, gsl],
                                    start=(cc == 0), stop=(cc == 7)),
                                [("MM", 1)] if cc == 7 else ())
                    mm_gu = w.c["MM"]
                    w.wait("scalar", "MM", mm_gu)
                    w.wait("scalar", "VE", sg_rd)
                    w.s(lambda s, pa=pa: s.activation(sg_b[:, 0:BCH],
                                                      p_h[pa][:, 0:BCH],
                                                      AF.Silu), [("SC", 1)])
                    sc_silu = w.c["SC"]
                    ph_rd[pa] = ("SC", sc_silu)
                    w.wait("vector", "SC", sc_silu)
                    w.wait("vector", "MM", hT_rd[ch % 2])
                    w.v(lambda v, iop=iop, ch=ch, pb_=pb_: v.tensor_tensor(
                        hT[ch % 2][:, iop, :], sg_b[:, 0:BCH],
                        p_h[pb_][:, 0:BCH], ALU.mult), [("VE", 1)])
                    sg_rd = w.c["VE"]
                    ph_rd[pb_] = ("VE", w.c["VE"])
                ve_h = w.c["VE"]
                if pend_z is not None:
                    emit_z(w, pend_z[0], pend_z[1])
                pend_z = (ch, ve_h)
            emit_z(w, pend_z[0], pend_z[1])
            if e != NE:
                continue
            if "tap_comb" in tap_t and qt_i == 3 and e == NE:
                w.wait("sync", "VE", w.c["VE"])
                w.y(lambda y: y.dma_start(out=tap_t["tap_comb"],
                                          in_=comb[:, :, :]), [("TAP", 16)])
            if "tap_yacc" in tap_t and qt_i == 0:
                w.wait("sync", "VE", w.c["VE"])
                w.y(lambda y: y.dma_start(out=tap_t["tap_yacc"],
                                          in_=y_acc[:, :, :]), [("TAP", 16)])
                tap_yg[0] = w.c["TAP"]
            w.drain("vector")
            for t_i in range(NQT):
                bi = t_i % 2
                pt_g = qt_i * NQT + t_i
                psl = slice(pt_g * 128, (pt_g + 1) * 128)
                w.wait("sync", "VE", x2res_rd[bi])
                x2s = "X2RES0" if bi == 0 else "X2RES1"
                w.y(lambda y, psl=psl, bi=bi, x2s=x2s: y.dma_start(
                    out=x2res[bi][:, :], in_=d_x1h[psl, :]), [(x2s, 16)])
                sv_r = w.c[x2s]
                w.wait("vector", x2s, sv_r)
                w.wait("vector", "SC", x2pre_rd)
                w.v(lambda v, t_i=t_i, bi=bi: v.tensor_tensor(
                    x2pre[:, :], y_acc[:, t_i, :], x2res[bi][:, :], ALU.add),
                    [("VE", 1)])
                x2res_rd[bi] = w.c["VE"]
                ve_pre = w.c["VE"]
                w.drain("vector")
                w.v(lambda v: v.bn_stats(st2[:, 0:6], x2pre[:, 0:512]),
                    [("VE", 1)])
                w.drain("vector")
                w.v(lambda v: v.bn_stats(st2[:, 6:12], x2pre[:, 512:1024]),
                    [("VE", 1)])
                w.drain("vector")
                w.wait("vector", "SC", st2_rd)
                w.v(lambda v: v.bn_aggr(st2[:, 12:14], st2[:, 0:12]),
                    [("VE", 1)])
                w.drain("vector")
                w.v(lambda v: v.tensor_scalar(st2[:, 14:15], st2[:, 13:14],
                                              1.0, EPS, ALU.mult, ALU.add),
                    [("VE", 1)])
                ve_var = w.c["VE"]
                w.wait("scalar", "VE", ve_var)
                w.s(lambda s: s.activation(st2[:, 15:16], st2[:, 14:15],
                                           AF.Ln), [("SC", 1)])
                w.drain("scalar")
                w.s(lambda s: s.activation(st2[:, 14:15], st2[:, 15:16],
                                           AF.Exp, scale=-0.5), [("SC", 1)])
                sc_rstd = w.c["SC"]
                st2_rd = sc_rstd
                x2pre_rd = sc_rstd
                w.wait("vector", "SC", sc_rstd)
                w.wait("vector", "OUT0" if bi == 0 else "OUT1", outf_rd[bi])
                w.drain("vector")
                w.v(lambda v, bi=bi: v.tensor_scalar(outf[bi][:, :],
                                                     x2pre[:, :],
                                                     st2[:, 12:13],
                                                     st2[:, 14:15],
                                                     ALU.subtract, ALU.mult),
                    [("VE", 1)])
                ve_out = w.c["VE"]
                w.wait("sync", "VE", ve_out)
                ots = "OUT0" if bi == 0 else "OUT1"
                w.y(lambda y, bi=bi, psl=psl, ots=ots: y.dma_start(
                    out=d_out[psl, :], in_=outf[bi][:, :]), [(ots, 16)])
                outf_rd[bi] = w.c[ots]
        if w.name == "sync":
            w.eng.wait_ge(w.sems["OUT0"], w.c["OUT0"])
            w.eng.wait_ge(w.sems["OUT1"], w.c["OUT1"])
            w.eng.wait_ge(w.sems["TAP"], w.c["TAP"])

    with nc.Block("blkB") as blkB:
        run_block(blkB, emit_B)

    st_b.close()
    st_x1t.close()
    st_persist.close()

    base = walks["tensor"].c
    for n, wk in walks.items():
        assert wk.c == base, (n, wk.c, base)
    return nc


def _host_consts():
    ident = np.eye(128, dtype=np.float32)
    k = np.arange(128)[:, None]
    q = np.arange(128)[None, :]
    cmask = np.ascontiguousarray((q >= k).astype(np.float32))
    ones = np.ones((1, 128), dtype=np.float32)
    o2w = np.zeros((128, 128), dtype=np.float32)
    o2w[:, 0] = 1.0
    o64 = np.ones((128, 64), dtype=np.float32)
    return ident, cmask, ones, o2w, o64


def make_shared_inputs(L0_attn_W_Q, L0_attn_W_K, L0_attn_W_V, L0_attn_W_O,
                       L0_ffn_gate_w, L0_ffn_gate_b, L0_ffn_w13, L0_ffn_w2,
                       L0_ffn_w13_s, L0_ffn_w2_s):
    ident, cmask, ones, o2w, o64 = _host_consts()
    f32 = np.float32
    A = lambda a: np.ascontiguousarray(np.asarray(a, f32))
    return {
        "wqT": A(np.asarray(L0_attn_W_Q, f32).T),
        "wkT": A(np.asarray(L0_attn_W_K, f32).T),
        "wvT": A(np.asarray(L0_attn_W_V, f32).T),
        "woT": A(np.asarray(L0_attn_W_O, f32).T),
        "gwT": A(np.asarray(L0_ffn_gate_w, f32).T),
        "gb": A(np.asarray(L0_ffn_gate_b, f32)[:, None]),
        "gbT": A(np.asarray(L0_ffn_gate_b, f32)[None, :]),
        "w13T": A(np.asarray(L0_ffn_w13, f32).transpose(0, 2, 1)),
        "w2T": A(np.asarray(L0_ffn_w2, f32).transpose(0, 2, 1)),
        "w13sT": A(np.asarray(L0_ffn_w13_s, f32).T),
        "w2sT": A(np.asarray(L0_ffn_w2_s, f32).T),
        "ident": ident, "ltri": cmask, "ones": ones, "ones2w": o2w,
        "ones64": o64,
    }


def kernel(L0_attn_x, L0_attn_W_Q, L0_attn_W_K, L0_attn_W_V, L0_attn_W_O,
           L0_ffn_input_ids, L0_ffn_gate_w, L0_ffn_gate_b, L0_ffn_w13,
           L0_ffn_w2, L0_ffn_w13_s, L0_ffn_w2_s):
    shared = make_shared_inputs(L0_attn_W_Q, L0_attn_W_K, L0_attn_W_V,
                                L0_attn_W_O, L0_ffn_gate_w, L0_ffn_gate_b,
                                L0_ffn_w13, L0_ffn_w2, L0_ffn_w13_s,
                                L0_ffn_w2_s)
    xs = np.asarray(L0_attn_x, np.float32)
    in_maps = []
    for c in range(8):
        x = np.ascontiguousarray(xs[c])
        xT = np.ascontiguousarray(x.T)
        m = {"x": x, "xT": xT}
        m.update(shared)
        in_maps.append(m)
    nc = build_program(S=4096)
    res = run_bass_kernel_spmd(nc, in_maps, core_ids=list(range(8)))
    out = np.stack([np.asarray(r["out"], dtype=np.float32)
                    for r in res.results], axis=0)
    return out


# version 17
# speedup vs baseline: 1.1532x; 1.0250x over previous
import numpy as np
from contextlib import ExitStack

import concourse.bass as bass
import concourse.mybir as mybir
from concourse.bass_utils import run_bass_kernel_spmd

F32 = mybir.dt.float32
BF16 = mybir.dt.bfloat16
AF = mybir.ActivationFunctionType
ALU = mybir.AluOpType
AX = mybir.AxisListType

D = 1024
H = 16
NE = 16
INTER = 512
ROUTE_SCALE = 2.5
EPS = 1e-5
NEG = -1.0e9

ENGS = ("tensor", "scalar", "vector", "gpsimd", "sync")


class Walk:

    def __init__(self, name, sem_names):
        self.name = name
        self.eng = None
        self.sems = None
        self.c = {k: 0 for k in sem_names}
        self._lastw = {}

    def _emit(self, owner, fn, incs):
        inst = fn(self.eng) if self.name == owner else None
        for sem, n in incs:
            self.c[sem] += n
            if inst is not None:
                inst.then_inc(self.sems[sem], n)

    def t(self, fn, incs=()):
        self._emit("tensor", fn, incs)

    def s(self, fn, incs=()):
        self._emit("scalar", fn, incs)

    def v(self, fn, incs=()):
        self._emit("vector", fn, incs)

    def g(self, fn, incs=()):
        self._emit("gpsimd", fn, incs)

    def y(self, fn, incs=()):
        self._emit("sync", fn, incs)

    def wait(self, eng_name, sem, val):
        if val <= 0 or self.name != eng_name:
            return
        if self._lastw.get(sem, 0) >= val:
            return
        self._lastw[sem] = val
        self.eng.wait_ge(self.sems[sem], val)

    def drain(self, eng_name):
        if self.name == eng_name:
            self.eng.drain()


def build_program(S=4096, taps=()):
    taps = set(taps)
    NQB = S // 512
    NKT = S // 128
    NCH = S // 512
    NPT = S // 128
    QTR = S // 4
    NQT = NPT // 4
    BCH = min(512, QTR)
    NBCH = QTR // BCH
    NST = BCH // 128
    SQW = 2048

    nc = bass.Bass("TRN2", target_bir_lowering=False)

    d_x = nc.dram_tensor("x", [S, D], F32, kind="ExternalInput").ap()
    d_xT = nc.dram_tensor("xT", [D, S], F32, kind="ExternalInput").ap()
    d_wqT = nc.dram_tensor("wqT", [D, D], F32, kind="ExternalInput").ap()
    d_wkT = nc.dram_tensor("wkT", [D, D], F32, kind="ExternalInput").ap()
    d_wvT = nc.dram_tensor("wvT", [D, D], F32, kind="ExternalInput").ap()
    d_woT = nc.dram_tensor("woT", [D, D], F32, kind="ExternalInput").ap()
    d_gwT = nc.dram_tensor("gwT", [D, NE], F32, kind="ExternalInput").ap()
    d_gb = nc.dram_tensor("gb", [NE, 1], F32, kind="ExternalInput").ap()
    d_gbT = nc.dram_tensor("gbT", [1, NE], F32, kind="ExternalInput").ap()
    d_w13T = nc.dram_tensor("w13T", [NE, D, D], F32, kind="ExternalInput").ap()
    d_w2T = nc.dram_tensor("w2T", [NE, INTER, D], F32, kind="ExternalInput").ap()
    d_w13sT = nc.dram_tensor("w13sT", [D, D], F32, kind="ExternalInput").ap()
    d_w2sT = nc.dram_tensor("w2sT", [INTER, D], F32, kind="ExternalInput").ap()
    d_ident = nc.dram_tensor("ident", [128, 128], F32, kind="ExternalInput").ap()
    d_ltri = nc.dram_tensor("ltri", [128, 128], F32, kind="ExternalInput").ap()
    d_ones = nc.dram_tensor("ones", [1, 128], F32, kind="ExternalInput").ap()
    d_ones2w = nc.dram_tensor("ones2w", [128, 128], F32, kind="ExternalInput").ap()
    d_ones64 = nc.dram_tensor("ones64", [128, 64], F32, kind="ExternalInput").ap()

    d_out = nc.dram_tensor("out", [S, D], F32, kind="ExternalOutput").ap()
    d_x1h = nc.dram_tensor("x1h", [S, D], F32, kind="Internal").ap()

    tap_t = {}
    for name, shape, dt in (
        ("tap_qt", [128, 4, S], BF16), ("tap_kt", [128, 4, S], BF16),
        ("tap_v", [128, NKT, 8, 65], BF16), ("tap_ytn", [2, 128, 4, S], BF16),
        ("tap_x1", [S, D], F32), ("tap_comb", [128, NPT, NE], F32),
        ("tap_x1t", [128, 8, S], BF16), ("tap_negm", [128, 2], F32),
        ("tap_yacc", [128, NQT, D], F32),
    ):
        if name in taps:
            tap_t[name] = nc.dram_tensor(name, shape, dt,
                                         kind="ExternalOutput").ap()

    sem_names = ("MM", "SC", "VE", "CONSTG", "CONSTS", "WQKV", "XTB",
                 "WO", "XRES0", "XRES1", "XRES2", "X1F0", "X1F1", "GW",
                 "GBS", "XTB2",
                 "W13A", "W13B", "X2RES0", "X2RES1", "OUT0", "OUT1",
                 "YST0", "YST1", "TAP")
    walks = {n: Walk(n, sem_names) for n in ENGS}

    st_persist = ExitStack()
    E = st_persist.enter_context
    semd = {n: E(nc.semaphore(n)) for n in sem_names}
    for wk in walks.values():
        wk.sems = semd

    c_id_b = E(nc.sbuf_tensor("c_id_b", [128, 128], BF16))
    c_id_f = E(nc.sbuf_tensor("c_id_f", [128, 128], F32))
    c_ones_f = E(nc.sbuf_tensor("c_ones_f", [1, 128], F32))
    c_o64_f = E(nc.sbuf_tensor("c_o64_f", [128, 64], F32))
    negm_b = E(nc.sbuf_tensor("negm_b", [128, 2], F32))

    st_ytn = ExitStack()
    yTn = [st_ytn.enter_context(nc.sbuf_tensor(f"yTn{g}", [128, 4, S], BF16))
           for g in range(2)]

    st_a = ExitStack()
    EA = st_a.enter_context
    c_ltri_b = EA(nc.sbuf_tensor("c_ltri_b", [128, 128], BF16))
    c_o64_b = EA(nc.sbuf_tensor("c_o64_b", [128, 64], BF16))
    c_o2w_b = EA(nc.sbuf_tensor("c_o2w_b", [128, 128], BF16))
    xTb = EA(nc.sbuf_tensor("xTb", [128, 8, 512], BF16))
    xTb2 = yTn[1][:, 0, :].rearrange("p (a b) -> p a b", a=8)
    xtbs = [xTb[:, :, :], xTb2]
    wq_b = EA(nc.sbuf_tensor("wq_b", [128, 8, 512], BF16))
    wk_b = EA(nc.sbuf_tensor("wk_b", [128, 8, 512], BF16))
    wv_b = EA(nc.sbuf_tensor("wv_b", [128, 8, 512], BF16))
    QT = EA(nc.sbuf_tensor("QT", [128, 4, S], BF16))
    KT = EA(nc.sbuf_tensor("KT", [128, 4, S], BF16))
    Vb = EA(nc.sbuf_tensor("Vb", [128, NKT, 8, 65], BF16))
    sq2 = EA(nc.sbuf_tensor("sq2", [128, SQW], BF16))
    ptb = [[sq2[:, (2 * i + e) * 512:(2 * i + e + 1) * 512]
            for i in range(2)] for e in range(2)]
    ptb3 = [[ptb[e][0], ptb[e][1], xTb[:, e, :]] for e in range(2)]
    mxq = EA(nc.sbuf_tensor("mxq", [128, 1], F32))
    mxk = EA(nc.sbuf_tensor("mxk", [128, 1], F32))
    mxt = EA(nc.sbuf_tensor("mxt", [128, 1], F32))
    sc1 = EA(nc.sbuf_tensor("sc1", [1, 4], F32))
    rdf = EA(nc.sbuf_tensor("rdf", [65, 1024], BF16))
    yev = EA(nc.sbuf_tensor("yev", [65, 2, 512], BF16))
    rb_t = EA(nc.sbuf_tensor("rb_t", [64, 512], F32))
    rb_sb = rb_t[:, :]
    yst = [EA(nc.sbuf_tensor("yst0", [64, 512], BF16))]
    p_sq = EA(nc.psum_tensor("p_sq", [128, 2048], F32))
    p_s = [p_sq[:, i * 512:(i + 1) * 512] for i in range(4)]
    p_y = [EA(nc.psum_tensor(f"p_y{e}", [65, 512], F32)) for e in range(2)]
    p_rb = EA(nc.psum_tensor("p_rb", [64, 512], F32))

    def emit_A(w: Walk):
        w.g(lambda g: g.dma_start(out=c_id_b[:, :], in_=d_ident),
            [("CONSTG", 16)])
        w.g(lambda g: g.dma_start(out=c_ltri_b[:, :], in_=d_ltri),
            [("CONSTG", 16)])
        w.g(lambda g: g.dma_start(out=c_o64_b[:, :], in_=d_ones64),
            [("CONSTG", 16)])
        w.g(lambda g: g.dma_start(out=c_o2w_b[:, :], in_=d_ones2w),
            [("CONSTG", 16)])
        gv_consts = w.c["CONSTG"]
        w.y(lambda y: y.dma_start(out=c_id_f[:, :], in_=d_ident),
            [("CONSTS", 16)])
        w.y(lambda y: y.dma_start(out=c_ones_f[:, :], in_=d_ones),
            [("CONSTS", 16)])
        w.y(lambda y: y.dma_start(out=c_o64_f[:, :], in_=d_ones64),
            [("CONSTS", 16)])
        sv_consts = w.c["CONSTS"]
        w.v(lambda v: v.memset(Vb[:, :, :, 64:65], 1.0), [("VE", 1)])
        ve_ones = w.c["VE"]

        wT_loaded = [0, 0]
        xtb_rd2 = [0, 0]
        wslice_rd = 0
        vb_rd_prev = 0
        qt_rd_prev = 0
        sq2_rd4 = [0, 0, 0, 0]
        ps_wr = [0] * 4
        ps_rd = [("SC", 0)] * 4
        py_rd = [0, 0]
        ptb_rd = [[0, 0, 0], [0, 0, 0]]
        ptb_ve = [[0, 0, 0], [0, 0, 0]]
        rb_rd = 0
        ti_rd = 0
        yst_st = [0, 0]
        yev_rd = 0

        for g in range(2):
            gsl = slice(g * 512, (g + 1) * 512)
            w.wait("gpsimd", "MM", wslice_rd)
            for dsrc, dst in ((d_wqT, wq_b), (d_wkT, wk_b), (d_wvT, wv_b)):
                w.g(lambda gg, dsrc=dsrc, dst=dst: gg.dma_start(
                    out=dst[:, :, :],
                    in_=dsrc.rearrange("(cc p) n -> p cc n", p=128)[:, :, gsl]),
                    [("WQKV", 16)])
            wT_loaded[g] = w.c["WQKV"]

            for ch in range(NCH):
                cb = ch % 2
                w.wait("gpsimd", "MM", xtb_rd2[cb])
                xts = "XTB" if cb == 0 else "XTB2"
                w.g(lambda gg, ch=ch, cb=cb, xts=xts: gg.dma_start(
                    out=xtbs[cb],
                    in_=d_xT.rearrange("(cc p) n -> p cc n", p=128)
                    [:, :, ch * 512:(ch + 1) * 512]), [(xts, 16)])
                gv_x = w.c[xts]
                w.wait("tensor", xts, gv_x)
                w.wait("tensor", "WQKV", wT_loaded[g])
                chs = slice(ch * 512, (ch + 1) * 512)
                for wi, (wbuf, dstT) in enumerate(((wq_b, QT), (wk_b, KT))):
                    for dt in range(4):
                        pb = (wi * 4 + dt) % 4
                        sem, cnt = ps_rd[pb]
                        w.wait("tensor", sem, cnt)
                        for cc in range(8):
                            w.t(lambda t, pb=pb, wbuf=wbuf, dt=dt, cc=cc,
                                cb=cb: t.matmul(
                                    p_s[pb],
                                    wbuf[:, cc, dt * 128:(dt + 1) * 128],
                                    xtbs[cb][:, cc, :],
                                    start=(cc == 0), stop=(cc == 7)),
                                [("MM", 1)] if cc == 7 else ())
                        mmv = w.c["MM"]
                        ps_wr[pb] = mmv
                        w.wait("vector", "MM", max(mmv, qt_rd_prev))
                        w.v(lambda v, pb=pb, dstT=dstT, dt=dt, chs=chs:
                            v.tensor_copy(dstT[:, dt, chs], p_s[pb]),
                            [("VE", 1)])
                        ps_rd[pb] = ("VE", w.c["VE"])
                for pt in range(4):
                    pb = pt % 4
                    sem, cnt = ps_rd[pb]
                    w.wait("tensor", sem, cnt)
                    for cc in range(8):
                        w.t(lambda t, pb=pb, cc=cc, pt=pt, cb=cb:
                            t.matmul(p_s[pb],
                                     xtbs[cb][:, cc, pt * 128:(pt + 1) * 128],
                                     wv_b[:, cc, :],
                                     start=(cc == 0), stop=(cc == 7)),
                            [("MM", 1)] if cc == 7 else ())
                    mmv = w.c["MM"]
                    ps_wr[pb] = mmv
                    kt_i = ch * 4 + pt
                    w.wait("vector", "VE", ve_ones)
                    w.wait("vector", "MM", max(mmv, vb_rd_prev))
                    w.v(lambda v, pb=pb, kt_i=kt_i:
                        v.tensor_copy(Vb[:, kt_i, 0:8, 0:64],
                                      p_s[pb].rearrange(
                                          "p (h d) -> p h d", d=64)),
                        [("VE", 1)])
                    ps_rd[pb] = ("VE", w.c["VE"])
                ptmax = max(ptb_rd[0] + ptb_rd[1])
                w.wait("tensor", "CONSTG", gv_consts)

                def norm_mm(w, item):
                    sqsl, nb, scv, mx, first = item
                    sem, cnt = ps_rd[nb]
                    w.wait("tensor", sem, cnt)
                    w.wait("tensor", "SC", scv)
                    w.t(lambda t, sqsl=sqsl, nb=nb: t.matmul(
                        p_s[nb], c_o2w_b[:, :], sq2[:, sqsl],
                        start=True, stop=True), [("MM", 1)])
                    ps_wr[nb] = w.c["MM"]
                    sq2_rd4[nb] = w.c["MM"]
                    w.wait("vector", "MM", ps_wr[nb])
                    w.v(lambda v, nb=nb: v.tensor_reduce(
                        mxt[:, :], p_s[nb], AX.X, ALU.max), [("VE", 1)])
                    ps_rd[nb] = ("VE", w.c["VE"])
                    w.drain("vector")
                    if first:
                        w.v(lambda v, mx=mx: v.tensor_copy(
                            mx[:, :], mxt[:, :]), [("VE", 1)])
                    else:
                        w.v(lambda v, mx=mx: v.tensor_tensor(
                            mx[:, :], mx[:, :], mxt[:, :], ALU.max),
                            [("VE", 1)])
                    w.drain("vector")

                pend_nm = None
                for si, (nsrc, mx) in enumerate(((QT, mxq), (KT, mxk))):
                    for pr in range(4):
                        nb = (si * 4 + pr) % 4
                        sqsl = slice(nb * 512, nb * 512 + 512)
                        w.wait("scalar", "MM", max(sq2_rd4[nb], ptmax))
                        w.wait("scalar", "VE", w.c["VE"])
                        w.s(lambda s, nsrc=nsrc, pr=pr, chs=chs, sqsl=sqsl:
                            s.activation(
                                sq2[:, sqsl], nsrc[:, pr, chs],
                                AF.Square), [("SC", 1)])
                        scv = w.c["SC"]
                        if pend_nm is not None:
                            norm_mm(w, pend_nm)
                        pend_nm = (sqsl, nb, scv, mx, ch == 0 and pr == 0)
                norm_mm(w, pend_nm)
                xtb_rd2[cb] = w.c["MM"]
            wslice_rd = w.c["MM"]
            ve_a1 = w.c["VE"]
            w.v(lambda v: v.tensor_tensor(sc1[:, 2:3], mxq[0:1, 0:1],
                                          mxk[0:1, 0:1], ALU.mult),
                [("VE", 1)])
            ve_m = w.c["VE"]
            w.wait("scalar", "VE", ve_m)
            w.s(lambda s: s.activation(sc1[:, 3:4], sc1[:, 2:3], AF.Ln),
                [("SC", 1)])
            w.drain("scalar")
            w.s(lambda s: s.activation(sc1[:, 2:3], sc1[:, 3:4], AF.Exp,
                                       scale=0.5), [("SC", 1)])
            scv_m = w.c["SC"]
            w.wait("vector", "SC", scv_m)
            w.drain("vector")
            w.v(lambda v: v.tensor_scalar_mul(sc1[:, 3:4], sc1[:, 2:3],
                                              -0.125), [("VE", 1)])
            ve_negm = w.c["VE"]
            sem0, cnt0 = ps_rd[0]
            w.wait("tensor", sem0, cnt0)
            w.wait("tensor", "VE", max(ve_negm, ti_rd))
            w.wait("tensor", "CONSTS", sv_consts)
            w.t(lambda t: t.matmul(p_sq[:, 0:1], c_ones_f[:, :], sc1[:, 3:4],
                                   start=True, stop=True), [("MM", 1)])
            mm_ti = w.c["MM"]
            w.wait("vector", "MM", mm_ti)
            w.v(lambda v, g=g: v.tensor_copy(negm_b[:, g:g + 1], p_sq[:, 0:1]),
                [("VE", 1)])
            ti_rd = w.c["VE"]
            ps_rd[0] = ("VE", w.c["VE"])
            ve_negm_done = w.c["VE"]

            w.wait("tensor", "CONSTG", gv_consts)
            pending_tail = [None]

            def flush_tail(w):
                if pending_tail[0] is None:
                    return
                fn = pending_tail[0]
                pending_tail[0] = None
                fn(w)

            for pr in range(4):
                for qb in range(NQB):
                    kts = 4 * qb + 4
                    qsl = slice(qb * 512, (qb + 1) * 512)
                    exp_v = [[("SC", 0)] * 3, [("SC", 0)] * 3]
                    for kt in range(kts + 2):
                        if kt == min(10, kts):
                            flush_tail(w)
                        if kt < kts:
                            j = kt - 4 * qb
                            i2 = kt % 2
                            i3 = kt % 3
                            qs = 128 * j if j > 0 else 0
                            for e in range(2):
                                sem, cnt = ps_rd[2 * i2 + e]
                                w.wait("tensor", sem, cnt)
                            w.wait("tensor", "VE", ve_a1)
                            for e in range(2):
                                hp = 64 * e
                                w.t(lambda t, i2=i2, e=e, hp=hp, pr=pr,
                                    kt=kt, qb=qb, qs=qs: t.matmul(
                                        p_s[2 * i2 + e][:, qs:512],
                                        KT[hp:hp + 64, pr,
                                           kt * 128:(kt + 1) * 128],
                                        QT[hp:hp + 64, pr,
                                           qb * 512 + qs:(qb + 1) * 512],
                                        start=True, stop=True,
                                        tile_position=(hp, 0)), [("MM", 1)])
                                ps_wr[2 * i2 + e] = w.c["MM"]
                            w.wait("scalar", "MM",
                                   max(ps_wr[2 * i2], ps_wr[2 * i2 + 1],
                                       ptb_rd[0][i3], ptb_rd[1][i3],
                                       max(sq2_rd4)))
                            w.wait("scalar", "VE",
                                   max(ve_negm_done, ptb_ve[0][i3],
                                       ptb_ve[1][i3]))
                            if qs == 0 and i3 < 2:
                                w.s(lambda s, i3=i3, i2=i2, g=g:
                                    s.activation(
                                        sq2[:, 2 * i3 * 512:
                                            (2 * i3 + 2) * 512],
                                        p_sq[:, 2 * i2 * 512:
                                             (2 * i2 + 2) * 512],
                                        AF.Exp, bias=negm_b[:, g:g + 1],
                                        scale=0.125), [("SC", 1)])
                            elif qs == 0:
                                w.s(lambda s, i2=i2, g=g: s.activation(
                                    xTb[:, 0:2, :].rearrange(
                                        "p a b -> p (a b)"),
                                    p_sq[:, 2 * i2 * 512:
                                         (2 * i2 + 2) * 512],
                                    AF.Exp, bias=negm_b[:, g:g + 1],
                                    scale=0.125), [("SC", 1)])
                            elif i3 < 2:
                                w.s(lambda s, i3=i3, i2=i2, g=g, qs=qs:
                                    s.activation(
                                        sq2[:, 2 * i3 * 512 + qs:
                                            (2 * i3 + 2) * 512],
                                        p_sq[:, 2 * i2 * 512 + qs:
                                             (2 * i2 + 2) * 512],
                                        AF.Exp, bias=negm_b[:, g:g + 1],
                                        scale=0.125), [("SC", 1)])
                            else:
                                w.s(lambda s, i2=i2, g=g, qs=qs:
                                    s.activation(
                                        xTb[:, 0:2, :].rearrange(
                                            "p a b -> p (a b)")[:, qs:1024],
                                        p_sq[:, 2 * i2 * 512 + qs:
                                             (2 * i2 + 2) * 512],
                                        AF.Exp, bias=negm_b[:, g:g + 1],
                                        scale=0.125), [("SC", 1)])
                            sc_exp = w.c["SC"]
                            ps_rd[2 * i2] = ("SC", sc_exp)
                            ps_rd[2 * i2 + 1] = ("SC", sc_exp)
                            if j >= 0:
                                w.wait("vector", "SC", sc_exp)
                                w.wait("vector", "CONSTG", gv_consts)
                                for e in range(2):
                                    w.v(lambda v, i3=i3, e=e, qs=qs:
                                        v.tensor_tensor(
                                            ptb3[e][i3][:, qs:qs + 128],
                                            ptb3[e][i3][:, qs:qs + 128],
                                            c_ltri_b[:, :], ALU.mult),
                                        [("VE", 1)])
                                    ptb_ve[e][i3] = w.c["VE"]
                                exp_v[0][i3] = ("VE", w.c["VE"])
                                exp_v[1][i3] = ("VE", w.c["VE"])
                            else:
                                exp_v[0][i3] = ("SC", sc_exp)
                                exp_v[1][i3] = ("SC", sc_exp)
                        if kt > 1:
                            pkt = kt - 2
                            pj = pkt - 4 * qb
                            pqs = 128 * pj if pj > 0 else 0
                            for e in range(2):
                                hg = 2 * pr + e
                                esem, ecnt = exp_v[e][pkt % 3]
                                w.wait("tensor", esem, ecnt)
                                if pkt == 0:
                                    w.wait("tensor", "VE", py_rd[e])
                                w.t(lambda t, e=e, pkt=pkt, hg=hg, kts=kts,
                                    pqs=pqs: t.matmul(
                                        p_y[e][:, pqs:512],
                                        Vb[:, pkt, hg, 0:65],
                                        ptb3[e][pkt % 3][:, pqs:512],
                                        start=(pkt == 0),
                                        stop=(pkt == kts - 1)),
                                    [("MM", 1)])
                                ptb_rd[e][pkt % 3] = w.c["MM"]
                    mm_pv = w.c["MM"]
                    w.wait("vector", "MM", mm_pv)
                    w.wait("vector", "VE", yev_rd)
                    w.drain("vector")
                    for e in range(2):
                        def _evac(v, e=e):
                            with nc.allow_low_precision(
                                    reason="bf16 attention-out operand"):
                                return v.tensor_copy(yev[:, e, :],
                                                     p_y[e][:, :])
                        w.v(_evac, [("VE", 1)])
                        py_rd[e] = w.c["VE"]
                    w.drain("vector")
                    for e in range(2):
                        def _recip(v, e=e):
                            with nc.allow_low_precision(
                                    reason="bf16 denom bcast operand"):
                                return v.reciprocal(
                                    rdf[64:65, e * 512:(e + 1) * 512],
                                    yev[64:65, e, :])
                        w.v(_recip, [("VE", 1)])
                    ve_rd2 = w.c["VE"]

                    def tail(w, g=g, pr=pr, qsl=qsl, ve_rd2=ve_rd2):
                        nonlocal rb_rd, yev_rd
                        ysl = 0
                        for e in range(2):
                            w.wait("tensor", "VE", max(ve_rd2, rb_rd))
                            w.wait("tensor", "CONSTG", gv_consts)
                            w.t(lambda t, e=e: t.matmul(
                                p_rb[:, :], c_o64_b[64:65, :],
                                rdf[64:65, e * 512:(e + 1) * 512],
                                start=True, stop=True), [("MM", 1)])
                            mm_rb = w.c["MM"]
                            w.wait("vector", "MM", mm_rb)
                            w.drain("vector")
                            w.v(lambda v: v.tensor_copy(rb_sb, p_rb[:, :]),
                                [("VE", 1)])
                            rb_rd = w.c["VE"]
                            w.drain("vector")
                            if e == 0:
                                def _m0(v, g=g, pr=pr, qsl=qsl):
                                    with nc.allow_low_precision(
                                            reason="bf16 yT operand"):
                                        return v.tensor_tensor(
                                            yTn[g][0:64, pr, qsl],
                                            yev[0:64, 0, :], rb_sb,
                                            ALU.mult)
                                w.v(_m0, [("VE", 1)])
                                yev_rd = w.c["VE"]
                            else:
                                yss = "YST0" if ysl == 0 else "YST1"
                                w.wait("vector", yss, yst_st[ysl])

                                def _m1(v, ysl=ysl):
                                    with nc.allow_low_precision(
                                            reason="bf16 yT operand"):
                                        return v.tensor_tensor(
                                            yst[ysl][:, :],
                                            yev[0:64, 1, :], rb_sb,
                                            ALU.mult)
                                w.v(_m1, [("VE", 1)])
                                yev_rd = w.c["VE"]
                                ve_yst = w.c["VE"]
                                w.wait("sync", "VE", ve_yst)
                                w.y(lambda y, g=g, pr=pr, qsl=qsl, ysl=ysl,
                                    yss=yss: y.dma_start(
                                        out=yTn[g][64:128, pr, qsl],
                                        in_=yst[ysl][:, :]), [(yss, 16)])
                                yst_st[ysl] = w.c[yss]
                    pending_tail[0] = tail
                flush_tail(w)
            flush_tail(w)
            vb_rd_prev = w.c["MM"]
            qt_rd_prev = w.c["MM"]
            xtb_rd2[0] = w.c["MM"]
            xtb_rd2[1] = w.c["MM"]

        if tap_t:
            w.wait("sync", "VE", w.c["VE"])
            w.wait("sync", "MM", w.c["MM"])
            w.wait("sync", "CONSTG", gv_consts)
            w.wait("sync", "YST0", w.c["YST0"])
            w.wait("sync", "YST1", w.c["YST1"])
        if "tap_qt" in tap_t:
            w.y(lambda y: y.dma_start(out=tap_t["tap_qt"], in_=QT[:, :, :]),
                [("TAP", 16)])
        if "tap_kt" in tap_t:
            w.y(lambda y: y.dma_start(out=tap_t["tap_kt"], in_=KT[:, :, :]),
                [("TAP", 16)])
        if "tap_v" in tap_t:
            w.y(lambda y: y.dma_start(out=tap_t["tap_v"], in_=Vb[:, :, :, :]),
                [("TAP", 16)])
        if "tap_negm" in tap_t:
            w.y(lambda y: y.dma_start(out=tap_t["tap_negm"], in_=negm_b[:, :]),
                [("TAP", 16)])
        if "tap_ytn" in tap_t:
            w.y(lambda y: y.dma_start(out=tap_t["tap_ytn"][0],
                                      in_=yTn[0][:, :, :]), [("TAP", 16)])
            w.y(lambda y: y.dma_start(out=tap_t["tap_ytn"][1],
                                      in_=yTn[1][:, :, :]), [("TAP", 16)])
        if w.name == "sync":
            w.eng.wait_ge(w.sems["YST0"], w.c["YST0"])
            w.eng.wait_ge(w.sems["YST1"], w.c["YST1"])
            w.eng.wait_ge(w.sems["TAP"], w.c["TAP"])

    def run_block(blk, emit):
        for en in ENGS:
            def cb(e, en=en):
                walks[en].eng = e
                emit(walks[en])
            getattr(blk, en)(cb)

    with nc.Block("blkA") as blkA:
        run_block(blkA, emit_A)

    st_a.close()

    st_x1t = ExitStack()
    x1T = st_x1t.enter_context(
        nc.sbuf_tensor("x1T", [128, 8, S], BF16, side="right"))
    st_a3 = ExitStack()
    E3 = st_a3.enter_context
    woT_b = E3(nc.sbuf_tensor("woT_b", [128, 8, 1024], BF16))
    xres = [E3(nc.sbuf_tensor(f"xres{i}", [128, D], F32)) for i in range(3)]
    x1pre = [E3(nc.sbuf_tensor(f"x1pre{i}", [128, D], F32)) for i in range(2)]
    x1f = [E3(nc.sbuf_tensor(f"x1f{i}", [128, D], F32)) for i in range(2)]
    x1bb = [E3(nc.sbuf_tensor(f"x1bb{i}", [128, D], BF16)) for i in range(3)]
    junk_b = E3(nc.sbuf_tensor("junk_b", [128, D], BF16))
    stt = E3(nc.sbuf_tensor("stt", [128, 16], F32))
    p_z = [E3(nc.psum_tensor(f"p_z{i}", [128, 512], F32)) for i in range(2)]
    p_t = [E3(nc.psum_tensor(f"p_t{i}", [128, 128], BF16)) for i in range(4)]

    def emit_A3(w: Walk):
        w.g(lambda g: g.dma_start(
            out=woT_b[:, :, :],
            in_=d_woT.rearrange("(cc p) n -> p cc n", p=128)), [("WO", 16)])
        gv_wo = w.c["WO"]
        xres_rd = [0, 0, 0]
        x1f_rd = [0, 0]
        x1b_rd = [0, 0]
        x1pre_rd = [0, 0]
        stt_rd = 0
        pz_rd = [0, 0]
        pt_rd = [0, 0, 0, 0]
        pend_tp = []
        x1b_rd3 = [0, 0, 0]
        for pt in range(NPT):
            bi = pt % 2
            ri = pt % 3
            psl = slice(pt * 128, (pt + 1) * 128)
            w.wait("sync", "VE", xres_rd[ri])
            xrs = ("XRES0", "XRES1", "XRES2")[ri]
            w.y(lambda y, psl=psl, ri=ri: y.dma_start(out=xres[ri][:, :],
                                                      in_=d_x[psl, :]),
                [(xrs, 16)])
            sv_x = w.c[xrs]
            w.wait("tensor", "WO", gv_wo)
            for half in range(2):
                w.wait("tensor", "VE", pz_rd[half])
                for cc in range(8):
                    g2, p2 = divmod(cc, 4)
                    w.t(lambda t, half=half, g2=g2, p2=p2, cc=cc, psl=psl:
                        t.matmul(p_z[half][:, :],
                                 yTn[g2][:, p2, psl],
                                 woT_b[:, cc, half * 512:(half + 1) * 512],
                                 start=(cc == 0), stop=(cc == 7)),
                        [("MM", 1)] if cc == 7 else ())
            mm_z = w.c["MM"]
            w.wait("vector", "MM", mm_z)
            w.wait("vector", xrs, sv_x)
            for half in range(2):
                w.v(lambda v, half=half, bi=bi, ri=ri: v.tensor_tensor(
                    x1pre[bi][:, half * 512:(half + 1) * 512],
                    p_z[half][:, :],
                    xres[ri][:, half * 512:(half + 1) * 512], ALU.add),
                    [("VE", 1)])
                pz_rd[half] = w.c["VE"]
            xres_rd[ri] = w.c["VE"]
            ve_pre = w.c["VE"]
            w.drain("vector")
            w.v(lambda v, bi=bi: v.bn_stats(stt[:, 0:6],
                                            x1pre[bi][:, 0:512]),
                [("VE", 1)])
            w.drain("vector")
            w.v(lambda v, bi=bi: v.bn_stats(stt[:, 6:12],
                                            x1pre[bi][:, 512:1024]),
                [("VE", 1)])
            w.drain("vector")
            w.wait("vector", "SC", stt_rd)
            w.v(lambda v: v.bn_aggr(stt[:, 12:14], stt[:, 0:12]),
                [("VE", 1)])
            w.drain("vector")
            w.v(lambda v: v.tensor_scalar(stt[:, 14:15], stt[:, 13:14],
                                          1.0, EPS, ALU.mult, ALU.add),
                [("VE", 1)])
            ve_var = w.c["VE"]
            w.wait("scalar", "VE", ve_var)
            w.s(lambda s: s.activation(stt[:, 15:16], stt[:, 14:15], AF.Ln),
                [("SC", 1)])
            w.drain("scalar")
            w.s(lambda s: s.activation(stt[:, 14:15], stt[:, 15:16], AF.Exp,
                                       scale=-0.5), [("SC", 1)])
            sc_rstd = w.c["SC"]
            stt_rd = sc_rstd
            w.wait("vector", "SC", sc_rstd)
            w.wait("vector", "X1F0" if bi == 0 else "X1F1", x1f_rd[bi])
            w.drain("vector")
            w.v(lambda v, bi=bi: v.tensor_scalar(x1f[bi][:, :],
                                                 x1pre[bi][:, :],
                                                 stt[:, 12:13], stt[:, 14:15],
                                                 ALU.subtract, ALU.mult),
                [("VE", 1)])
            ve_x1 = w.c["VE"]
            ti3 = pt % 3
            w.drain("vector")
            w.wait("vector", "MM", x1b_rd3[ti3])
            w.v(lambda v, ti3=ti3, bi=bi: v.tensor_copy(x1bb[ti3][:, :],
                                                        x1f[bi][:, :]),
                [("VE", 1)])
            ve_x1b = w.c["VE"]
            w.wait("sync", "VE", ve_x1)
            x1s = "X1F0" if bi == 0 else "X1F1"
            w.y(lambda y, bi=bi, psl=psl, x1s=x1s: y.dma_start(
                out=d_x1h[psl, :], in_=x1f[bi][:, :]), [(x1s, 16)])
            x1f_rd[bi] = w.c[x1s]
            pend_tp.append((pt, ve_x1b))
            if len(pend_tp) > 2:
                tpt, tve = pend_tp.pop(0)
                tb3 = tpt % 3
                for cc in range(8):
                    ti = cc % 4
                    w.wait("tensor", "VE", max(tve, pt_rd[ti]))
                    w.t(lambda t, ti=ti, tb3=tb3, cc=cc: t.transpose(
                        p_t[ti][:, :], x1bb[tb3][:, cc * 128:(cc + 1) * 128],
                        c_id_b[:, :]), [("MM", 1)])
                    mm_t = w.c["MM"]
                    if cc == 7:
                        x1b_rd3[tb3] = mm_t
                    w.wait("vector", "MM", mm_t)
                    w.v(lambda v, cc=cc, tpt=tpt, ti=ti: v.tensor_copy(
                        x1T[:, cc, tpt * 128:(tpt + 1) * 128], p_t[ti][:, :]),
                        [("VE", 1)])
                    pt_rd[ti] = w.c["VE"]
        while pend_tp:
            tpt, tve = pend_tp.pop(0)
            tb3 = tpt % 3
            for cc in range(8):
                ti = cc % 2
                w.wait("tensor", "VE", max(tve, pt_rd[ti]))
                w.t(lambda t, ti=ti, tb3=tb3, cc=cc: t.transpose(
                    p_t[ti][:, :], x1bb[tb3][:, cc * 128:(cc + 1) * 128],
                    c_id_b[:, :]), [("MM", 1)])
                mm_t = w.c["MM"]
                if cc == 7:
                    x1b_rd3[tb3] = mm_t
                w.wait("vector", "MM", mm_t)
                w.v(lambda v, cc=cc, tpt=tpt, ti=ti: v.tensor_copy(
                    x1T[:, cc, tpt * 128:(tpt + 1) * 128], p_t[ti][:, :]),
                    [("VE", 1)])
                pt_rd[ti] = w.c["VE"]
        if "tap_x1" in tap_t:
            w.wait("sync", "X1F0", x1f_rd[0])
            w.wait("sync", "X1F1", x1f_rd[1])
            w.y(lambda y: y.dma_start(out=tap_t["tap_x1"], in_=d_x1h),
                [("TAP", 16)])
        if "tap_x1t" in tap_t:
            w.wait("sync", "VE", w.c["VE"])
            w.y(lambda y: y.dma_start(out=tap_t["tap_x1t"], in_=x1T[:, :, :]),
                [("TAP", 16)])
        if w.name == "sync":
            w.eng.wait_ge(w.sems["X1F0"], w.c["X1F0"])
            w.eng.wait_ge(w.sems["X1F1"], w.c["X1F1"])
            w.eng.wait_ge(w.sems["TAP"], w.c["TAP"])

    with nc.Block("blkA3") as blkA3:
        run_block(blkA3, emit_A3)

    st_a3.close()
    st_ytn.close()

    st_b = ExitStack()
    EB = st_b.enter_context
    y_acc = EB(nc.sbuf_tensor("y_acc", [128, NQT, D], F32))
    w13_b = [EB(nc.sbuf_tensor(f"w13_b{i}", [128, 8, 1024], BF16))
             for i in range(2)]
    w2_b = [EB(nc.sbuf_tensor(f"w2_b{i}", [128, 4, 1024], BF16))
            for i in range(2)]
    hT = [EB(nc.sbuf_tensor(f"hT{i}", [128, 4, BCH], BF16)) for i in range(2)]
    sg_b = EB(nc.sbuf_tensor("sg_b", [128, 512], BF16))
    gw_b = EB(nc.sbuf_tensor("gw_b", [128, 8, NE], BF16))
    gb_sb = EB(nc.sbuf_tensor("gb_sb", [NE, 1], F32))
    gbT_sb = EB(nc.sbuf_tensor("gbT_sb", [1, NE], F32))
    biT = EB(nc.sbuf_tensor("biT", [NE, S], F32))
    bi_pm = EB(nc.sbuf_tensor("bi_pm", [128, NPT, NE], F32))
    gbb = EB(nc.sbuf_tensor("gbb", [128, NE], F32))
    sp_pm = EB(nc.sbuf_tensor("sp_pm", [128, NE], F32))
    top8 = EB(nc.sbuf_tensor("top8", [128, 8], F32))
    msk_b = EB(nc.sbuf_tensor("msk_b", [128, NE], F32))
    wsel = EB(nc.sbuf_tensor("wsel", [128, NE], F32))
    dnr = EB(nc.sbuf_tensor("dnr", [128, 1], F32))
    comb = EB(nc.sbuf_tensor("comb", [128, NPT, NE], F32))
    x2res = [EB(nc.sbuf_tensor(f"x2res{i}", [128, D], F32)) for i in range(2)]
    x2pre = EB(nc.sbuf_tensor("x2pre", [128, D], F32))
    outf = [EB(nc.sbuf_tensor(f"outf{i}", [128, D], F32)) for i in range(2)]
    junk2 = EB(nc.sbuf_tensor("junk2", [128, D], BF16))
    st2 = EB(nc.sbuf_tensor("st2", [128, 16], F32))
    p_h = [EB(nc.psum_tensor(f"p_h{i}", [128, 512], F32)) for i in range(4)]
    p_zb = [EB(nc.psum_tensor(f"p_zb{i}", [128, 512], F32)) for i in range(2)]
    p_r16 = EB(nc.psum_tensor("p_r16", [128, 512], F32))
    p_rt2 = EB(nc.psum_tensor("p_rt2", [128, 16], F32))
    p_tb = p_r16[:, 32:48]
    p_tb2 = [p_r16[:, 64:80], p_rt2[:, :]]

    NEXP = NE + 1
    NGI = 4 * NEXP

    def exp_dma(w, e, bslot):
        ws = "W13A" if bslot == 0 else "W13B"
        if e < NE:
            w.g(lambda g, e=e, bslot=bslot: g.dma_start(
                out=w13_b[bslot][:, :, :],
                in_=d_w13T.rearrange("e (cc p) n -> e p cc n", p=128)[e]),
                [(ws, 16)])
            w.g(lambda g, e=e, bslot=bslot: g.dma_start(
                out=w2_b[bslot][:, :, :],
                in_=d_w2T.rearrange("e (cc p) n -> e p cc n", p=128)[e]),
                [(ws, 16)])
        else:
            w.g(lambda g, bslot=bslot: g.dma_start(
                out=w13_b[bslot][:, :, :],
                in_=d_w13sT.rearrange("(cc p) n -> p cc n", p=128)),
                [(ws, 16)])
            w.g(lambda g, bslot=bslot: g.dma_start(
                out=w2_b[bslot][:, :, :],
                in_=d_w2sT.rearrange("(cc p) n -> p cc n", p=128)),
                [(ws, 16)])
        return w.c[ws]

    def emit_B(w: Walk):
        w.g(lambda g: g.dma_start(
            out=gw_b[:, :, :],
            in_=d_gwT.rearrange("(cc p) n -> p cc n", p=128)), [("GW", 16)])
        gv_gw = w.c["GW"]
        w.y(lambda y: y.dma_start(out=gb_sb[:, :], in_=d_gb), [("GBS", 16)])
        w.y(lambda y: y.dma_start(out=gbT_sb[:, :], in_=d_gbT), [("GBS", 16)])
        sv_gb = w.c["GBS"]
        r16_rd = 0
        for ch in range(NCH):
            chs = slice(ch * 512, (ch + 1) * 512)
            w.wait("tensor", "GW", gv_gw)
            w.wait("tensor", "SC", r16_rd)
            for cc in range(8):
                w.t(lambda t, cc=cc, chs=chs: t.matmul(
                    p_r16[0:NE, :], gw_b[:, cc, :], x1T[:, cc, chs],
                    start=(cc == 0), stop=(cc == 7)),
                    [("MM", 1)] if cc == 7 else ())
            mm_r = w.c["MM"]
            w.wait("scalar", "MM", mm_r)
            w.s(lambda s, chs=chs: s.activation(biT[:, chs], p_r16[0:NE, :],
                                                AF.Sigmoid), [("SC", 1)])
            r16_rd = w.c["SC"]
        sc_router = w.c["SC"]
        w.wait("tensor", "GBS", sv_gb)
        w.wait("tensor", "SC", sc_router)
        w.t(lambda t: t.matmul(p_tb, c_ones_f[:, :], gbT_sb[:, :],
                               start=True, stop=True), [("MM", 1)])
        mm_gbb = w.c["MM"]
        w.wait("vector", "MM", mm_gbb)
        w.v(lambda v: v.tensor_copy(gbb[:, :], p_tb), [("VE", 1)])
        ptb_rd = w.c["VE"]
        ptb_rd2 = [ptb_rd, ptb_rd]
        w.wait("vector", "SC", sc_router)
        w.wait("vector", "GBS", sv_gb)
        w.v(lambda v: v.tensor_scalar(biT[:, :], biT[:, :], gb_sb[:, :], None,
                                      ALU.add), [("VE", 1)])
        ve_bi = w.c["VE"]

        def route_tiles(w, pts):
            nonlocal ptb_rd
            for pt in pts:
                tb = pt % 2
                w.wait("tensor", "VE", max(ve_bi, ptb_rd2[tb]))
                w.t(lambda t, pt=pt, tb=tb: t.transpose(
                    p_tb2[tb], biT[:, pt * 128:(pt + 1) * 128],
                    c_id_f[0:16, 0:16]), [("MM", 1)])
                mm_t = w.c["MM"]
                w.wait("vector", "MM", mm_t)
                w.drain("vector")
                w.v(lambda v, pt=pt, tb=tb: v.tensor_copy(bi_pm[:, pt, :],
                                                          p_tb2[tb]),
                    [("VE", 1)])
                ptb_rd2[tb] = w.c["VE"]
                ptb_rd = w.c["VE"]
                w.drain("vector")
                w.v(lambda v, pt=pt: v.tensor_tensor(
                    sp_pm[:, :], bi_pm[:, pt, :], gbb[:, :], ALU.subtract),
                    [("VE", 1)])
                w.drain("vector")
                w.v(lambda v, pt=pt: v.max(out=top8[:, :],
                                           in_=bi_pm[:, pt, :]), [("VE", 1)])
                w.drain("vector")
                w.v(lambda v, pt=pt: v.tensor_scalar(
                    msk_b[:, :], bi_pm[:, pt, :], top8[:, 3:4], None,
                    ALU.is_ge), [("VE", 1)])
                w.drain("vector")
                w.v(lambda v: v.tensor_tensor(wsel[:, :], sp_pm[:, :],
                                              msk_b[:, :], ALU.mult),
                    [("VE", 1)])
                w.drain("vector")
                w.v(lambda v: v.tensor_reduce(dnr[:, :], wsel[:, :], AX.X,
                                              ALU.add), [("VE", 1)])
                w.drain("vector")
                w.v(lambda v: v.reciprocal(dnr[:, :], dnr[:, :]),
                    [("VE", 1)])
                w.drain("vector")
                w.v(lambda v, pt=pt: v.tensor_scalar(
                    comb[:, pt, :], wsel[:, :], dnr[:, :], ROUTE_SCALE,
                    ALU.mult, ALU.mult), [("VE", 1)])

        route_tiles(w, range(0, NQT))

        wload = [0, 0]
        w13_rd = [0, 0]
        hT_rd = [0, 0]
        ph_rd = [("SC", 0), ("VE", 0), ("SC", 0), ("VE", 0)]
        pz_rd = [0, 0]
        sg_rd = 0
        x2res_rd = [0, 0]
        outf_rd = [0, 0]
        x2pre_rd = 0
        st2_rd = 0
        tap_yg = [0]
        wload[0] = exp_dma(w, 0, 0)
        for gi in range(NGI):
            qt_i, e = divmod(gi, NEXP)
            bslot = gi % 2
            if e == NEXP // 2 and qt_i + 1 < 4:
                route_tiles(w, range((qt_i + 1) * NQT, (qt_i + 2) * NQT))
            if gi + 1 < NGI:
                nslot = (gi + 1) % 2
                w.wait("gpsimd", "MM", w13_rd[nslot])
                wload[nslot] = exp_dma(w, (gi + 1) % NEXP, nslot)
            def emit_z(w, ch, ve_h, bslot=bslot, qt_i=qt_i, e=e):
                nonlocal pz_rd, hT_rd, w13_rd
                w.drain("vector")
                for pst in range(NST):
                    t_i = ch * NST + pst
                    pt_g = qt_i * NQT + t_i
                    for co in range(2):
                        zb = co
                        w.wait("tensor", "VE", max(ve_h, pz_rd[zb]))
                        for iop in range(4):
                            w.t(lambda t, zb=zb, iop=iop, ch=ch, pst=pst,
                                bslot=bslot, co=co: t.matmul(
                                    p_zb[zb][:, :],
                                    hT[ch % 2][:, iop,
                                               pst * 128:(pst + 1) * 128],
                                    w2_b[bslot][:, iop,
                                                co * 512:(co + 1) * 512],
                                    start=(iop == 0), stop=(iop == 3)),
                                [("MM", 1)] if iop == 3 else ())
                        mm_zb = w.c["MM"]
                        hT_rd[ch % 2] = mm_zb
                        w13_rd[bslot] = mm_zb
                        w.wait("vector", "MM", mm_zb)
                        w.wait("vector", "TAP", tap_yg[0])
                        csl = slice(co * 512, (co + 1) * 512)
                        if e == 0:
                            w.v(lambda v, t_i=t_i, csl=csl, zb=zb, pt_g=pt_g:
                                v.tensor_scalar(y_acc[:, t_i, csl],
                                                p_zb[zb][:, :],
                                                comb[:, pt_g, 0:1], None,
                                                ALU.mult), [("VE", 1)])
                        elif e < NE:
                            w.v(lambda v, t_i=t_i, csl=csl, zb=zb, pt_g=pt_g,
                                e=e: v.scalar_tensor_tensor(
                                    y_acc[:, t_i, csl], p_zb[zb][:, :],
                                    comb[:, pt_g, e:e + 1],
                                    y_acc[:, t_i, csl],
                                    ALU.mult, ALU.add), [("VE", 1)])
                        else:
                            w.v(lambda v, t_i=t_i, csl=csl, zb=zb:
                                v.scalar_tensor_tensor(
                                    y_acc[:, t_i, csl], p_zb[zb][:, :], 1.0,
                                    y_acc[:, t_i, csl],
                                    ALU.mult, ALU.add), [("VE", 1)])
                        pz_rd[zb] = w.c["VE"]

            pend_z = None
            for ch in range(NBCH):
                gpos = qt_i * QTR + ch * BCH
                gsl = slice(gpos, gpos + BCH)
                w.wait("tensor", "W13A" if bslot == 0 else "W13B",
                       wload[bslot])
                if ch > 0:
                    w.wait("tensor", "MM", hT_rd[ch % 2])
                for iop in range(4):
                    pa = 2 * (iop % 2)
                    pb_ = pa + 1
                    for pslot, iotile in ((pa, iop), (pb_, 4 + iop)):
                        sem, cnt = ph_rd[pslot]
                        w.wait("tensor", sem, cnt)
                        for cc in range(8):
                            w.t(lambda t, pslot=pslot, cc=cc, iotile=iotile,
                                bslot=bslot, gsl=gsl: t.matmul(
                                    p_h[pslot][:, 0:BCH],
                                    w13_b[bslot][:, cc,
                                                 iotile * 128:
                                                 (iotile + 1) * 128],
                                    x1T[:, cc, gsl],
                                    start=(cc == 0), stop=(cc == 7)),
                                [("MM", 1)] if cc == 7 else ())
                    mm_gu = w.c["MM"]
                    w.wait("scalar", "MM", mm_gu)
                    w.wait("scalar", "VE", sg_rd)
                    w.s(lambda s, pa=pa: s.activation(sg_b[:, 0:BCH],
                                                      p_h[pa][:, 0:BCH],
                                                      AF.Silu), [("SC", 1)])
                    sc_silu = w.c["SC"]
                    ph_rd[pa] = ("SC", sc_silu)
                    w.wait("vector", "SC", sc_silu)
                    w.wait("vector", "MM", hT_rd[ch % 2])
                    w.v(lambda v, iop=iop, ch=ch, pb_=pb_: v.tensor_tensor(
                        hT[ch % 2][:, iop, :], sg_b[:, 0:BCH],
                        p_h[pb_][:, 0:BCH], ALU.mult), [("VE", 1)])
                    sg_rd = w.c["VE"]
                    ph_rd[pb_] = ("VE", w.c["VE"])
                ve_h = w.c["VE"]
                if pend_z is not None:
                    emit_z(w, pend_z[0], pend_z[1])
                pend_z = (ch, ve_h)
            emit_z(w, pend_z[0], pend_z[1])
            if e != NE:
                continue
            if "tap_comb" in tap_t and qt_i == 3 and e == NE:
                w.wait("sync", "VE", w.c["VE"])
                w.y(lambda y: y.dma_start(out=tap_t["tap_comb"],
                                          in_=comb[:, :, :]), [("TAP", 16)])
            if "tap_yacc" in tap_t and qt_i == 0:
                w.wait("sync", "VE", w.c["VE"])
                w.y(lambda y: y.dma_start(out=tap_t["tap_yacc"],
                                          in_=y_acc[:, :, :]), [("TAP", 16)])
                tap_yg[0] = w.c["TAP"]
            w.drain("vector")
            for t_i in range(NQT):
                bi = t_i % 2
                pt_g = qt_i * NQT + t_i
                psl = slice(pt_g * 128, (pt_g + 1) * 128)
                w.wait("sync", "VE", x2res_rd[bi])
                x2s = "X2RES0" if bi == 0 else "X2RES1"
                w.y(lambda y, psl=psl, bi=bi, x2s=x2s: y.dma_start(
                    out=x2res[bi][:, :], in_=d_x1h[psl, :]), [(x2s, 16)])
                sv_r = w.c[x2s]
                w.wait("vector", x2s, sv_r)
                w.wait("vector", "SC", x2pre_rd)
                w.v(lambda v, t_i=t_i, bi=bi: v.tensor_tensor(
                    x2pre[:, :], y_acc[:, t_i, :], x2res[bi][:, :], ALU.add),
                    [("VE", 1)])
                x2res_rd[bi] = w.c["VE"]
                ve_pre = w.c["VE"]
                w.drain("vector")
                w.v(lambda v: v.bn_stats(st2[:, 0:6], x2pre[:, 0:512]),
                    [("VE", 1)])
                w.drain("vector")
                w.v(lambda v: v.bn_stats(st2[:, 6:12], x2pre[:, 512:1024]),
                    [("VE", 1)])
                w.drain("vector")
                w.wait("vector", "SC", st2_rd)
                w.v(lambda v: v.bn_aggr(st2[:, 12:14], st2[:, 0:12]),
                    [("VE", 1)])
                w.drain("vector")
                w.v(lambda v: v.tensor_scalar(st2[:, 14:15], st2[:, 13:14],
                                              1.0, EPS, ALU.mult, ALU.add),
                    [("VE", 1)])
                ve_var = w.c["VE"]
                w.wait("scalar", "VE", ve_var)
                w.s(lambda s: s.activation(st2[:, 15:16], st2[:, 14:15],
                                           AF.Ln), [("SC", 1)])
                w.drain("scalar")
                w.s(lambda s: s.activation(st2[:, 14:15], st2[:, 15:16],
                                           AF.Exp, scale=-0.5), [("SC", 1)])
                sc_rstd = w.c["SC"]
                st2_rd = sc_rstd
                x2pre_rd = sc_rstd
                w.wait("vector", "SC", sc_rstd)
                w.wait("vector", "OUT0" if bi == 0 else "OUT1", outf_rd[bi])
                w.drain("vector")
                w.v(lambda v, bi=bi: v.tensor_scalar(outf[bi][:, :],
                                                     x2pre[:, :],
                                                     st2[:, 12:13],
                                                     st2[:, 14:15],
                                                     ALU.subtract, ALU.mult),
                    [("VE", 1)])
                ve_out = w.c["VE"]
                w.wait("sync", "VE", ve_out)
                ots = "OUT0" if bi == 0 else "OUT1"
                w.y(lambda y, bi=bi, psl=psl, ots=ots: y.dma_start(
                    out=d_out[psl, :], in_=outf[bi][:, :]), [(ots, 16)])
                outf_rd[bi] = w.c[ots]
        if w.name == "sync":
            w.eng.wait_ge(w.sems["OUT0"], w.c["OUT0"])
            w.eng.wait_ge(w.sems["OUT1"], w.c["OUT1"])
            w.eng.wait_ge(w.sems["TAP"], w.c["TAP"])

    with nc.Block("blkB") as blkB:
        run_block(blkB, emit_B)

    st_b.close()
    st_x1t.close()
    st_persist.close()

    base = walks["tensor"].c
    for n, wk in walks.items():
        assert wk.c == base, (n, wk.c, base)
    return nc


def _host_consts():
    ident = np.eye(128, dtype=np.float32)
    k = np.arange(128)[:, None]
    q = np.arange(128)[None, :]
    cmask = np.ascontiguousarray((q >= k).astype(np.float32))
    ones = np.ones((1, 128), dtype=np.float32)
    o2w = np.zeros((128, 128), dtype=np.float32)
    o2w[:, 0] = 1.0
    o64 = np.ones((128, 64), dtype=np.float32)
    return ident, cmask, ones, o2w, o64


def make_shared_inputs(L0_attn_W_Q, L0_attn_W_K, L0_attn_W_V, L0_attn_W_O,
                       L0_ffn_gate_w, L0_ffn_gate_b, L0_ffn_w13, L0_ffn_w2,
                       L0_ffn_w13_s, L0_ffn_w2_s):
    ident, cmask, ones, o2w, o64 = _host_consts()
    f32 = np.float32
    A = lambda a: np.ascontiguousarray(np.asarray(a, f32))
    return {
        "wqT": A(np.asarray(L0_attn_W_Q, f32).T),
        "wkT": A(np.asarray(L0_attn_W_K, f32).T),
        "wvT": A(np.asarray(L0_attn_W_V, f32).T),
        "woT": A(np.asarray(L0_attn_W_O, f32).T),
        "gwT": A(np.asarray(L0_ffn_gate_w, f32).T),
        "gb": A(np.asarray(L0_ffn_gate_b, f32)[:, None]),
        "gbT": A(np.asarray(L0_ffn_gate_b, f32)[None, :]),
        "w13T": A(np.asarray(L0_ffn_w13, f32).transpose(0, 2, 1)),
        "w2T": A(np.asarray(L0_ffn_w2, f32).transpose(0, 2, 1)),
        "w13sT": A(np.asarray(L0_ffn_w13_s, f32).T),
        "w2sT": A(np.asarray(L0_ffn_w2_s, f32).T),
        "ident": ident, "ltri": cmask, "ones": ones, "ones2w": o2w,
        "ones64": o64,
    }


def kernel(L0_attn_x, L0_attn_W_Q, L0_attn_W_K, L0_attn_W_V, L0_attn_W_O,
           L0_ffn_input_ids, L0_ffn_gate_w, L0_ffn_gate_b, L0_ffn_w13,
           L0_ffn_w2, L0_ffn_w13_s, L0_ffn_w2_s):
    shared = make_shared_inputs(L0_attn_W_Q, L0_attn_W_K, L0_attn_W_V,
                                L0_attn_W_O, L0_ffn_gate_w, L0_ffn_gate_b,
                                L0_ffn_w13, L0_ffn_w2, L0_ffn_w13_s,
                                L0_ffn_w2_s)
    xs = np.asarray(L0_attn_x, np.float32)
    in_maps = []
    for c in range(8):
        x = np.ascontiguousarray(xs[c])
        xT = np.ascontiguousarray(x.T)
        m = {"x": x, "xT": xT}
        m.update(shared)
        in_maps.append(m)
    nc = build_program(S=4096)
    res = run_bass_kernel_spmd(nc, in_maps, core_ids=list(range(8)))
    out = np.stack([np.asarray(r["out"], dtype=np.float32)
                    for r in res.results], axis=0)
    return out


# version 18
# speedup vs baseline: 1.1699x; 1.0145x over previous
import numpy as np
from contextlib import ExitStack

import concourse.bass as bass
import concourse.mybir as mybir
from concourse.bass_utils import run_bass_kernel_spmd

F32 = mybir.dt.float32
BF16 = mybir.dt.bfloat16
AF = mybir.ActivationFunctionType
ALU = mybir.AluOpType
AX = mybir.AxisListType

D = 1024
H = 16
NE = 16
INTER = 512
ROUTE_SCALE = 2.5
EPS = 1e-5
NEG = -1.0e9

ENGS = ("tensor", "scalar", "vector", "gpsimd", "sync")


class Walk:

    def __init__(self, name, sem_names):
        self.name = name
        self.eng = None
        self.sems = None
        self.c = {k: 0 for k in sem_names}
        self._lastw = {}

    def _emit(self, owner, fn, incs):
        inst = fn(self.eng) if self.name == owner else None
        for sem, n in incs:
            self.c[sem] += n
            if inst is not None:
                inst.then_inc(self.sems[sem], n)

    def t(self, fn, incs=()):
        self._emit("tensor", fn, incs)

    def s(self, fn, incs=()):
        self._emit("scalar", fn, incs)

    def v(self, fn, incs=()):
        self._emit("vector", fn, incs)

    def g(self, fn, incs=()):
        self._emit("gpsimd", fn, incs)

    def y(self, fn, incs=()):
        self._emit("sync", fn, incs)

    def wait(self, eng_name, sem, val):
        if val <= 0 or self.name != eng_name:
            return
        if self._lastw.get(sem, 0) >= val:
            return
        self._lastw[sem] = val
        self.eng.wait_ge(self.sems[sem], val)

    def drain(self, eng_name):
        if self.name == eng_name:
            self.eng.drain()


def build_program(S=4096, taps=()):
    taps = set(taps)
    NQB = S // 512
    NKT = S // 128
    NCH = S // 512
    NPT = S // 128
    QTR = S // 4
    NQT = NPT // 4
    BCH = min(512, QTR)
    NBCH = QTR // BCH
    NST = BCH // 128
    SQW = 2048

    nc = bass.Bass("TRN2", target_bir_lowering=False)

    d_x = nc.dram_tensor("x", [S, D], F32, kind="ExternalInput").ap()
    d_xT = nc.dram_tensor("xT", [D, S], F32, kind="ExternalInput").ap()
    d_wqT = nc.dram_tensor("wqT", [D, D], F32, kind="ExternalInput").ap()
    d_wkT = nc.dram_tensor("wkT", [D, D], F32, kind="ExternalInput").ap()
    d_wvT = nc.dram_tensor("wvT", [D, D], F32, kind="ExternalInput").ap()
    d_woT = nc.dram_tensor("woT", [D, D], F32, kind="ExternalInput").ap()
    d_gwT = nc.dram_tensor("gwT", [D, NE], F32, kind="ExternalInput").ap()
    d_gb = nc.dram_tensor("gb", [NE, 1], F32, kind="ExternalInput").ap()
    d_gbT = nc.dram_tensor("gbT", [1, NE], F32, kind="ExternalInput").ap()
    d_w13T = nc.dram_tensor("w13T", [NE, D, D], F32, kind="ExternalInput").ap()
    d_w2T = nc.dram_tensor("w2T", [NE, INTER, D], F32, kind="ExternalInput").ap()
    d_w13sT = nc.dram_tensor("w13sT", [D, D], F32, kind="ExternalInput").ap()
    d_w2sT = nc.dram_tensor("w2sT", [INTER, D], F32, kind="ExternalInput").ap()
    d_ident = nc.dram_tensor("ident", [128, 128], F32, kind="ExternalInput").ap()
    d_ltri = nc.dram_tensor("ltri", [128, 128], F32, kind="ExternalInput").ap()
    d_ones = nc.dram_tensor("ones", [1, 128], F32, kind="ExternalInput").ap()
    d_ones2w = nc.dram_tensor("ones2w", [128, 128], F32, kind="ExternalInput").ap()
    d_ones64 = nc.dram_tensor("ones64", [128, 64], F32, kind="ExternalInput").ap()

    d_out = nc.dram_tensor("out", [S, D], F32, kind="ExternalOutput").ap()
    d_x1h = nc.dram_tensor("x1h", [S, D], F32, kind="Internal").ap()

    tap_t = {}
    for name, shape, dt in (
        ("tap_qt", [128, 4, S], BF16), ("tap_kt", [128, 4, S], BF16),
        ("tap_v", [128, NKT, 8, 65], BF16), ("tap_ytn", [2, 128, 4, S], BF16),
        ("tap_x1", [S, D], F32), ("tap_comb", [128, NPT, NE], F32),
        ("tap_x1t", [128, 8, S], BF16), ("tap_negm", [128, 2], F32),
        ("tap_yacc", [128, NQT, D], F32),
    ):
        if name in taps:
            tap_t[name] = nc.dram_tensor(name, shape, dt,
                                         kind="ExternalOutput").ap()

    sem_names = ("MM", "SC", "VE", "CONSTG", "CONSTS", "WQKV", "XTB",
                 "WO", "XRES0", "XRES1", "XRES2", "X1F0", "X1F1", "GW",
                 "GBS", "XTB2",
                 "W13A", "W13B", "X2RES0", "X2RES1", "OUT0", "OUT1",
                 "YST0", "YST1", "TAP")
    walks = {n: Walk(n, sem_names) for n in ENGS}

    st_persist = ExitStack()
    E = st_persist.enter_context
    semd = {n: E(nc.semaphore(n)) for n in sem_names}
    for wk in walks.values():
        wk.sems = semd

    c_id_b = E(nc.sbuf_tensor("c_id_b", [128, 128], BF16))
    c_id_f = E(nc.sbuf_tensor("c_id_f", [128, 128], F32))
    c_ones_f = E(nc.sbuf_tensor("c_ones_f", [1, 128], F32))
    c_o64_f = E(nc.sbuf_tensor("c_o64_f", [128, 64], F32))
    negm_b = E(nc.sbuf_tensor("negm_b", [128, 2], F32))

    st_ytn = ExitStack()
    yTn = [st_ytn.enter_context(nc.sbuf_tensor(f"yTn{g}", [128, 4, S], BF16))
           for g in range(2)]

    st_a = ExitStack()
    EA = st_a.enter_context
    c_ltri_b = EA(nc.sbuf_tensor("c_ltri_b", [128, 128], BF16))
    c_o64_b = EA(nc.sbuf_tensor("c_o64_b", [128, 64], BF16))
    c_o2w_b = EA(nc.sbuf_tensor("c_o2w_b", [128, 128], BF16))
    xTb = EA(nc.sbuf_tensor("xTb", [128, 8, 512], BF16))
    xTb2 = yTn[1][:, 0, :].rearrange("p (a b) -> p a b", a=8)
    xtbs = [xTb[:, :, :], xTb2]
    wq_b = EA(nc.sbuf_tensor("wq_b", [128, 8, 512], BF16))
    wk_b = EA(nc.sbuf_tensor("wk_b", [128, 8, 512], BF16))
    wv_b = EA(nc.sbuf_tensor("wv_b", [128, 8, 512], BF16))
    QT = EA(nc.sbuf_tensor("QT", [128, 4, S], BF16))
    KT = EA(nc.sbuf_tensor("KT", [128, 4, S], BF16))
    Vb = EA(nc.sbuf_tensor("Vb", [128, NKT, 8, 65], BF16))
    sq2 = EA(nc.sbuf_tensor("sq2", [128, SQW], BF16))
    ptb = [[sq2[:, (2 * i + e) * 512:(2 * i + e + 1) * 512]
            for i in range(2)] for e in range(2)]
    ptb3 = [[ptb[e][0], ptb[e][1], xTb[:, e, :]] for e in range(2)]
    mxq = EA(nc.sbuf_tensor("mxq", [128, 1], F32))
    mxk = EA(nc.sbuf_tensor("mxk", [128, 1], F32))
    mxt = EA(nc.sbuf_tensor("mxt", [128, 1], F32))
    sc1 = EA(nc.sbuf_tensor("sc1", [1, 4], F32))
    rdf = EA(nc.sbuf_tensor("rdf", [65, 1024], BF16))
    yev = EA(nc.sbuf_tensor("yev", [65, 2, 512], BF16))
    rb_t = EA(nc.sbuf_tensor("rb_t", [64, 512], F32))
    rb_sb = rb_t[:, :]
    yst = [EA(nc.sbuf_tensor("yst0", [64, 512], BF16))]
    p_sq = EA(nc.psum_tensor("p_sq", [128, 2048], F32))
    p_s = [p_sq[:, i * 512:(i + 1) * 512] for i in range(4)]
    p_y = [EA(nc.psum_tensor(f"p_y{e}", [65, 512], F32)) for e in range(2)]
    p_rb = EA(nc.psum_tensor("p_rb", [64, 512], F32))

    def emit_A(w: Walk):
        w.g(lambda g: g.dma_start(out=c_id_b[:, :], in_=d_ident),
            [("CONSTG", 16)])
        w.g(lambda g: g.dma_start(out=c_ltri_b[:, :], in_=d_ltri),
            [("CONSTG", 16)])
        w.g(lambda g: g.dma_start(out=c_o64_b[:, :], in_=d_ones64),
            [("CONSTG", 16)])
        w.g(lambda g: g.dma_start(out=c_o2w_b[:, :], in_=d_ones2w),
            [("CONSTG", 16)])
        gv_consts = w.c["CONSTG"]
        w.y(lambda y: y.dma_start(out=c_id_f[:, :], in_=d_ident),
            [("CONSTS", 16)])
        w.y(lambda y: y.dma_start(out=c_ones_f[:, :], in_=d_ones),
            [("CONSTS", 16)])
        w.y(lambda y: y.dma_start(out=c_o64_f[:, :], in_=d_ones64),
            [("CONSTS", 16)])
        sv_consts = w.c["CONSTS"]
        w.v(lambda v: v.memset(Vb[:, :, :, 64:65], 1.0), [("VE", 1)])
        ve_ones = w.c["VE"]

        wT_loaded = [0, 0]
        xtb_rd2 = [0, 0]
        wslice_rd = 0
        vb_rd_prev = 0
        qt_rd_prev = 0
        sq2_rd4 = [0, 0, 0, 0]
        ps_wr = [0] * 4
        ps_rd = [("SC", 0)] * 4
        py_rd = [0, 0]
        ptb_rd = [[0, 0, 0], [0, 0, 0]]
        ptb_ve = [[0, 0, 0], [0, 0, 0]]
        rb_rd = 0
        ti_rd = 0
        yst_st = [0, 0]
        yev_rd = 0

        for g in range(2):
            gsl = slice(g * 512, (g + 1) * 512)
            w.wait("gpsimd", "MM", wslice_rd)
            for dsrc, dst in ((d_wqT, wq_b), (d_wkT, wk_b), (d_wvT, wv_b)):
                w.g(lambda gg, dsrc=dsrc, dst=dst: gg.dma_start(
                    out=dst[:, :, :],
                    in_=dsrc.rearrange("(cc p) n -> p cc n", p=128)[:, :, gsl]),
                    [("WQKV", 16)])
            wT_loaded[g] = w.c["WQKV"]

            for ch in range(NCH):
                cb = ch % 2
                w.wait("gpsimd", "MM", xtb_rd2[cb])
                xts = "XTB" if cb == 0 else "XTB2"
                w.g(lambda gg, ch=ch, cb=cb, xts=xts: gg.dma_start(
                    out=xtbs[cb],
                    in_=d_xT.rearrange("(cc p) n -> p cc n", p=128)
                    [:, :, ch * 512:(ch + 1) * 512]), [(xts, 16)])
                gv_x = w.c[xts]
                w.wait("tensor", xts, gv_x)
                w.wait("tensor", "WQKV", wT_loaded[g])
                chs = slice(ch * 512, (ch + 1) * 512)
                for wi, (wbuf, dstT) in enumerate(((wq_b, QT), (wk_b, KT))):
                    for dt in range(4):
                        pb = (wi * 4 + dt) % 4
                        sem, cnt = ps_rd[pb]
                        w.wait("tensor", sem, cnt)
                        for cc in range(8):
                            w.t(lambda t, pb=pb, wbuf=wbuf, dt=dt, cc=cc,
                                cb=cb: t.matmul(
                                    p_s[pb],
                                    wbuf[:, cc, dt * 128:(dt + 1) * 128],
                                    xtbs[cb][:, cc, :],
                                    start=(cc == 0), stop=(cc == 7)),
                                [("MM", 1)] if cc == 7 else ())
                        mmv = w.c["MM"]
                        ps_wr[pb] = mmv
                        w.wait("vector", "MM", max(mmv, qt_rd_prev))
                        w.v(lambda v, pb=pb, dstT=dstT, dt=dt, chs=chs:
                            v.tensor_copy(dstT[:, dt, chs], p_s[pb]),
                            [("VE", 1)])
                        ps_rd[pb] = ("VE", w.c["VE"])
                for pt in range(4):
                    pb = pt % 4
                    sem, cnt = ps_rd[pb]
                    w.wait("tensor", sem, cnt)
                    for cc in range(8):
                        w.t(lambda t, pb=pb, cc=cc, pt=pt, cb=cb:
                            t.matmul(p_s[pb],
                                     xtbs[cb][:, cc, pt * 128:(pt + 1) * 128],
                                     wv_b[:, cc, :],
                                     start=(cc == 0), stop=(cc == 7)),
                            [("MM", 1)] if cc == 7 else ())
                    mmv = w.c["MM"]
                    ps_wr[pb] = mmv
                    kt_i = ch * 4 + pt
                    w.wait("vector", "VE", ve_ones)
                    w.wait("vector", "MM", max(mmv, vb_rd_prev))
                    w.v(lambda v, pb=pb, kt_i=kt_i:
                        v.tensor_copy(Vb[:, kt_i, 0:8, 0:64],
                                      p_s[pb].rearrange(
                                          "p (h d) -> p h d", d=64)),
                        [("VE", 1)])
                    ps_rd[pb] = ("VE", w.c["VE"])
                ptmax = max(ptb_rd[0] + ptb_rd[1])
                w.wait("tensor", "CONSTG", gv_consts)

                def norm_mm(w, item):
                    sqsl, nb, scv, mx, first = item
                    sem, cnt = ps_rd[nb]
                    w.wait("tensor", sem, cnt)
                    w.wait("tensor", "SC", scv)
                    w.t(lambda t, sqsl=sqsl, nb=nb: t.matmul(
                        p_s[nb], c_o2w_b[:, :], sq2[:, sqsl],
                        start=True, stop=True), [("MM", 1)])
                    ps_wr[nb] = w.c["MM"]
                    sq2_rd4[nb] = w.c["MM"]
                    w.wait("vector", "MM", ps_wr[nb])
                    w.v(lambda v, nb=nb: v.tensor_reduce(
                        mxt[:, :], p_s[nb], AX.X, ALU.max), [("VE", 1)])
                    ps_rd[nb] = ("VE", w.c["VE"])
                    w.drain("vector")
                    if first:
                        w.v(lambda v, mx=mx: v.tensor_copy(
                            mx[:, :], mxt[:, :]), [("VE", 1)])
                    else:
                        w.v(lambda v, mx=mx: v.tensor_tensor(
                            mx[:, :], mx[:, :], mxt[:, :], ALU.max),
                            [("VE", 1)])
                    w.drain("vector")

                pend_nm = None
                for si, (nsrc, mx) in enumerate(((QT, mxq), (KT, mxk))):
                    for pr in range(4):
                        nb = (si * 4 + pr) % 4
                        sqsl = slice(nb * 512, nb * 512 + 512)
                        w.wait("scalar", "MM", max(sq2_rd4[nb], ptmax))
                        w.wait("scalar", "VE", w.c["VE"])
                        w.s(lambda s, nsrc=nsrc, pr=pr, chs=chs, sqsl=sqsl:
                            s.activation(
                                sq2[:, sqsl], nsrc[:, pr, chs],
                                AF.Square), [("SC", 1)])
                        scv = w.c["SC"]
                        if pend_nm is not None:
                            norm_mm(w, pend_nm)
                        pend_nm = (sqsl, nb, scv, mx, ch == 0 and pr == 0)
                norm_mm(w, pend_nm)
                xtb_rd2[cb] = w.c["MM"]
            wslice_rd = w.c["MM"]
            ve_a1 = w.c["VE"]
            w.v(lambda v: v.tensor_tensor(sc1[:, 2:3], mxq[0:1, 0:1],
                                          mxk[0:1, 0:1], ALU.mult),
                [("VE", 1)])
            ve_m = w.c["VE"]
            w.wait("scalar", "VE", ve_m)
            w.s(lambda s: s.activation(sc1[:, 3:4], sc1[:, 2:3], AF.Ln),
                [("SC", 1)])
            w.drain("scalar")
            w.s(lambda s: s.activation(sc1[:, 2:3], sc1[:, 3:4], AF.Exp,
                                       scale=0.5), [("SC", 1)])
            scv_m = w.c["SC"]
            w.wait("vector", "SC", scv_m)
            w.drain("vector")
            w.v(lambda v: v.tensor_scalar_mul(sc1[:, 3:4], sc1[:, 2:3],
                                              -0.125), [("VE", 1)])
            ve_negm = w.c["VE"]
            sem0, cnt0 = ps_rd[0]
            w.wait("tensor", sem0, cnt0)
            w.wait("tensor", "VE", max(ve_negm, ti_rd))
            w.wait("tensor", "CONSTS", sv_consts)
            w.t(lambda t: t.matmul(p_sq[:, 0:1], c_ones_f[:, :], sc1[:, 3:4],
                                   start=True, stop=True), [("MM", 1)])
            mm_ti = w.c["MM"]
            w.wait("vector", "MM", mm_ti)
            w.v(lambda v, g=g: v.tensor_copy(negm_b[:, g:g + 1], p_sq[:, 0:1]),
                [("VE", 1)])
            ti_rd = w.c["VE"]
            ps_rd[0] = ("VE", w.c["VE"])
            ve_negm_done = w.c["VE"]

            w.wait("tensor", "CONSTG", gv_consts)
            pending_tail = [None]

            def flush_tail(w):
                if pending_tail[0] is None:
                    return
                fn = pending_tail[0]
                pending_tail[0] = None
                fn(w)

            for pr in range(4):
                for qb in range(NQB):
                    kts = 4 * qb + 4
                    qsl = slice(qb * 512, (qb + 1) * 512)
                    exp_v = [[("SC", 0)] * 3, [("SC", 0)] * 3]
                    for kt in range(kts + 2):
                        if kt == min(10, kts):
                            flush_tail(w)
                        if kt < kts:
                            j = kt - 4 * qb
                            i2 = kt % 2
                            i3 = kt % 3
                            qs = 128 * j if j > 0 else 0
                            for e in range(2):
                                sem, cnt = ps_rd[2 * i2 + e]
                                w.wait("tensor", sem, cnt)
                            w.wait("tensor", "VE", ve_a1)
                            for e in range(2):
                                hp = 64 * e
                                w.t(lambda t, i2=i2, e=e, hp=hp, pr=pr,
                                    kt=kt, qb=qb, qs=qs: t.matmul(
                                        p_s[2 * i2 + e][:, qs:512],
                                        KT[hp:hp + 64, pr,
                                           kt * 128:(kt + 1) * 128],
                                        QT[hp:hp + 64, pr,
                                           qb * 512 + qs:(qb + 1) * 512],
                                        start=True, stop=True,
                                        tile_position=(hp, 0)), [("MM", 1)])
                                ps_wr[2 * i2 + e] = w.c["MM"]
                            w.wait("scalar", "MM",
                                   max(ps_wr[2 * i2], ps_wr[2 * i2 + 1],
                                       ptb_rd[0][i3], ptb_rd[1][i3],
                                       max(sq2_rd4)))
                            w.wait("scalar", "VE",
                                   max(ve_negm_done, ptb_ve[0][i3],
                                       ptb_ve[1][i3]))
                            if qs == 0 and i3 < 2:
                                w.s(lambda s, i3=i3, i2=i2, g=g:
                                    s.activation(
                                        sq2[:, 2 * i3 * 512:
                                            (2 * i3 + 2) * 512],
                                        p_sq[:, 2 * i2 * 512:
                                             (2 * i2 + 2) * 512],
                                        AF.Exp, bias=negm_b[:, g:g + 1],
                                        scale=0.125), [("SC", 1)])
                            elif qs == 0:
                                w.s(lambda s, i2=i2, g=g: s.activation(
                                    xTb[:, 0:2, :].rearrange(
                                        "p a b -> p (a b)"),
                                    p_sq[:, 2 * i2 * 512:
                                         (2 * i2 + 2) * 512],
                                    AF.Exp, bias=negm_b[:, g:g + 1],
                                    scale=0.125), [("SC", 1)])
                            elif i3 < 2:
                                w.s(lambda s, i3=i3, i2=i2, g=g, qs=qs:
                                    s.activation(
                                        sq2[:, 2 * i3 * 512 + qs:
                                            (2 * i3 + 2) * 512],
                                        p_sq[:, 2 * i2 * 512 + qs:
                                             (2 * i2 + 2) * 512],
                                        AF.Exp, bias=negm_b[:, g:g + 1],
                                        scale=0.125), [("SC", 1)])
                            else:
                                w.s(lambda s, i2=i2, g=g, qs=qs:
                                    s.activation(
                                        xTb[:, 0:2, :].rearrange(
                                            "p a b -> p (a b)")[:, qs:1024],
                                        p_sq[:, 2 * i2 * 512 + qs:
                                             (2 * i2 + 2) * 512],
                                        AF.Exp, bias=negm_b[:, g:g + 1],
                                        scale=0.125), [("SC", 1)])
                            sc_exp = w.c["SC"]
                            ps_rd[2 * i2] = ("SC", sc_exp)
                            ps_rd[2 * i2 + 1] = ("SC", sc_exp)
                            if j >= 0:
                                w.wait("vector", "SC", sc_exp)
                                w.wait("vector", "CONSTG", gv_consts)
                                for e in range(2):
                                    w.v(lambda v, i3=i3, e=e, qs=qs:
                                        v.tensor_tensor(
                                            ptb3[e][i3][:, qs:qs + 128],
                                            ptb3[e][i3][:, qs:qs + 128],
                                            c_ltri_b[:, :], ALU.mult),
                                        [("VE", 1)])
                                    ptb_ve[e][i3] = w.c["VE"]
                                exp_v[0][i3] = ("VE", w.c["VE"])
                                exp_v[1][i3] = ("VE", w.c["VE"])
                            else:
                                exp_v[0][i3] = ("SC", sc_exp)
                                exp_v[1][i3] = ("SC", sc_exp)
                        if kt > 1:
                            pkt = kt - 2
                            pj = pkt - 4 * qb
                            pqs = 128 * pj if pj > 0 else 0
                            for e in range(2):
                                hg = 2 * pr + e
                                esem, ecnt = exp_v[e][pkt % 3]
                                w.wait("tensor", esem, ecnt)
                                if pkt == 0:
                                    w.wait("tensor", "VE", py_rd[e])
                                w.t(lambda t, e=e, pkt=pkt, hg=hg, kts=kts,
                                    pqs=pqs: t.matmul(
                                        p_y[e][:, pqs:512],
                                        Vb[:, pkt, hg, 0:65],
                                        ptb3[e][pkt % 3][:, pqs:512],
                                        start=(pkt == 0),
                                        stop=(pkt == kts - 1)),
                                    [("MM", 1)])
                                ptb_rd[e][pkt % 3] = w.c["MM"]
                    mm_pv = w.c["MM"]
                    w.wait("vector", "MM", mm_pv)
                    w.wait("vector", "VE", yev_rd)
                    w.drain("vector")
                    for e in range(2):
                        def _evac(v, e=e):
                            with nc.allow_low_precision(
                                    reason="bf16 attention-out operand"):
                                return v.tensor_copy(yev[:, e, :],
                                                     p_y[e][:, :])
                        w.v(_evac, [("VE", 1)])
                        py_rd[e] = w.c["VE"]
                    w.drain("vector")
                    for e in range(2):
                        def _recip(v, e=e):
                            with nc.allow_low_precision(
                                    reason="bf16 denom bcast operand"):
                                return v.reciprocal(
                                    rdf[64:65, e * 512:(e + 1) * 512],
                                    yev[64:65, e, :])
                        w.v(_recip, [("VE", 1)])
                    ve_rd2 = w.c["VE"]

                    def tail(w, g=g, pr=pr, qsl=qsl, ve_rd2=ve_rd2):
                        nonlocal rb_rd, yev_rd
                        ysl = 0
                        for e in range(2):
                            w.wait("tensor", "VE", max(ve_rd2, rb_rd))
                            w.wait("tensor", "CONSTG", gv_consts)
                            w.t(lambda t, e=e: t.matmul(
                                p_rb[:, :], c_o64_b[64:65, :],
                                rdf[64:65, e * 512:(e + 1) * 512],
                                start=True, stop=True), [("MM", 1)])
                            mm_rb = w.c["MM"]
                            w.wait("vector", "MM", mm_rb)
                            w.drain("vector")
                            w.v(lambda v: v.tensor_copy(rb_sb, p_rb[:, :]),
                                [("VE", 1)])
                            rb_rd = w.c["VE"]
                            w.drain("vector")
                            if e == 0:
                                def _m0(v, g=g, pr=pr, qsl=qsl):
                                    with nc.allow_low_precision(
                                            reason="bf16 yT operand"):
                                        return v.tensor_tensor(
                                            yTn[g][0:64, pr, qsl],
                                            yev[0:64, 0, :], rb_sb,
                                            ALU.mult)
                                w.v(_m0, [("VE", 1)])
                                yev_rd = w.c["VE"]
                            else:
                                yss = "YST0" if ysl == 0 else "YST1"
                                w.wait("vector", yss, yst_st[ysl])

                                def _m1(v, ysl=ysl):
                                    with nc.allow_low_precision(
                                            reason="bf16 yT operand"):
                                        return v.tensor_tensor(
                                            yst[ysl][:, :],
                                            yev[0:64, 1, :], rb_sb,
                                            ALU.mult)
                                w.v(_m1, [("VE", 1)])
                                yev_rd = w.c["VE"]
                                ve_yst = w.c["VE"]
                                w.wait("sync", "VE", ve_yst)
                                w.y(lambda y, g=g, pr=pr, qsl=qsl, ysl=ysl,
                                    yss=yss: y.dma_start(
                                        out=yTn[g][64:128, pr, qsl],
                                        in_=yst[ysl][:, :]), [(yss, 16)])
                                yst_st[ysl] = w.c[yss]
                    pending_tail[0] = tail
                flush_tail(w)
            flush_tail(w)
            vb_rd_prev = w.c["MM"]
            qt_rd_prev = w.c["MM"]
            xtb_rd2[0] = w.c["MM"]
            xtb_rd2[1] = w.c["MM"]

        if tap_t:
            w.wait("sync", "VE", w.c["VE"])
            w.wait("sync", "MM", w.c["MM"])
            w.wait("sync", "CONSTG", gv_consts)
            w.wait("sync", "YST0", w.c["YST0"])
            w.wait("sync", "YST1", w.c["YST1"])
        if "tap_qt" in tap_t:
            w.y(lambda y: y.dma_start(out=tap_t["tap_qt"], in_=QT[:, :, :]),
                [("TAP", 16)])
        if "tap_kt" in tap_t:
            w.y(lambda y: y.dma_start(out=tap_t["tap_kt"], in_=KT[:, :, :]),
                [("TAP", 16)])
        if "tap_v" in tap_t:
            w.y(lambda y: y.dma_start(out=tap_t["tap_v"], in_=Vb[:, :, :, :]),
                [("TAP", 16)])
        if "tap_negm" in tap_t:
            w.y(lambda y: y.dma_start(out=tap_t["tap_negm"], in_=negm_b[:, :]),
                [("TAP", 16)])
        if "tap_ytn" in tap_t:
            w.y(lambda y: y.dma_start(out=tap_t["tap_ytn"][0],
                                      in_=yTn[0][:, :, :]), [("TAP", 16)])
            w.y(lambda y: y.dma_start(out=tap_t["tap_ytn"][1],
                                      in_=yTn[1][:, :, :]), [("TAP", 16)])
        if w.name == "sync":
            w.eng.wait_ge(w.sems["YST0"], w.c["YST0"])
            w.eng.wait_ge(w.sems["YST1"], w.c["YST1"])
            w.eng.wait_ge(w.sems["TAP"], w.c["TAP"])

    def run_block(blk, emit):
        for en in ENGS:
            def cb(e, en=en):
                walks[en].eng = e
                emit(walks[en])
            getattr(blk, en)(cb)

    with nc.Block("blkA") as blkA:
        run_block(blkA, emit_A)

    st_a.close()

    st_x1t = ExitStack()
    x1T = st_x1t.enter_context(
        nc.sbuf_tensor("x1T", [128, 8, S], BF16, side="right"))
    st_a3 = ExitStack()
    E3 = st_a3.enter_context
    woT_b = E3(nc.sbuf_tensor("woT_b", [128, 8, 1024], BF16))
    xres = [E3(nc.sbuf_tensor(f"xres{i}", [128, D], F32)) for i in range(3)]
    x1pre = [E3(nc.sbuf_tensor(f"x1pre{i}", [128, D], F32)) for i in range(2)]
    x1f = [E3(nc.sbuf_tensor(f"x1f{i}", [128, D], F32)) for i in range(2)]
    x1bb = [E3(nc.sbuf_tensor(f"x1bb{i}", [128, D], BF16)) for i in range(3)]
    junk_b = E3(nc.sbuf_tensor("junk_b", [128, D], BF16))
    stt = E3(nc.sbuf_tensor("stt", [128, 2, 16], F32))
    p_z = [E3(nc.psum_tensor(f"p_z{i}", [128, 512], F32)) for i in range(2)]
    p_t = [E3(nc.psum_tensor(f"p_t{i}", [128, 128], BF16)) for i in range(4)]

    def emit_A3(w: Walk):
        w.g(lambda g: g.dma_start(
            out=woT_b[:, :, :],
            in_=d_woT.rearrange("(cc p) n -> p cc n", p=128)), [("WO", 16)])
        gv_wo = w.c["WO"]
        xres_rd = [0, 0, 0]
        x1f_rd = [0, 0]
        x1b_rd = [0, 0]
        x1pre_rd = [0, 0]
        stt_rd = 0
        pz_rd = [0, 0]
        pt_rd = [0, 0, 0, 0]
        pend_tp = []
        pend_ln = []
        x1b_rd3 = [0, 0, 0]

        def ln_stage2(w, item):
            nonlocal x1f_rd, pend_tp
            pt, bi, ve_var = item
            psl = slice(pt * 128, (pt + 1) * 128)
            w.wait("scalar", "VE", ve_var)
            w.s(lambda s, bi=bi: s.activation(stt[:, bi, 15:16],
                                              stt[:, bi, 14:15], AF.Ln),
                [("SC", 1)])
            w.drain("scalar")
            w.s(lambda s, bi=bi: s.activation(stt[:, bi, 14:15],
                                              stt[:, bi, 15:16], AF.Exp,
                                              scale=-0.5), [("SC", 1)])
            sc_rstd = w.c["SC"]
            w.wait("vector", "SC", sc_rstd)
            w.wait("vector", "X1F0" if bi == 0 else "X1F1", x1f_rd[bi])
            w.drain("vector")
            w.v(lambda v, bi=bi: v.tensor_scalar(x1f[bi][:, :],
                                                 x1pre[bi][:, :],
                                                 stt[:, bi, 12:13],
                                                 stt[:, bi, 14:15],
                                                 ALU.subtract, ALU.mult),
                [("VE", 1)])
            ve_x1 = w.c["VE"]
            ti3 = pt % 3
            w.drain("vector")
            w.wait("vector", "MM", x1b_rd3[ti3])
            w.v(lambda v, ti3=ti3, bi=bi: v.tensor_copy(x1bb[ti3][:, :],
                                                        x1f[bi][:, :]),
                [("VE", 1)])
            ve_x1b = w.c["VE"]
            w.wait("sync", "VE", ve_x1)
            x1s = "X1F0" if bi == 0 else "X1F1"
            w.y(lambda y, bi=bi, psl=psl, x1s=x1s: y.dma_start(
                out=d_x1h[psl, :], in_=x1f[bi][:, :]), [(x1s, 16)])
            x1f_rd[bi] = w.c[x1s]
            pend_tp.append((pt, ve_x1b))

        for pt in range(NPT):
            bi = pt % 2
            ri = pt % 3
            psl = slice(pt * 128, (pt + 1) * 128)
            w.wait("sync", "VE", xres_rd[ri])
            xrs = ("XRES0", "XRES1", "XRES2")[ri]
            w.y(lambda y, psl=psl, ri=ri: y.dma_start(out=xres[ri][:, :],
                                                      in_=d_x[psl, :]),
                [(xrs, 16)])
            sv_x = w.c[xrs]
            w.wait("tensor", "WO", gv_wo)
            for half in range(2):
                w.wait("tensor", "VE", pz_rd[half])
                for cc in range(8):
                    g2, p2 = divmod(cc, 4)
                    w.t(lambda t, half=half, g2=g2, p2=p2, cc=cc, psl=psl:
                        t.matmul(p_z[half][:, :],
                                 yTn[g2][:, p2, psl],
                                 woT_b[:, cc, half * 512:(half + 1) * 512],
                                 start=(cc == 0), stop=(cc == 7)),
                        [("MM", 1)] if cc == 7 else ())
            mm_z = w.c["MM"]
            w.wait("vector", "MM", mm_z)
            w.wait("vector", xrs, sv_x)
            for half in range(2):
                w.v(lambda v, half=half, bi=bi, ri=ri: v.tensor_tensor(
                    x1pre[bi][:, half * 512:(half + 1) * 512],
                    p_z[half][:, :],
                    xres[ri][:, half * 512:(half + 1) * 512], ALU.add),
                    [("VE", 1)])
                pz_rd[half] = w.c["VE"]
            xres_rd[ri] = w.c["VE"]
            ve_pre = w.c["VE"]
            w.drain("vector")
            w.v(lambda v, bi=bi: v.bn_stats(stt[:, bi, 0:6],
                                            x1pre[bi][:, 0:512]),
                [("VE", 1)])
            w.drain("vector")
            w.v(lambda v, bi=bi: v.bn_stats(stt[:, bi, 6:12],
                                            x1pre[bi][:, 512:1024]),
                [("VE", 1)])
            w.drain("vector")
            w.v(lambda v, bi=bi: v.bn_aggr(stt[:, bi, 12:14],
                                           stt[:, bi, 0:12]), [("VE", 1)])
            w.drain("vector")
            w.v(lambda v, bi=bi: v.tensor_scalar(stt[:, bi, 14:15],
                                                 stt[:, bi, 13:14],
                                                 1.0, EPS, ALU.mult, ALU.add),
                [("VE", 1)])
            ve_var = w.c["VE"]
            pend_ln.append((pt, bi, ve_var))
            if len(pend_ln) > 1:
                ln_stage2(w, pend_ln.pop(0))
            if len(pend_tp) > 2:
                tpt, tve = pend_tp.pop(0)
                tb3 = tpt % 3
                for cc in range(8):
                    ti = cc % 4
                    w.wait("tensor", "VE", max(tve, pt_rd[ti]))
                    w.t(lambda t, ti=ti, tb3=tb3, cc=cc: t.transpose(
                        p_t[ti][:, :], x1bb[tb3][:, cc * 128:(cc + 1) * 128],
                        c_id_b[:, :]), [("MM", 1)])
                    mm_t = w.c["MM"]
                    if cc == 7:
                        x1b_rd3[tb3] = mm_t
                    w.wait("vector", "MM", mm_t)
                    w.v(lambda v, cc=cc, tpt=tpt, ti=ti: v.tensor_copy(
                        x1T[:, cc, tpt * 128:(tpt + 1) * 128], p_t[ti][:, :]),
                        [("VE", 1)])
                    pt_rd[ti] = w.c["VE"]
        while pend_ln:
            ln_stage2(w, pend_ln.pop(0))
        while pend_tp:
            tpt, tve = pend_tp.pop(0)
            tb3 = tpt % 3
            for cc in range(8):
                ti = cc % 2
                w.wait("tensor", "VE", max(tve, pt_rd[ti]))
                w.t(lambda t, ti=ti, tb3=tb3, cc=cc: t.transpose(
                    p_t[ti][:, :], x1bb[tb3][:, cc * 128:(cc + 1) * 128],
                    c_id_b[:, :]), [("MM", 1)])
                mm_t = w.c["MM"]
                if cc == 7:
                    x1b_rd3[tb3] = mm_t
                w.wait("vector", "MM", mm_t)
                w.v(lambda v, cc=cc, tpt=tpt, ti=ti: v.tensor_copy(
                    x1T[:, cc, tpt * 128:(tpt + 1) * 128], p_t[ti][:, :]),
                    [("VE", 1)])
                pt_rd[ti] = w.c["VE"]
        if "tap_x1" in tap_t:
            w.wait("sync", "X1F0", x1f_rd[0])
            w.wait("sync", "X1F1", x1f_rd[1])
            w.y(lambda y: y.dma_start(out=tap_t["tap_x1"], in_=d_x1h),
                [("TAP", 16)])
        if "tap_x1t" in tap_t:
            w.wait("sync", "VE", w.c["VE"])
            w.y(lambda y: y.dma_start(out=tap_t["tap_x1t"], in_=x1T[:, :, :]),
                [("TAP", 16)])
        if w.name == "sync":
            w.eng.wait_ge(w.sems["X1F0"], w.c["X1F0"])
            w.eng.wait_ge(w.sems["X1F1"], w.c["X1F1"])
            w.eng.wait_ge(w.sems["TAP"], w.c["TAP"])

    with nc.Block("blkA3") as blkA3:
        run_block(blkA3, emit_A3)

    st_a3.close()
    st_ytn.close()

    st_b = ExitStack()
    EB = st_b.enter_context
    y_acc = EB(nc.sbuf_tensor("y_acc", [128, NQT, D], F32))
    w13_b = [EB(nc.sbuf_tensor(f"w13_b{i}", [128, 8, 1024], BF16))
             for i in range(2)]
    w2_b = [EB(nc.sbuf_tensor(f"w2_b{i}", [128, 4, 1024], BF16))
            for i in range(2)]
    hT = [EB(nc.sbuf_tensor(f"hT{i}", [128, 4, BCH], BF16)) for i in range(2)]
    sg_b = EB(nc.sbuf_tensor("sg_b", [128, 512], BF16))
    gw_b = EB(nc.sbuf_tensor("gw_b", [128, 8, NE], BF16))
    gb_sb = EB(nc.sbuf_tensor("gb_sb", [NE, 1], F32))
    gbT_sb = EB(nc.sbuf_tensor("gbT_sb", [1, NE], F32))
    biT = EB(nc.sbuf_tensor("biT", [NE, S], F32))
    bi_pm = EB(nc.sbuf_tensor("bi_pm", [128, NPT, NE], F32))
    gbb = EB(nc.sbuf_tensor("gbb", [128, NE], F32))
    sp_pm = EB(nc.sbuf_tensor("sp_pm", [128, NE], F32))
    top8 = EB(nc.sbuf_tensor("top8", [128, 8], F32))
    msk_b = EB(nc.sbuf_tensor("msk_b", [128, NE], F32))
    wsel = EB(nc.sbuf_tensor("wsel", [128, NE], F32))
    dnr = EB(nc.sbuf_tensor("dnr", [128, 1], F32))
    comb = EB(nc.sbuf_tensor("comb", [128, NPT, NE], F32))
    x2res = [EB(nc.sbuf_tensor(f"x2res{i}", [128, D], F32)) for i in range(2)]
    x2pre = [EB(nc.sbuf_tensor(f"x2pre{i}", [128, D], F32)) for i in range(2)]
    outf = [EB(nc.sbuf_tensor(f"outf{i}", [128, D], F32)) for i in range(2)]
    junk2 = EB(nc.sbuf_tensor("junk2", [128, D], BF16))
    st2 = EB(nc.sbuf_tensor("st2", [128, 2, 16], F32))
    p_h = [EB(nc.psum_tensor(f"p_h{i}", [128, 512], F32)) for i in range(4)]
    p_zb = [EB(nc.psum_tensor(f"p_zb{i}", [128, 512], F32)) for i in range(2)]
    p_r16 = EB(nc.psum_tensor("p_r16", [128, 512], F32))
    p_rt2 = EB(nc.psum_tensor("p_rt2", [128, 16], F32))
    p_tb = p_r16[:, 32:48]
    p_tb2 = [p_r16[:, 64:80], p_rt2[:, :]]

    NEXP = NE + 1
    NGI = 4 * NEXP

    def exp_dma(w, e, bslot):
        ws = "W13A" if bslot == 0 else "W13B"
        if e < NE:
            w.g(lambda g, e=e, bslot=bslot: g.dma_start(
                out=w13_b[bslot][:, :, :],
                in_=d_w13T.rearrange("e (cc p) n -> e p cc n", p=128)[e]),
                [(ws, 16)])
            w.g(lambda g, e=e, bslot=bslot: g.dma_start(
                out=w2_b[bslot][:, :, :],
                in_=d_w2T.rearrange("e (cc p) n -> e p cc n", p=128)[e]),
                [(ws, 16)])
        else:
            w.g(lambda g, bslot=bslot: g.dma_start(
                out=w13_b[bslot][:, :, :],
                in_=d_w13sT.rearrange("(cc p) n -> p cc n", p=128)),
                [(ws, 16)])
            w.g(lambda g, bslot=bslot: g.dma_start(
                out=w2_b[bslot][:, :, :],
                in_=d_w2sT.rearrange("(cc p) n -> p cc n", p=128)),
                [(ws, 16)])
        return w.c[ws]

    def emit_B(w: Walk):
        w.g(lambda g: g.dma_start(
            out=gw_b[:, :, :],
            in_=d_gwT.rearrange("(cc p) n -> p cc n", p=128)), [("GW", 16)])
        gv_gw = w.c["GW"]
        w.y(lambda y: y.dma_start(out=gb_sb[:, :], in_=d_gb), [("GBS", 16)])
        w.y(lambda y: y.dma_start(out=gbT_sb[:, :], in_=d_gbT), [("GBS", 16)])
        sv_gb = w.c["GBS"]
        r16_rd = 0
        for ch in range(NCH):
            chs = slice(ch * 512, (ch + 1) * 512)
            w.wait("tensor", "GW", gv_gw)
            w.wait("tensor", "SC", r16_rd)
            for cc in range(8):
                w.t(lambda t, cc=cc, chs=chs: t.matmul(
                    p_r16[0:NE, :], gw_b[:, cc, :], x1T[:, cc, chs],
                    start=(cc == 0), stop=(cc == 7)),
                    [("MM", 1)] if cc == 7 else ())
            mm_r = w.c["MM"]
            w.wait("scalar", "MM", mm_r)
            w.s(lambda s, chs=chs: s.activation(biT[:, chs], p_r16[0:NE, :],
                                                AF.Sigmoid), [("SC", 1)])
            r16_rd = w.c["SC"]
        sc_router = w.c["SC"]
        w.wait("tensor", "GBS", sv_gb)
        w.wait("tensor", "SC", sc_router)
        w.t(lambda t: t.matmul(p_tb, c_ones_f[:, :], gbT_sb[:, :],
                               start=True, stop=True), [("MM", 1)])
        mm_gbb = w.c["MM"]
        w.wait("vector", "MM", mm_gbb)
        w.v(lambda v: v.tensor_copy(gbb[:, :], p_tb), [("VE", 1)])
        ptb_rd = w.c["VE"]
        ptb_rd2 = [ptb_rd, ptb_rd]
        w.wait("vector", "SC", sc_router)
        w.wait("vector", "GBS", sv_gb)
        w.v(lambda v: v.tensor_scalar(biT[:, :], biT[:, :], gb_sb[:, :], None,
                                      ALU.add), [("VE", 1)])
        ve_bi = w.c["VE"]

        def route_tiles(w, pts):
            nonlocal ptb_rd
            for pt in pts:
                tb = pt % 2
                w.wait("tensor", "VE", max(ve_bi, ptb_rd2[tb]))
                w.t(lambda t, pt=pt, tb=tb: t.transpose(
                    p_tb2[tb], biT[:, pt * 128:(pt + 1) * 128],
                    c_id_f[0:16, 0:16]), [("MM", 1)])
                mm_t = w.c["MM"]
                w.wait("vector", "MM", mm_t)
                w.drain("vector")
                w.v(lambda v, pt=pt, tb=tb: v.tensor_copy(bi_pm[:, pt, :],
                                                          p_tb2[tb]),
                    [("VE", 1)])
                ptb_rd2[tb] = w.c["VE"]
                ptb_rd = w.c["VE"]
                w.drain("vector")
                w.v(lambda v, pt=pt: v.tensor_tensor(
                    sp_pm[:, :], bi_pm[:, pt, :], gbb[:, :], ALU.subtract),
                    [("VE", 1)])
                w.drain("vector")
                w.v(lambda v, pt=pt: v.max(out=top8[:, :],
                                           in_=bi_pm[:, pt, :]), [("VE", 1)])
                w.drain("vector")
                w.v(lambda v, pt=pt: v.tensor_scalar(
                    msk_b[:, :], bi_pm[:, pt, :], top8[:, 3:4], None,
                    ALU.is_ge), [("VE", 1)])
                w.drain("vector")
                w.v(lambda v: v.tensor_tensor(wsel[:, :], sp_pm[:, :],
                                              msk_b[:, :], ALU.mult),
                    [("VE", 1)])
                w.drain("vector")
                w.v(lambda v: v.tensor_reduce(dnr[:, :], wsel[:, :], AX.X,
                                              ALU.add), [("VE", 1)])
                w.drain("vector")
                w.v(lambda v: v.reciprocal(dnr[:, :], dnr[:, :]),
                    [("VE", 1)])
                w.drain("vector")
                w.v(lambda v, pt=pt: v.tensor_scalar(
                    comb[:, pt, :], wsel[:, :], dnr[:, :], ROUTE_SCALE,
                    ALU.mult, ALU.mult), [("VE", 1)])

        route_tiles(w, range(0, NQT))

        wload = [0, 0]
        w13_rd = [0, 0]
        hT_rd = [0, 0]
        ph_rd = [("SC", 0), ("VE", 0), ("SC", 0), ("VE", 0)]
        pz_rd = [0, 0]
        sg_rd = 0
        x2res_rd = [0, 0]
        outf_rd = [0, 0]
        pend_l2 = []

        def ln2_stage2(w, item):
            nonlocal outf_rd
            pt_g, bi, ve_var = item
            psl = slice(pt_g * 128, (pt_g + 1) * 128)
            w.wait("scalar", "VE", ve_var)
            w.s(lambda s, bi=bi: s.activation(st2[:, bi, 15:16],
                                              st2[:, bi, 14:15],
                                              AF.Ln), [("SC", 1)])
            w.drain("scalar")
            w.s(lambda s, bi=bi: s.activation(st2[:, bi, 14:15],
                                              st2[:, bi, 15:16],
                                              AF.Exp, scale=-0.5),
                [("SC", 1)])
            sc_rstd = w.c["SC"]
            w.wait("vector", "SC", sc_rstd)
            w.wait("vector", "OUT0" if bi == 0 else "OUT1", outf_rd[bi])
            w.drain("vector")
            w.v(lambda v, bi=bi: v.tensor_scalar(outf[bi][:, :],
                                                 x2pre[bi][:, :],
                                                 st2[:, bi, 12:13],
                                                 st2[:, bi, 14:15],
                                                 ALU.subtract, ALU.mult),
                [("VE", 1)])
            ve_out = w.c["VE"]
            w.wait("sync", "VE", ve_out)
            ots = "OUT0" if bi == 0 else "OUT1"
            w.y(lambda y, bi=bi, psl=psl, ots=ots: y.dma_start(
                out=d_out[psl, :], in_=outf[bi][:, :]), [(ots, 16)])
            outf_rd[bi] = w.c[ots]
        tap_yg = [0]
        wload[0] = exp_dma(w, 0, 0)
        for gi in range(NGI):
            qt_i, e = divmod(gi, NEXP)
            bslot = gi % 2
            if e == NEXP // 2 and qt_i + 1 < 4:
                route_tiles(w, range((qt_i + 1) * NQT, (qt_i + 2) * NQT))
            if gi + 1 < NGI:
                nslot = (gi + 1) % 2
                w.wait("gpsimd", "MM", w13_rd[nslot])
                wload[nslot] = exp_dma(w, (gi + 1) % NEXP, nslot)
            def emit_z(w, ch, ve_h, bslot=bslot, qt_i=qt_i, e=e):
                nonlocal pz_rd, hT_rd, w13_rd
                w.drain("vector")
                for pst in range(NST):
                    t_i = ch * NST + pst
                    pt_g = qt_i * NQT + t_i
                    for co in range(2):
                        zb = co
                        w.wait("tensor", "VE", max(ve_h, pz_rd[zb]))
                        for iop in range(4):
                            w.t(lambda t, zb=zb, iop=iop, ch=ch, pst=pst,
                                bslot=bslot, co=co: t.matmul(
                                    p_zb[zb][:, :],
                                    hT[ch % 2][:, iop,
                                               pst * 128:(pst + 1) * 128],
                                    w2_b[bslot][:, iop,
                                                co * 512:(co + 1) * 512],
                                    start=(iop == 0), stop=(iop == 3)),
                                [("MM", 1)] if iop == 3 else ())
                        mm_zb = w.c["MM"]
                        hT_rd[ch % 2] = mm_zb
                        w13_rd[bslot] = mm_zb
                        w.wait("vector", "MM", mm_zb)
                        w.wait("vector", "TAP", tap_yg[0])
                        csl = slice(co * 512, (co + 1) * 512)
                        if e == 0:
                            w.v(lambda v, t_i=t_i, csl=csl, zb=zb, pt_g=pt_g:
                                v.tensor_scalar(y_acc[:, t_i, csl],
                                                p_zb[zb][:, :],
                                                comb[:, pt_g, 0:1], None,
                                                ALU.mult), [("VE", 1)])
                        elif e < NE:
                            w.v(lambda v, t_i=t_i, csl=csl, zb=zb, pt_g=pt_g,
                                e=e: v.scalar_tensor_tensor(
                                    y_acc[:, t_i, csl], p_zb[zb][:, :],
                                    comb[:, pt_g, e:e + 1],
                                    y_acc[:, t_i, csl],
                                    ALU.mult, ALU.add), [("VE", 1)])
                        else:
                            w.v(lambda v, t_i=t_i, csl=csl, zb=zb:
                                v.scalar_tensor_tensor(
                                    y_acc[:, t_i, csl], p_zb[zb][:, :], 1.0,
                                    y_acc[:, t_i, csl],
                                    ALU.mult, ALU.add), [("VE", 1)])
                        pz_rd[zb] = w.c["VE"]

            pend_z = None
            for ch in range(NBCH):
                gpos = qt_i * QTR + ch * BCH
                gsl = slice(gpos, gpos + BCH)
                w.wait("tensor", "W13A" if bslot == 0 else "W13B",
                       wload[bslot])
                if ch > 0:
                    w.wait("tensor", "MM", hT_rd[ch % 2])
                for iop in range(4):
                    pa = 2 * (iop % 2)
                    pb_ = pa + 1
                    for pslot, iotile in ((pa, iop), (pb_, 4 + iop)):
                        sem, cnt = ph_rd[pslot]
                        w.wait("tensor", sem, cnt)
                        for cc in range(8):
                            w.t(lambda t, pslot=pslot, cc=cc, iotile=iotile,
                                bslot=bslot, gsl=gsl: t.matmul(
                                    p_h[pslot][:, 0:BCH],
                                    w13_b[bslot][:, cc,
                                                 iotile * 128:
                                                 (iotile + 1) * 128],
                                    x1T[:, cc, gsl],
                                    start=(cc == 0), stop=(cc == 7)),
                                [("MM", 1)] if cc == 7 else ())
                    mm_gu = w.c["MM"]
                    w.wait("scalar", "MM", mm_gu)
                    w.wait("scalar", "VE", sg_rd)
                    w.s(lambda s, pa=pa: s.activation(sg_b[:, 0:BCH],
                                                      p_h[pa][:, 0:BCH],
                                                      AF.Silu), [("SC", 1)])
                    sc_silu = w.c["SC"]
                    ph_rd[pa] = ("SC", sc_silu)
                    w.wait("vector", "SC", sc_silu)
                    w.wait("vector", "MM", hT_rd[ch % 2])
                    w.v(lambda v, iop=iop, ch=ch, pb_=pb_: v.tensor_tensor(
                        hT[ch % 2][:, iop, :], sg_b[:, 0:BCH],
                        p_h[pb_][:, 0:BCH], ALU.mult), [("VE", 1)])
                    sg_rd = w.c["VE"]
                    ph_rd[pb_] = ("VE", w.c["VE"])
                ve_h = w.c["VE"]
                if pend_z is not None:
                    emit_z(w, pend_z[0], pend_z[1])
                pend_z = (ch, ve_h)
            emit_z(w, pend_z[0], pend_z[1])
            if e != NE:
                continue
            if "tap_comb" in tap_t and qt_i == 3 and e == NE:
                w.wait("sync", "VE", w.c["VE"])
                w.y(lambda y: y.dma_start(out=tap_t["tap_comb"],
                                          in_=comb[:, :, :]), [("TAP", 16)])
            if "tap_yacc" in tap_t and qt_i == 0:
                w.wait("sync", "VE", w.c["VE"])
                w.y(lambda y: y.dma_start(out=tap_t["tap_yacc"],
                                          in_=y_acc[:, :, :]), [("TAP", 16)])
                tap_yg[0] = w.c["TAP"]
            w.drain("vector")
            for t_i in range(NQT):
                bi = t_i % 2
                pt_g = qt_i * NQT + t_i
                psl = slice(pt_g * 128, (pt_g + 1) * 128)
                w.wait("sync", "VE", x2res_rd[bi])
                x2s = "X2RES0" if bi == 0 else "X2RES1"
                w.y(lambda y, psl=psl, bi=bi, x2s=x2s: y.dma_start(
                    out=x2res[bi][:, :], in_=d_x1h[psl, :]), [(x2s, 16)])
                sv_r = w.c[x2s]
                w.wait("vector", x2s, sv_r)
                w.v(lambda v, t_i=t_i, bi=bi: v.tensor_tensor(
                    x2pre[bi][:, :], y_acc[:, t_i, :], x2res[bi][:, :],
                    ALU.add), [("VE", 1)])
                x2res_rd[bi] = w.c["VE"]
                w.drain("vector")
                w.v(lambda v, bi=bi: v.bn_stats(st2[:, bi, 0:6],
                                                x2pre[bi][:, 0:512]),
                    [("VE", 1)])
                w.drain("vector")
                w.v(lambda v, bi=bi: v.bn_stats(st2[:, bi, 6:12],
                                                x2pre[bi][:, 512:1024]),
                    [("VE", 1)])
                w.drain("vector")
                w.v(lambda v, bi=bi: v.bn_aggr(st2[:, bi, 12:14],
                                               st2[:, bi, 0:12]),
                    [("VE", 1)])
                w.drain("vector")
                w.v(lambda v, bi=bi: v.tensor_scalar(st2[:, bi, 14:15],
                                                     st2[:, bi, 13:14],
                                                     1.0, EPS, ALU.mult,
                                                     ALU.add), [("VE", 1)])
                ve_var = w.c["VE"]
                pend_l2.append((pt_g, bi, ve_var))
                if len(pend_l2) > 1:
                    ln2_stage2(w, pend_l2.pop(0))
            while pend_l2:
                ln2_stage2(w, pend_l2.pop(0))
        if w.name == "sync":
            w.eng.wait_ge(w.sems["OUT0"], w.c["OUT0"])
            w.eng.wait_ge(w.sems["OUT1"], w.c["OUT1"])
            w.eng.wait_ge(w.sems["TAP"], w.c["TAP"])

    with nc.Block("blkB") as blkB:
        run_block(blkB, emit_B)

    st_b.close()
    st_x1t.close()
    st_persist.close()

    base = walks["tensor"].c
    for n, wk in walks.items():
        assert wk.c == base, (n, wk.c, base)
    return nc


def _host_consts():
    ident = np.eye(128, dtype=np.float32)
    k = np.arange(128)[:, None]
    q = np.arange(128)[None, :]
    cmask = np.ascontiguousarray((q >= k).astype(np.float32))
    ones = np.ones((1, 128), dtype=np.float32)
    o2w = np.zeros((128, 128), dtype=np.float32)
    o2w[:, 0] = 1.0
    o64 = np.ones((128, 64), dtype=np.float32)
    return ident, cmask, ones, o2w, o64


def make_shared_inputs(L0_attn_W_Q, L0_attn_W_K, L0_attn_W_V, L0_attn_W_O,
                       L0_ffn_gate_w, L0_ffn_gate_b, L0_ffn_w13, L0_ffn_w2,
                       L0_ffn_w13_s, L0_ffn_w2_s):
    ident, cmask, ones, o2w, o64 = _host_consts()
    f32 = np.float32
    A = lambda a: np.ascontiguousarray(np.asarray(a, f32))
    return {
        "wqT": A(np.asarray(L0_attn_W_Q, f32).T),
        "wkT": A(np.asarray(L0_attn_W_K, f32).T),
        "wvT": A(np.asarray(L0_attn_W_V, f32).T),
        "woT": A(np.asarray(L0_attn_W_O, f32).T),
        "gwT": A(np.asarray(L0_ffn_gate_w, f32).T),
        "gb": A(np.asarray(L0_ffn_gate_b, f32)[:, None]),
        "gbT": A(np.asarray(L0_ffn_gate_b, f32)[None, :]),
        "w13T": A(np.asarray(L0_ffn_w13, f32).transpose(0, 2, 1)),
        "w2T": A(np.asarray(L0_ffn_w2, f32).transpose(0, 2, 1)),
        "w13sT": A(np.asarray(L0_ffn_w13_s, f32).T),
        "w2sT": A(np.asarray(L0_ffn_w2_s, f32).T),
        "ident": ident, "ltri": cmask, "ones": ones, "ones2w": o2w,
        "ones64": o64,
    }


def kernel(L0_attn_x, L0_attn_W_Q, L0_attn_W_K, L0_attn_W_V, L0_attn_W_O,
           L0_ffn_input_ids, L0_ffn_gate_w, L0_ffn_gate_b, L0_ffn_w13,
           L0_ffn_w2, L0_ffn_w13_s, L0_ffn_w2_s):
    shared = make_shared_inputs(L0_attn_W_Q, L0_attn_W_K, L0_attn_W_V,
                                L0_attn_W_O, L0_ffn_gate_w, L0_ffn_gate_b,
                                L0_ffn_w13, L0_ffn_w2, L0_ffn_w13_s,
                                L0_ffn_w2_s)
    xs = np.asarray(L0_attn_x, np.float32)
    in_maps = []
    for c in range(8):
        x = np.ascontiguousarray(xs[c])
        xT = np.ascontiguousarray(x.T)
        m = {"x": x, "xT": xT}
        m.update(shared)
        in_maps.append(m)
    nc = build_program(S=4096)
    res = run_bass_kernel_spmd(nc, in_maps, core_ids=list(range(8)))
    out = np.stack([np.asarray(r["out"], dtype=np.float32)
                    for r in res.results], axis=0)
    return out


# version 19
# speedup vs baseline: 1.1757x; 1.0049x over previous
import numpy as np
from contextlib import ExitStack

import concourse.bass as bass
import concourse.mybir as mybir
from concourse.bass_utils import run_bass_kernel_spmd

F32 = mybir.dt.float32
BF16 = mybir.dt.bfloat16
AF = mybir.ActivationFunctionType
ALU = mybir.AluOpType
AX = mybir.AxisListType

D = 1024
H = 16
NE = 16
INTER = 512
ROUTE_SCALE = 2.5
EPS = 1e-5
NEG = -1.0e9

ENGS = ("tensor", "scalar", "vector", "gpsimd", "sync")


class Walk:

    def __init__(self, name, sem_names):
        self.name = name
        self.eng = None
        self.sems = None
        self.c = {k: 0 for k in sem_names}
        self._lastw = {}

    def _emit(self, owner, fn, incs):
        inst = fn(self.eng) if self.name == owner else None
        for sem, n in incs:
            self.c[sem] += n
            if inst is not None:
                inst.then_inc(self.sems[sem], n)

    def t(self, fn, incs=()):
        self._emit("tensor", fn, incs)

    def s(self, fn, incs=()):
        self._emit("scalar", fn, incs)

    def v(self, fn, incs=()):
        self._emit("vector", fn, incs)

    def g(self, fn, incs=()):
        self._emit("gpsimd", fn, incs)

    def y(self, fn, incs=()):
        self._emit("sync", fn, incs)

    def wait(self, eng_name, sem, val):
        if val <= 0 or self.name != eng_name:
            return
        if self._lastw.get(sem, 0) >= val:
            return
        self._lastw[sem] = val
        self.eng.wait_ge(self.sems[sem], val)

    def drain(self, eng_name):
        if self.name == eng_name:
            self.eng.drain()


def build_program(S=4096, taps=()):
    taps = set(taps)
    NQB = S // 512
    NKT = S // 128
    NCH = S // 512
    NPT = S // 128
    QTR = S // 4
    NQT = NPT // 4
    BCH = min(512, QTR)
    NBCH = QTR // BCH
    NST = BCH // 128
    SQW = 2048

    nc = bass.Bass("TRN2", target_bir_lowering=False)

    d_x = nc.dram_tensor("x", [S, D], F32, kind="ExternalInput").ap()
    d_xT = nc.dram_tensor("xT", [D, S], F32, kind="ExternalInput").ap()
    d_wqT = nc.dram_tensor("wqT", [D, D], F32, kind="ExternalInput").ap()
    d_wkT = nc.dram_tensor("wkT", [D, D], F32, kind="ExternalInput").ap()
    d_wvT = nc.dram_tensor("wvT", [D, D], F32, kind="ExternalInput").ap()
    d_woT = nc.dram_tensor("woT", [D, D], F32, kind="ExternalInput").ap()
    d_gwT = nc.dram_tensor("gwT", [D, NE], F32, kind="ExternalInput").ap()
    d_gb = nc.dram_tensor("gb", [NE, 1], F32, kind="ExternalInput").ap()
    d_gbT = nc.dram_tensor("gbT", [1, NE], F32, kind="ExternalInput").ap()
    d_w13T = nc.dram_tensor("w13T", [NE, D, D], F32, kind="ExternalInput").ap()
    d_w2T = nc.dram_tensor("w2T", [NE, INTER, D], F32, kind="ExternalInput").ap()
    d_w13sT = nc.dram_tensor("w13sT", [D, D], F32, kind="ExternalInput").ap()
    d_w2sT = nc.dram_tensor("w2sT", [INTER, D], F32, kind="ExternalInput").ap()
    d_ident = nc.dram_tensor("ident", [128, 128], F32, kind="ExternalInput").ap()
    d_ltri = nc.dram_tensor("ltri", [128, 128], F32, kind="ExternalInput").ap()
    d_ones = nc.dram_tensor("ones", [1, 128], F32, kind="ExternalInput").ap()
    d_ones2w = nc.dram_tensor("ones2w", [128, 128], F32, kind="ExternalInput").ap()
    d_ones64 = nc.dram_tensor("ones64", [128, 64], F32, kind="ExternalInput").ap()

    d_out = nc.dram_tensor("out", [S, D], F32, kind="ExternalOutput").ap()
    d_x1h = nc.dram_tensor("x1h", [S, D], F32, kind="Internal").ap()

    tap_t = {}
    for name, shape, dt in (
        ("tap_qt", [128, 4, S], BF16), ("tap_kt", [128, 4, S], BF16),
        ("tap_v", [128, NKT, 8, 65], BF16), ("tap_ytn", [2, 128, 4, S], BF16),
        ("tap_x1", [S, D], F32), ("tap_comb", [128, NPT, NE], F32),
        ("tap_x1t", [128, 8, S], BF16), ("tap_negm", [128, 2], F32),
        ("tap_yacc", [128, NQT, D], F32),
    ):
        if name in taps:
            tap_t[name] = nc.dram_tensor(name, shape, dt,
                                         kind="ExternalOutput").ap()

    sem_names = ("MM", "SC", "VE", "CONSTG", "CONSTS", "WQKV", "XTB",
                 "WO", "XRES0", "XRES1", "XRES2", "X1F0", "X1F1", "GW",
                 "GBS", "XTB2",
                 "W13A", "W13B", "X2RES0", "X2RES1", "OUT0", "OUT1",
                 "YST0", "YST1", "TAP")
    walks = {n: Walk(n, sem_names) for n in ENGS}

    st_persist = ExitStack()
    E = st_persist.enter_context
    semd = {n: E(nc.semaphore(n)) for n in sem_names}
    for wk in walks.values():
        wk.sems = semd

    c_id_b = E(nc.sbuf_tensor("c_id_b", [128, 128], BF16))
    c_id_f = E(nc.sbuf_tensor("c_id_f", [128, 128], F32))
    c_ones_f = E(nc.sbuf_tensor("c_ones_f", [1, 128], F32))
    c_o64_f = E(nc.sbuf_tensor("c_o64_f", [128, 64], F32))
    negm_b = E(nc.sbuf_tensor("negm_b", [128, 2], F32))

    st_ytn = ExitStack()
    yTn = [st_ytn.enter_context(nc.sbuf_tensor(f"yTn{g}", [128, 4, S], BF16))
           for g in range(2)]

    st_a = ExitStack()
    EA = st_a.enter_context
    c_ltri_b = EA(nc.sbuf_tensor("c_ltri_b", [128, 128], BF16))
    c_o64_b = EA(nc.sbuf_tensor("c_o64_b", [128, 64], BF16))
    c_o2w_b = EA(nc.sbuf_tensor("c_o2w_b", [128, 128], BF16))
    xTb = EA(nc.sbuf_tensor("xTb", [128, 8, 512], BF16))
    xTb2 = yTn[1][:, 0, :].rearrange("p (a b) -> p a b", a=8)
    xtbs = [xTb[:, :, :], xTb2]
    wq_b = EA(nc.sbuf_tensor("wq_b", [128, 8, 512], BF16))
    wk_b = EA(nc.sbuf_tensor("wk_b", [128, 8, 512], BF16))
    wv_b = EA(nc.sbuf_tensor("wv_b", [128, 8, 512], BF16))
    QT = EA(nc.sbuf_tensor("QT", [128, 4, S], BF16))
    KT = EA(nc.sbuf_tensor("KT", [128, 4, S], BF16))
    Vb = EA(nc.sbuf_tensor("Vb", [128, NKT, 8, 65], BF16))
    sq2 = EA(nc.sbuf_tensor("sq2", [128, SQW], BF16))
    ptb = [[sq2[:, (2 * i + e) * 512:(2 * i + e + 1) * 512]
            for i in range(2)] for e in range(2)]
    ptb3 = [[ptb[e][0], ptb[e][1], xTb[:, e, :]] for e in range(2)]
    mxq = EA(nc.sbuf_tensor("mxq", [128, 1], F32))
    mxk = EA(nc.sbuf_tensor("mxk", [128, 1], F32))
    mxt = EA(nc.sbuf_tensor("mxt", [128, 1], F32))
    sc1 = EA(nc.sbuf_tensor("sc1", [1, 4], F32))
    rdf = EA(nc.sbuf_tensor("rdf", [65, 1024], BF16))
    yev = EA(nc.sbuf_tensor("yev", [65, 2, 512], BF16))
    rb_t = EA(nc.sbuf_tensor("rb_t", [64, 512], F32))
    rb_sb = rb_t[:, :]
    yst = [EA(nc.sbuf_tensor("yst0", [64, 512], BF16))]
    p_sq = EA(nc.psum_tensor("p_sq", [128, 2048], F32))
    p_s = [p_sq[:, i * 512:(i + 1) * 512] for i in range(4)]
    p_y = [EA(nc.psum_tensor(f"p_y{e}", [65, 512], F32)) for e in range(2)]
    p_rb = EA(nc.psum_tensor("p_rb", [64, 512], F32))

    def emit_A(w: Walk):
        w.g(lambda g: g.dma_start(out=c_id_b[:, :], in_=d_ident),
            [("CONSTG", 16)])
        w.g(lambda g: g.dma_start(out=c_ltri_b[:, :], in_=d_ltri),
            [("CONSTG", 16)])
        w.g(lambda g: g.dma_start(out=c_o64_b[:, :], in_=d_ones64),
            [("CONSTG", 16)])
        w.g(lambda g: g.dma_start(out=c_o2w_b[:, :], in_=d_ones2w),
            [("CONSTG", 16)])
        gv_consts = w.c["CONSTG"]
        w.y(lambda y: y.dma_start(out=c_id_f[:, :], in_=d_ident),
            [("CONSTS", 16)])
        w.y(lambda y: y.dma_start(out=c_ones_f[:, :], in_=d_ones),
            [("CONSTS", 16)])
        w.y(lambda y: y.dma_start(out=c_o64_f[:, :], in_=d_ones64),
            [("CONSTS", 16)])
        sv_consts = w.c["CONSTS"]
        w.v(lambda v: v.memset(Vb[:, :, :, 64:65], 1.0), [("VE", 1)])
        ve_ones = w.c["VE"]

        wT_loaded = [0, 0]
        xtb_rd2 = [0, 0]
        wslice_rd = 0
        vb_rd_prev = 0
        qt_rd_prev = 0
        sq2_rd4 = [0, 0, 0, 0]
        ps_wr = [0] * 4
        ps_rd = [("SC", 0)] * 4
        py_rd = [0, 0]
        ptb_rd = [[0, 0, 0], [0, 0, 0]]
        ptb_ve = [[0, 0, 0], [0, 0, 0]]
        rb_rd = 0
        ti_rd = 0
        yst_st = [0, 0]
        yev_rd = 0

        for g in range(2):
            gsl = slice(g * 512, (g + 1) * 512)
            w.wait("gpsimd", "MM", wslice_rd)
            for dsrc, dst in ((d_wqT, wq_b), (d_wkT, wk_b), (d_wvT, wv_b)):
                w.g(lambda gg, dsrc=dsrc, dst=dst: gg.dma_start(
                    out=dst[:, :, :],
                    in_=dsrc.rearrange("(cc p) n -> p cc n", p=128)[:, :, gsl]),
                    [("WQKV", 16)])
            wT_loaded[g] = w.c["WQKV"]

            for ch in range(NCH):
                cb = ch % 2
                w.wait("gpsimd", "MM", xtb_rd2[cb])
                xts = "XTB" if cb == 0 else "XTB2"
                w.g(lambda gg, ch=ch, cb=cb, xts=xts: gg.dma_start(
                    out=xtbs[cb],
                    in_=d_xT.rearrange("(cc p) n -> p cc n", p=128)
                    [:, :, ch * 512:(ch + 1) * 512]), [(xts, 16)])
                gv_x = w.c[xts]
                w.wait("tensor", xts, gv_x)
                w.wait("tensor", "WQKV", wT_loaded[g])
                chs = slice(ch * 512, (ch + 1) * 512)
                for wi, (wbuf, dstT) in enumerate(((wq_b, QT), (wk_b, KT))):
                    for dt in range(4):
                        pb = (wi * 4 + dt) % 4
                        sem, cnt = ps_rd[pb]
                        w.wait("tensor", sem, cnt)
                        for cc in range(8):
                            w.t(lambda t, pb=pb, wbuf=wbuf, dt=dt, cc=cc,
                                cb=cb: t.matmul(
                                    p_s[pb],
                                    wbuf[:, cc, dt * 128:(dt + 1) * 128],
                                    xtbs[cb][:, cc, :],
                                    start=(cc == 0), stop=(cc == 7)),
                                [("MM", 1)] if cc == 7 else ())
                        mmv = w.c["MM"]
                        ps_wr[pb] = mmv
                        w.wait("vector", "MM", max(mmv, qt_rd_prev))
                        w.v(lambda v, pb=pb, dstT=dstT, dt=dt, chs=chs:
                            v.tensor_copy(dstT[:, dt, chs], p_s[pb]),
                            [("VE", 1)])
                        ps_rd[pb] = ("VE", w.c["VE"])
                for pt in range(4):
                    pb = pt % 4
                    sem, cnt = ps_rd[pb]
                    w.wait("tensor", sem, cnt)
                    for cc in range(8):
                        w.t(lambda t, pb=pb, cc=cc, pt=pt, cb=cb:
                            t.matmul(p_s[pb],
                                     xtbs[cb][:, cc, pt * 128:(pt + 1) * 128],
                                     wv_b[:, cc, :],
                                     start=(cc == 0), stop=(cc == 7)),
                            [("MM", 1)] if cc == 7 else ())
                    mmv = w.c["MM"]
                    ps_wr[pb] = mmv
                    kt_i = ch * 4 + pt
                    w.wait("vector", "VE", ve_ones)
                    w.wait("vector", "MM", max(mmv, vb_rd_prev))
                    w.v(lambda v, pb=pb, kt_i=kt_i:
                        v.tensor_copy(Vb[:, kt_i, 0:8, 0:64],
                                      p_s[pb].rearrange(
                                          "p (h d) -> p h d", d=64)),
                        [("VE", 1)])
                    ps_rd[pb] = ("VE", w.c["VE"])
                ptmax = max(ptb_rd[0] + ptb_rd[1])
                w.wait("tensor", "CONSTG", gv_consts)

                def norm_mm(w, item):
                    sqsl, nb, scv, mx, first = item
                    sem, cnt = ps_rd[nb]
                    w.wait("tensor", sem, cnt)
                    w.wait("tensor", "SC", scv)
                    w.t(lambda t, sqsl=sqsl, nb=nb: t.matmul(
                        p_s[nb], c_o2w_b[:, :], sq2[:, sqsl],
                        start=True, stop=True), [("MM", 1)])
                    ps_wr[nb] = w.c["MM"]
                    sq2_rd4[nb] = w.c["MM"]
                    w.wait("vector", "MM", ps_wr[nb])
                    w.v(lambda v, nb=nb: v.tensor_reduce(
                        mxt[:, :], p_s[nb], AX.X, ALU.max), [("VE", 1)])
                    ps_rd[nb] = ("VE", w.c["VE"])
                    w.drain("vector")
                    if first:
                        w.v(lambda v, mx=mx: v.tensor_copy(
                            mx[:, :], mxt[:, :]), [("VE", 1)])
                    else:
                        w.v(lambda v, mx=mx: v.tensor_tensor(
                            mx[:, :], mx[:, :], mxt[:, :], ALU.max),
                            [("VE", 1)])
                    w.drain("vector")

                pend_nm = None
                for si, (nsrc, mx) in enumerate(((QT, mxq), (KT, mxk))):
                    for pr in range(4):
                        nb = (si * 4 + pr) % 4
                        sqsl = slice(nb * 512, nb * 512 + 512)
                        w.wait("scalar", "MM", max(sq2_rd4[nb], ptmax))
                        w.wait("scalar", "VE", w.c["VE"])
                        w.s(lambda s, nsrc=nsrc, pr=pr, chs=chs, sqsl=sqsl:
                            s.activation(
                                sq2[:, sqsl], nsrc[:, pr, chs],
                                AF.Square), [("SC", 1)])
                        scv = w.c["SC"]
                        if pend_nm is not None:
                            norm_mm(w, pend_nm)
                        pend_nm = (sqsl, nb, scv, mx, ch == 0 and pr == 0)
                norm_mm(w, pend_nm)
                xtb_rd2[cb] = w.c["MM"]
            wslice_rd = w.c["MM"]
            ve_a1 = w.c["VE"]
            w.v(lambda v: v.tensor_tensor(sc1[:, 2:3], mxq[0:1, 0:1],
                                          mxk[0:1, 0:1], ALU.mult),
                [("VE", 1)])
            ve_m = w.c["VE"]
            w.wait("scalar", "VE", ve_m)
            w.s(lambda s: s.activation(sc1[:, 3:4], sc1[:, 2:3], AF.Ln),
                [("SC", 1)])
            w.drain("scalar")
            w.s(lambda s: s.activation(sc1[:, 2:3], sc1[:, 3:4], AF.Exp,
                                       scale=0.5), [("SC", 1)])
            scv_m = w.c["SC"]
            w.wait("vector", "SC", scv_m)
            w.drain("vector")
            w.v(lambda v: v.tensor_scalar_mul(sc1[:, 3:4], sc1[:, 2:3],
                                              -0.125), [("VE", 1)])
            ve_negm = w.c["VE"]
            sem0, cnt0 = ps_rd[0]
            w.wait("tensor", sem0, cnt0)
            w.wait("tensor", "VE", max(ve_negm, ti_rd))
            w.wait("tensor", "CONSTS", sv_consts)
            w.t(lambda t: t.matmul(p_sq[:, 0:1], c_ones_f[:, :], sc1[:, 3:4],
                                   start=True, stop=True), [("MM", 1)])
            mm_ti = w.c["MM"]
            w.wait("vector", "MM", mm_ti)
            w.v(lambda v, g=g: v.tensor_copy(negm_b[:, g:g + 1], p_sq[:, 0:1]),
                [("VE", 1)])
            ti_rd = w.c["VE"]
            ps_rd[0] = ("VE", w.c["VE"])
            ve_negm_done = w.c["VE"]

            w.wait("tensor", "CONSTG", gv_consts)
            pending_tail = [None]

            def flush_tail(w):
                if pending_tail[0] is None:
                    return
                fn = pending_tail[0]
                pending_tail[0] = None
                fn(w)

            for pr in range(4):
                for qb in range(NQB):
                    kts = 4 * qb + 4
                    qsl = slice(qb * 512, (qb + 1) * 512)
                    exp_v = [[("SC", 0)] * 3, [("SC", 0)] * 3]
                    for kt in range(kts + 2):
                        if kt == min(10, kts):
                            flush_tail(w)
                        if kt < kts:
                            j = kt - 4 * qb
                            i2 = kt % 2
                            i3 = kt % 3
                            qs = 128 * j if j > 0 else 0
                            for e in range(2):
                                sem, cnt = ps_rd[2 * i2 + e]
                                w.wait("tensor", sem, cnt)
                            w.wait("tensor", "VE", ve_a1)
                            for e in range(2):
                                hp = 64 * e
                                w.t(lambda t, i2=i2, e=e, hp=hp, pr=pr,
                                    kt=kt, qb=qb, qs=qs: t.matmul(
                                        p_s[2 * i2 + e][:, qs:512],
                                        KT[hp:hp + 64, pr,
                                           kt * 128:(kt + 1) * 128],
                                        QT[hp:hp + 64, pr,
                                           qb * 512 + qs:(qb + 1) * 512],
                                        start=True, stop=True,
                                        tile_position=(hp, 0)), [("MM", 1)])
                                ps_wr[2 * i2 + e] = w.c["MM"]
                            w.wait("scalar", "MM",
                                   max(ps_wr[2 * i2], ps_wr[2 * i2 + 1],
                                       ptb_rd[0][i3], ptb_rd[1][i3],
                                       max(sq2_rd4)))
                            w.wait("scalar", "VE",
                                   max(ve_negm_done, ptb_ve[0][i3],
                                       ptb_ve[1][i3]))
                            if qs == 0 and i3 < 2:
                                w.s(lambda s, i3=i3, i2=i2, g=g:
                                    s.activation(
                                        sq2[:, 2 * i3 * 512:
                                            (2 * i3 + 2) * 512],
                                        p_sq[:, 2 * i2 * 512:
                                             (2 * i2 + 2) * 512],
                                        AF.Exp, bias=negm_b[:, g:g + 1],
                                        scale=0.125), [("SC", 1)])
                            elif qs == 0:
                                w.s(lambda s, i2=i2, g=g: s.activation(
                                    xTb[:, 0:2, :].rearrange(
                                        "p a b -> p (a b)"),
                                    p_sq[:, 2 * i2 * 512:
                                         (2 * i2 + 2) * 512],
                                    AF.Exp, bias=negm_b[:, g:g + 1],
                                    scale=0.125), [("SC", 1)])
                            elif i3 < 2:
                                w.s(lambda s, i3=i3, i2=i2, g=g, qs=qs:
                                    s.activation(
                                        sq2[:, 2 * i3 * 512 + qs:
                                            (2 * i3 + 2) * 512],
                                        p_sq[:, 2 * i2 * 512 + qs:
                                             (2 * i2 + 2) * 512],
                                        AF.Exp, bias=negm_b[:, g:g + 1],
                                        scale=0.125), [("SC", 1)])
                            else:
                                w.s(lambda s, i2=i2, g=g, qs=qs:
                                    s.activation(
                                        xTb[:, 0:2, :].rearrange(
                                            "p a b -> p (a b)")[:, qs:1024],
                                        p_sq[:, 2 * i2 * 512 + qs:
                                             (2 * i2 + 2) * 512],
                                        AF.Exp, bias=negm_b[:, g:g + 1],
                                        scale=0.125), [("SC", 1)])
                            sc_exp = w.c["SC"]
                            ps_rd[2 * i2] = ("SC", sc_exp)
                            ps_rd[2 * i2 + 1] = ("SC", sc_exp)
                            if j >= 0:
                                w.wait("vector", "SC", sc_exp)
                                w.wait("vector", "CONSTG", gv_consts)
                                for e in range(2):
                                    w.v(lambda v, i3=i3, e=e, qs=qs:
                                        v.tensor_tensor(
                                            ptb3[e][i3][:, qs:qs + 128],
                                            ptb3[e][i3][:, qs:qs + 128],
                                            c_ltri_b[:, :], ALU.mult),
                                        [("VE", 1)])
                                    ptb_ve[e][i3] = w.c["VE"]
                                exp_v[0][i3] = ("VE", w.c["VE"])
                                exp_v[1][i3] = ("VE", w.c["VE"])
                            else:
                                exp_v[0][i3] = ("SC", sc_exp)
                                exp_v[1][i3] = ("SC", sc_exp)
                        if kt > 1:
                            pkt = kt - 2
                            pj = pkt - 4 * qb
                            pqs = 128 * pj if pj > 0 else 0
                            for e in range(2):
                                hg = 2 * pr + e
                                esem, ecnt = exp_v[e][pkt % 3]
                                w.wait("tensor", esem, ecnt)
                                if pkt == 0:
                                    w.wait("tensor", "VE", py_rd[e])
                                w.t(lambda t, e=e, pkt=pkt, hg=hg, kts=kts,
                                    pqs=pqs: t.matmul(
                                        p_y[e][:, pqs:512],
                                        Vb[:, pkt, hg, 0:65],
                                        ptb3[e][pkt % 3][:, pqs:512],
                                        start=(pkt == 0),
                                        stop=(pkt == kts - 1)),
                                    [("MM", 1)])
                                ptb_rd[e][pkt % 3] = w.c["MM"]
                    mm_pv = w.c["MM"]
                    w.wait("vector", "MM", mm_pv)
                    w.wait("vector", "VE", yev_rd)
                    w.drain("vector")
                    for e in range(2):
                        def _evac(v, e=e):
                            with nc.allow_low_precision(
                                    reason="bf16 attention-out operand"):
                                return v.tensor_copy(yev[:, e, :],
                                                     p_y[e][:, :])
                        w.v(_evac, [("VE", 1)])
                        py_rd[e] = w.c["VE"]
                    w.drain("vector")
                    for e in range(2):
                        def _recip(v, e=e):
                            with nc.allow_low_precision(
                                    reason="bf16 denom bcast operand"):
                                return v.reciprocal(
                                    rdf[64:65, e * 512:(e + 1) * 512],
                                    yev[64:65, e, :])
                        w.v(_recip, [("VE", 1)])
                    ve_rd2 = w.c["VE"]

                    def tail(w, g=g, pr=pr, qsl=qsl, ve_rd2=ve_rd2):
                        nonlocal rb_rd, yev_rd
                        ysl = 0
                        for e in range(2):
                            w.wait("tensor", "VE", max(ve_rd2, rb_rd))
                            w.wait("tensor", "CONSTG", gv_consts)
                            w.t(lambda t, e=e: t.matmul(
                                p_rb[:, :], c_o64_b[64:65, :],
                                rdf[64:65, e * 512:(e + 1) * 512],
                                start=True, stop=True), [("MM", 1)])
                            mm_rb = w.c["MM"]
                            w.wait("vector", "MM", mm_rb)
                            w.drain("vector")
                            w.v(lambda v: v.tensor_copy(rb_sb, p_rb[:, :]),
                                [("VE", 1)])
                            rb_rd = w.c["VE"]
                            w.drain("vector")
                            if e == 0:
                                def _m0(v, g=g, pr=pr, qsl=qsl):
                                    with nc.allow_low_precision(
                                            reason="bf16 yT operand"):
                                        return v.tensor_tensor(
                                            yTn[g][0:64, pr, qsl],
                                            yev[0:64, 0, :], rb_sb,
                                            ALU.mult)
                                w.v(_m0, [("VE", 1)])
                                yev_rd = w.c["VE"]
                            else:
                                yss = "YST0" if ysl == 0 else "YST1"
                                w.wait("vector", yss, yst_st[ysl])

                                def _m1(v, ysl=ysl):
                                    with nc.allow_low_precision(
                                            reason="bf16 yT operand"):
                                        return v.tensor_tensor(
                                            yst[ysl][:, :],
                                            yev[0:64, 1, :], rb_sb,
                                            ALU.mult)
                                w.v(_m1, [("VE", 1)])
                                yev_rd = w.c["VE"]
                                ve_yst = w.c["VE"]
                                w.wait("sync", "VE", ve_yst)
                                w.y(lambda y, g=g, pr=pr, qsl=qsl, ysl=ysl,
                                    yss=yss: y.dma_start(
                                        out=yTn[g][64:128, pr, qsl],
                                        in_=yst[ysl][:, :]), [(yss, 16)])
                                yst_st[ysl] = w.c[yss]
                    pending_tail[0] = tail
                flush_tail(w)
            flush_tail(w)
            vb_rd_prev = w.c["MM"]
            qt_rd_prev = w.c["MM"]
            xtb_rd2[0] = w.c["MM"]
            xtb_rd2[1] = w.c["MM"]

        if tap_t:
            w.wait("sync", "VE", w.c["VE"])
            w.wait("sync", "MM", w.c["MM"])
            w.wait("sync", "CONSTG", gv_consts)
            w.wait("sync", "YST0", w.c["YST0"])
            w.wait("sync", "YST1", w.c["YST1"])
        if "tap_qt" in tap_t:
            w.y(lambda y: y.dma_start(out=tap_t["tap_qt"], in_=QT[:, :, :]),
                [("TAP", 16)])
        if "tap_kt" in tap_t:
            w.y(lambda y: y.dma_start(out=tap_t["tap_kt"], in_=KT[:, :, :]),
                [("TAP", 16)])
        if "tap_v" in tap_t:
            w.y(lambda y: y.dma_start(out=tap_t["tap_v"], in_=Vb[:, :, :, :]),
                [("TAP", 16)])
        if "tap_negm" in tap_t:
            w.y(lambda y: y.dma_start(out=tap_t["tap_negm"], in_=negm_b[:, :]),
                [("TAP", 16)])
        if "tap_ytn" in tap_t:
            w.y(lambda y: y.dma_start(out=tap_t["tap_ytn"][0],
                                      in_=yTn[0][:, :, :]), [("TAP", 16)])
            w.y(lambda y: y.dma_start(out=tap_t["tap_ytn"][1],
                                      in_=yTn[1][:, :, :]), [("TAP", 16)])
        if w.name == "sync":
            w.eng.wait_ge(w.sems["YST0"], w.c["YST0"])
            w.eng.wait_ge(w.sems["YST1"], w.c["YST1"])
            w.eng.wait_ge(w.sems["TAP"], w.c["TAP"])

    def run_block(blk, emit):
        for en in ENGS:
            def cb(e, en=en):
                walks[en].eng = e
                emit(walks[en])
            getattr(blk, en)(cb)

    with nc.Block("blkA") as blkA:
        run_block(blkA, emit_A)

    st_a.close()

    st_x1t = ExitStack()
    x1T = st_x1t.enter_context(
        nc.sbuf_tensor("x1T", [128, 8, S], BF16, side="right"))
    st_a3 = ExitStack()
    E3 = st_a3.enter_context
    woT_b = E3(nc.sbuf_tensor("woT_b", [128, 8, 1024], BF16))
    xres = [E3(nc.sbuf_tensor(f"xres{i}", [128, D], F32)) for i in range(3)]
    x1pre = [E3(nc.sbuf_tensor(f"x1pre{i}", [128, D], F32)) for i in range(2)]
    x1f = [E3(nc.sbuf_tensor(f"x1f{i}", [128, D], F32)) for i in range(2)]
    x1bb = [E3(nc.sbuf_tensor(f"x1bb{i}", [128, D], BF16)) for i in range(4)]
    junk_b = E3(nc.sbuf_tensor("junk_b", [128, D], BF16))
    stt = E3(nc.sbuf_tensor("stt", [128, 2, 16], F32))
    p_z = [E3(nc.psum_tensor(f"p_z{i}", [128, 512], F32)) for i in range(2)]
    p_t = [E3(nc.psum_tensor(f"p_t{i}", [128, 128], BF16)) for i in range(4)]

    def emit_A3(w: Walk):
        w.g(lambda g: g.dma_start(
            out=woT_b[:, :, :],
            in_=d_woT.rearrange("(cc p) n -> p cc n", p=128)), [("WO", 16)])
        gv_wo = w.c["WO"]
        xres_rd = [0, 0, 0]
        x1f_rd = [0, 0]
        x1b_rd = [0, 0]
        x1pre_rd = [0, 0]
        stt_rd = 0
        pz_rd = [0, 0]
        pt_rd = [0, 0, 0, 0]
        pend_tp = []
        pend_ln = []
        x1b_rd3 = [0, 0, 0, 0]

        def ln_stage2(w, item):
            nonlocal x1f_rd, pend_tp
            pt, bi, ve_var = item
            psl = slice(pt * 128, (pt + 1) * 128)
            w.wait("scalar", "VE", ve_var)
            w.s(lambda s, bi=bi: s.activation(stt[:, bi, 15:16],
                                              stt[:, bi, 14:15], AF.Ln),
                [("SC", 1)])
            w.drain("scalar")
            w.s(lambda s, bi=bi: s.activation(stt[:, bi, 14:15],
                                              stt[:, bi, 15:16], AF.Exp,
                                              scale=-0.5), [("SC", 1)])
            sc_rstd = w.c["SC"]
            w.wait("vector", "SC", sc_rstd)
            w.wait("vector", "X1F0" if bi == 0 else "X1F1", x1f_rd[bi])
            w.drain("vector")
            w.v(lambda v, bi=bi: v.tensor_scalar(x1f[bi][:, :],
                                                 x1pre[bi][:, :],
                                                 stt[:, bi, 12:13],
                                                 stt[:, bi, 14:15],
                                                 ALU.subtract, ALU.mult),
                [("VE", 1)])
            ve_x1 = w.c["VE"]
            ti3 = pt % 4
            w.drain("vector")
            w.wait("vector", "MM", x1b_rd3[ti3])
            w.v(lambda v, ti3=ti3, bi=bi: v.tensor_copy(x1bb[ti3][:, :],
                                                        x1f[bi][:, :]),
                [("VE", 1)])
            ve_x1b = w.c["VE"]
            w.wait("sync", "VE", ve_x1)
            x1s = "X1F0" if bi == 0 else "X1F1"
            w.y(lambda y, bi=bi, psl=psl, x1s=x1s: y.dma_start(
                out=d_x1h[psl, :], in_=x1f[bi][:, :]), [(x1s, 16)])
            x1f_rd[bi] = w.c[x1s]
            pend_tp.append((pt, ve_x1b))

        for pt in range(NPT):
            bi = pt % 2
            ri = pt % 3
            psl = slice(pt * 128, (pt + 1) * 128)
            w.wait("sync", "VE", xres_rd[ri])
            xrs = ("XRES0", "XRES1", "XRES2")[ri]
            w.y(lambda y, psl=psl, ri=ri: y.dma_start(out=xres[ri][:, :],
                                                      in_=d_x[psl, :]),
                [(xrs, 16)])
            sv_x = w.c[xrs]
            w.wait("tensor", "WO", gv_wo)
            for half in range(2):
                w.wait("tensor", "VE", pz_rd[half])
                for cc in range(8):
                    g2, p2 = divmod(cc, 4)
                    w.t(lambda t, half=half, g2=g2, p2=p2, cc=cc, psl=psl:
                        t.matmul(p_z[half][:, :],
                                 yTn[g2][:, p2, psl],
                                 woT_b[:, cc, half * 512:(half + 1) * 512],
                                 start=(cc == 0), stop=(cc == 7)),
                        [("MM", 1)] if cc == 7 else ())
            mm_z = w.c["MM"]
            w.wait("vector", "MM", mm_z)
            w.wait("vector", xrs, sv_x)
            for half in range(2):
                w.v(lambda v, half=half, bi=bi, ri=ri: v.tensor_tensor(
                    x1pre[bi][:, half * 512:(half + 1) * 512],
                    p_z[half][:, :],
                    xres[ri][:, half * 512:(half + 1) * 512], ALU.add),
                    [("VE", 1)])
                pz_rd[half] = w.c["VE"]
            xres_rd[ri] = w.c["VE"]
            ve_pre = w.c["VE"]
            w.drain("vector")
            w.v(lambda v, bi=bi: v.bn_stats(stt[:, bi, 0:6],
                                            x1pre[bi][:, 0:512]),
                [("VE", 1)])
            w.drain("vector")
            w.v(lambda v, bi=bi: v.bn_stats(stt[:, bi, 6:12],
                                            x1pre[bi][:, 512:1024]),
                [("VE", 1)])
            w.drain("vector")
            w.v(lambda v, bi=bi: v.bn_aggr(stt[:, bi, 12:14],
                                           stt[:, bi, 0:12]), [("VE", 1)])
            w.drain("vector")
            w.v(lambda v, bi=bi: v.tensor_scalar(stt[:, bi, 14:15],
                                                 stt[:, bi, 13:14],
                                                 1.0, EPS, ALU.mult, ALU.add),
                [("VE", 1)])
            ve_var = w.c["VE"]
            pend_ln.append((pt, bi, ve_var))
            if len(pend_ln) > 1:
                ln_stage2(w, pend_ln.pop(0))
            if len(pend_tp) > 3:
                tpt, tve = pend_tp.pop(0)
                tb3 = tpt % 4
                for cc in range(8):
                    ti = cc % 4
                    w.wait("tensor", "VE", max(tve, pt_rd[ti]))
                    w.t(lambda t, ti=ti, tb3=tb3, cc=cc: t.transpose(
                        p_t[ti][:, :], x1bb[tb3][:, cc * 128:(cc + 1) * 128],
                        c_id_b[:, :]), [("MM", 1)])
                    mm_t = w.c["MM"]
                    if cc == 7:
                        x1b_rd3[tb3] = mm_t
                    w.wait("vector", "MM", mm_t)
                    w.v(lambda v, cc=cc, tpt=tpt, ti=ti: v.tensor_copy(
                        x1T[:, cc, tpt * 128:(tpt + 1) * 128], p_t[ti][:, :]),
                        [("VE", 1)])
                    pt_rd[ti] = w.c["VE"]
        while pend_ln:
            ln_stage2(w, pend_ln.pop(0))
        while pend_tp:
            tpt, tve = pend_tp.pop(0)
            tb3 = tpt % 4
            for cc in range(8):
                ti = cc % 2
                w.wait("tensor", "VE", max(tve, pt_rd[ti]))
                w.t(lambda t, ti=ti, tb3=tb3, cc=cc: t.transpose(
                    p_t[ti][:, :], x1bb[tb3][:, cc * 128:(cc + 1) * 128],
                    c_id_b[:, :]), [("MM", 1)])
                mm_t = w.c["MM"]
                if cc == 7:
                    x1b_rd3[tb3] = mm_t
                w.wait("vector", "MM", mm_t)
                w.v(lambda v, cc=cc, tpt=tpt, ti=ti: v.tensor_copy(
                    x1T[:, cc, tpt * 128:(tpt + 1) * 128], p_t[ti][:, :]),
                    [("VE", 1)])
                pt_rd[ti] = w.c["VE"]
        if "tap_x1" in tap_t:
            w.wait("sync", "X1F0", x1f_rd[0])
            w.wait("sync", "X1F1", x1f_rd[1])
            w.y(lambda y: y.dma_start(out=tap_t["tap_x1"], in_=d_x1h),
                [("TAP", 16)])
        if "tap_x1t" in tap_t:
            w.wait("sync", "VE", w.c["VE"])
            w.y(lambda y: y.dma_start(out=tap_t["tap_x1t"], in_=x1T[:, :, :]),
                [("TAP", 16)])
        if w.name == "sync":
            w.eng.wait_ge(w.sems["X1F0"], w.c["X1F0"])
            w.eng.wait_ge(w.sems["X1F1"], w.c["X1F1"])
            w.eng.wait_ge(w.sems["TAP"], w.c["TAP"])

    with nc.Block("blkA3") as blkA3:
        run_block(blkA3, emit_A3)

    st_a3.close()
    st_ytn.close()

    st_b = ExitStack()
    EB = st_b.enter_context
    y_acc = EB(nc.sbuf_tensor("y_acc", [128, NQT, D], F32))
    w13_b = [EB(nc.sbuf_tensor(f"w13_b{i}", [128, 8, 1024], BF16))
             for i in range(2)]
    w2_b = [EB(nc.sbuf_tensor(f"w2_b{i}", [128, 4, 1024], BF16))
            for i in range(2)]
    hT = [EB(nc.sbuf_tensor(f"hT{i}", [128, 4, BCH], BF16)) for i in range(2)]
    sg_b = EB(nc.sbuf_tensor("sg_b", [128, 512], BF16))
    gw_b = EB(nc.sbuf_tensor("gw_b", [128, 8, NE], BF16))
    gb_sb = EB(nc.sbuf_tensor("gb_sb", [NE, 1], F32))
    gbT_sb = EB(nc.sbuf_tensor("gbT_sb", [1, NE], F32))
    biT = EB(nc.sbuf_tensor("biT", [NE, S], F32))
    bi_pm = EB(nc.sbuf_tensor("bi_pm", [128, NPT, NE], F32))
    gbb = EB(nc.sbuf_tensor("gbb", [128, NE], F32))
    sp_pm = EB(nc.sbuf_tensor("sp_pm", [128, NE], F32))
    top8 = EB(nc.sbuf_tensor("top8", [128, 8], F32))
    msk_b = EB(nc.sbuf_tensor("msk_b", [128, NE], F32))
    wsel = EB(nc.sbuf_tensor("wsel", [128, NE], F32))
    dnr = EB(nc.sbuf_tensor("dnr", [128, 1], F32))
    comb = EB(nc.sbuf_tensor("comb", [128, NPT, NE], F32))
    x2res = [EB(nc.sbuf_tensor(f"x2res{i}", [128, D], F32)) for i in range(2)]
    x2pre = [EB(nc.sbuf_tensor(f"x2pre{i}", [128, D], F32)) for i in range(2)]
    outf = [EB(nc.sbuf_tensor(f"outf{i}", [128, D], F32)) for i in range(2)]
    junk2 = EB(nc.sbuf_tensor("junk2", [128, D], BF16))
    st2 = EB(nc.sbuf_tensor("st2", [128, 2, 16], F32))
    p_h = [EB(nc.psum_tensor(f"p_h{i}", [128, 512], F32)) for i in range(4)]
    p_zb = [EB(nc.psum_tensor(f"p_zb{i}", [128, 512], F32)) for i in range(2)]
    p_r16 = EB(nc.psum_tensor("p_r16", [128, 512], F32))
    p_rt2 = EB(nc.psum_tensor("p_rt2", [128, 16], F32))
    p_tb = p_r16[:, 32:48]
    p_tb2 = [p_r16[:, 64:80], p_rt2[:, :]]

    NEXP = NE + 1
    NGI = 4 * NEXP

    def exp_dma(w, e, bslot):
        ws = "W13A" if bslot == 0 else "W13B"
        if e < NE:
            w.g(lambda g, e=e, bslot=bslot: g.dma_start(
                out=w13_b[bslot][:, :, :],
                in_=d_w13T.rearrange("e (cc p) n -> e p cc n", p=128)[e]),
                [(ws, 16)])
            w.g(lambda g, e=e, bslot=bslot: g.dma_start(
                out=w2_b[bslot][:, :, :],
                in_=d_w2T.rearrange("e (cc p) n -> e p cc n", p=128)[e]),
                [(ws, 16)])
        else:
            w.g(lambda g, bslot=bslot: g.dma_start(
                out=w13_b[bslot][:, :, :],
                in_=d_w13sT.rearrange("(cc p) n -> p cc n", p=128)),
                [(ws, 16)])
            w.g(lambda g, bslot=bslot: g.dma_start(
                out=w2_b[bslot][:, :, :],
                in_=d_w2sT.rearrange("(cc p) n -> p cc n", p=128)),
                [(ws, 16)])
        return w.c[ws]

    def emit_B(w: Walk):
        w.g(lambda g: g.dma_start(
            out=gw_b[:, :, :],
            in_=d_gwT.rearrange("(cc p) n -> p cc n", p=128)), [("GW", 16)])
        gv_gw = w.c["GW"]
        w.y(lambda y: y.dma_start(out=gb_sb[:, :], in_=d_gb), [("GBS", 16)])
        w.y(lambda y: y.dma_start(out=gbT_sb[:, :], in_=d_gbT), [("GBS", 16)])
        sv_gb = w.c["GBS"]
        r16_rd = 0
        for ch in range(NCH):
            chs = slice(ch * 512, (ch + 1) * 512)
            w.wait("tensor", "GW", gv_gw)
            w.wait("tensor", "SC", r16_rd)
            for cc in range(8):
                w.t(lambda t, cc=cc, chs=chs: t.matmul(
                    p_r16[0:NE, :], gw_b[:, cc, :], x1T[:, cc, chs],
                    start=(cc == 0), stop=(cc == 7)),
                    [("MM", 1)] if cc == 7 else ())
            mm_r = w.c["MM"]
            w.wait("scalar", "MM", mm_r)
            w.s(lambda s, chs=chs: s.activation(biT[:, chs], p_r16[0:NE, :],
                                                AF.Sigmoid), [("SC", 1)])
            r16_rd = w.c["SC"]
        sc_router = w.c["SC"]
        w.wait("tensor", "GBS", sv_gb)
        w.wait("tensor", "SC", sc_router)
        w.t(lambda t: t.matmul(p_tb, c_ones_f[:, :], gbT_sb[:, :],
                               start=True, stop=True), [("MM", 1)])
        mm_gbb = w.c["MM"]
        w.wait("vector", "MM", mm_gbb)
        w.v(lambda v: v.tensor_copy(gbb[:, :], p_tb), [("VE", 1)])
        ptb_rd = w.c["VE"]
        ptb_rd2 = [ptb_rd, ptb_rd]
        w.wait("vector", "SC", sc_router)
        w.wait("vector", "GBS", sv_gb)
        w.v(lambda v: v.tensor_scalar(biT[:, :], biT[:, :], gb_sb[:, :], None,
                                      ALU.add), [("VE", 1)])
        ve_bi = w.c["VE"]

        pending_route = []

        def route_one(w, pt):
            nonlocal ptb_rd
            if True:
                tb = pt % 2
                w.wait("tensor", "VE", max(ve_bi, ptb_rd2[tb]))
                w.t(lambda t, pt=pt, tb=tb: t.transpose(
                    p_tb2[tb], biT[:, pt * 128:(pt + 1) * 128],
                    c_id_f[0:16, 0:16]), [("MM", 1)])
                mm_t = w.c["MM"]
                w.wait("vector", "MM", mm_t)
                w.drain("vector")
                w.v(lambda v, pt=pt, tb=tb: v.tensor_copy(bi_pm[:, pt, :],
                                                          p_tb2[tb]),
                    [("VE", 1)])
                ptb_rd2[tb] = w.c["VE"]
                ptb_rd = w.c["VE"]
                w.drain("vector")
                w.v(lambda v, pt=pt: v.tensor_tensor(
                    sp_pm[:, :], bi_pm[:, pt, :], gbb[:, :], ALU.subtract),
                    [("VE", 1)])
                w.drain("vector")
                w.v(lambda v, pt=pt: v.max(out=top8[:, :],
                                           in_=bi_pm[:, pt, :]), [("VE", 1)])
                w.drain("vector")
                w.v(lambda v, pt=pt: v.tensor_scalar(
                    msk_b[:, :], bi_pm[:, pt, :], top8[:, 3:4], None,
                    ALU.is_ge), [("VE", 1)])
                w.drain("vector")
                w.v(lambda v: v.tensor_tensor(wsel[:, :], sp_pm[:, :],
                                              msk_b[:, :], ALU.mult),
                    [("VE", 1)])
                w.drain("vector")
                w.v(lambda v: v.tensor_reduce(dnr[:, :], wsel[:, :], AX.X,
                                              ALU.add), [("VE", 1)])
                w.drain("vector")
                w.v(lambda v: v.reciprocal(dnr[:, :], dnr[:, :]),
                    [("VE", 1)])
                w.drain("vector")
                w.v(lambda v, pt=pt: v.tensor_scalar(
                    comb[:, pt, :], wsel[:, :], dnr[:, :], ROUTE_SCALE,
                    ALU.mult, ALU.mult), [("VE", 1)])

        def route_tiles(w, pts):
            for pt in pts:
                route_one(w, pt)

        route_tiles(w, range(0, NQT))

        wload = [0, 0]
        w13_rd = [0, 0]
        hT_rd = [0, 0]
        ph_rd = [("SC", 0), ("VE", 0), ("SC", 0), ("VE", 0)]
        pz_rd = [0, 0]
        sg_rd = 0
        x2res_rd = [0, 0]
        outf_rd = [0, 0]
        pend_l2 = []

        def ln2_stage2(w, item):
            nonlocal outf_rd
            pt_g, bi, ve_var = item
            psl = slice(pt_g * 128, (pt_g + 1) * 128)
            w.wait("scalar", "VE", ve_var)
            w.s(lambda s, bi=bi: s.activation(st2[:, bi, 15:16],
                                              st2[:, bi, 14:15],
                                              AF.Ln), [("SC", 1)])
            w.drain("scalar")
            w.s(lambda s, bi=bi: s.activation(st2[:, bi, 14:15],
                                              st2[:, bi, 15:16],
                                              AF.Exp, scale=-0.5),
                [("SC", 1)])
            sc_rstd = w.c["SC"]
            w.wait("vector", "SC", sc_rstd)
            w.wait("vector", "OUT0" if bi == 0 else "OUT1", outf_rd[bi])
            w.drain("vector")
            w.v(lambda v, bi=bi: v.tensor_scalar(outf[bi][:, :],
                                                 x2pre[bi][:, :],
                                                 st2[:, bi, 12:13],
                                                 st2[:, bi, 14:15],
                                                 ALU.subtract, ALU.mult),
                [("VE", 1)])
            ve_out = w.c["VE"]
            w.wait("sync", "VE", ve_out)
            ots = "OUT0" if bi == 0 else "OUT1"
            w.y(lambda y, bi=bi, psl=psl, ots=ots: y.dma_start(
                out=d_out[psl, :], in_=outf[bi][:, :]), [(ots, 16)])
            outf_rd[bi] = w.c[ots]
        tap_yg = [0]
        wload[0] = exp_dma(w, 0, 0)
        for gi in range(NGI):
            qt_i, e = divmod(gi, NEXP)
            bslot = gi % 2
            if e == NEXP // 2 - 2 and qt_i + 1 < 4:
                pending_route.extend(range((qt_i + 1) * NQT,
                                           (qt_i + 2) * NQT))
            if gi + 1 < NGI:
                nslot = (gi + 1) % 2
                w.wait("gpsimd", "MM", w13_rd[nslot])
                wload[nslot] = exp_dma(w, (gi + 1) % NEXP, nslot)
            def emit_z(w, ch, ve_h, bslot=bslot, qt_i=qt_i, e=e):
                nonlocal pz_rd, hT_rd, w13_rd
                w.drain("vector")
                for pst in range(NST):
                    t_i = ch * NST + pst
                    pt_g = qt_i * NQT + t_i
                    for co in range(2):
                        zb = co
                        w.wait("tensor", "VE", max(ve_h, pz_rd[zb]))
                        for iop in range(4):
                            w.t(lambda t, zb=zb, iop=iop, ch=ch, pst=pst,
                                bslot=bslot, co=co: t.matmul(
                                    p_zb[zb][:, :],
                                    hT[ch % 2][:, iop,
                                               pst * 128:(pst + 1) * 128],
                                    w2_b[bslot][:, iop,
                                                co * 512:(co + 1) * 512],
                                    start=(iop == 0), stop=(iop == 3)),
                                [("MM", 1)] if iop == 3 else ())
                        mm_zb = w.c["MM"]
                        hT_rd[ch % 2] = mm_zb
                        w13_rd[bslot] = mm_zb
                        w.wait("vector", "MM", mm_zb)
                        w.wait("vector", "TAP", tap_yg[0])
                        csl = slice(co * 512, (co + 1) * 512)
                        if e == 0:
                            w.v(lambda v, t_i=t_i, csl=csl, zb=zb, pt_g=pt_g:
                                v.tensor_scalar(y_acc[:, t_i, csl],
                                                p_zb[zb][:, :],
                                                comb[:, pt_g, 0:1], None,
                                                ALU.mult), [("VE", 1)])
                        elif e < NE:
                            w.v(lambda v, t_i=t_i, csl=csl, zb=zb, pt_g=pt_g,
                                e=e: v.scalar_tensor_tensor(
                                    y_acc[:, t_i, csl], p_zb[zb][:, :],
                                    comb[:, pt_g, e:e + 1],
                                    y_acc[:, t_i, csl],
                                    ALU.mult, ALU.add), [("VE", 1)])
                        else:
                            w.v(lambda v, t_i=t_i, csl=csl, zb=zb:
                                v.scalar_tensor_tensor(
                                    y_acc[:, t_i, csl], p_zb[zb][:, :], 1.0,
                                    y_acc[:, t_i, csl],
                                    ALU.mult, ALU.add), [("VE", 1)])
                        pz_rd[zb] = w.c["VE"]

            pend_z = None
            for ch in range(NBCH):
                gpos = qt_i * QTR + ch * BCH
                gsl = slice(gpos, gpos + BCH)
                w.wait("tensor", "W13A" if bslot == 0 else "W13B",
                       wload[bslot])
                if ch > 0:
                    w.wait("tensor", "MM", hT_rd[ch % 2])
                for iop in range(4):
                    pa = 2 * (iop % 2)
                    pb_ = pa + 1
                    for pslot, iotile in ((pa, iop), (pb_, 4 + iop)):
                        sem, cnt = ph_rd[pslot]
                        w.wait("tensor", sem, cnt)
                        for cc in range(8):
                            w.t(lambda t, pslot=pslot, cc=cc, iotile=iotile,
                                bslot=bslot, gsl=gsl: t.matmul(
                                    p_h[pslot][:, 0:BCH],
                                    w13_b[bslot][:, cc,
                                                 iotile * 128:
                                                 (iotile + 1) * 128],
                                    x1T[:, cc, gsl],
                                    start=(cc == 0), stop=(cc == 7)),
                                [("MM", 1)] if cc == 7 else ())
                    mm_gu = w.c["MM"]
                    if pending_route:
                        route_one(w, pending_route.pop(0))
                    w.wait("scalar", "MM", mm_gu)
                    w.wait("scalar", "VE", sg_rd)
                    w.s(lambda s, pa=pa: s.activation(sg_b[:, 0:BCH],
                                                      p_h[pa][:, 0:BCH],
                                                      AF.Silu), [("SC", 1)])
                    sc_silu = w.c["SC"]
                    ph_rd[pa] = ("SC", sc_silu)
                    w.wait("vector", "SC", sc_silu)
                    w.wait("vector", "MM", hT_rd[ch % 2])
                    w.v(lambda v, iop=iop, ch=ch, pb_=pb_: v.tensor_tensor(
                        hT[ch % 2][:, iop, :], sg_b[:, 0:BCH],
                        p_h[pb_][:, 0:BCH], ALU.mult), [("VE", 1)])
                    sg_rd = w.c["VE"]
                    ph_rd[pb_] = ("VE", w.c["VE"])
                ve_h = w.c["VE"]
                if pend_z is not None:
                    emit_z(w, pend_z[0], pend_z[1])
                pend_z = (ch, ve_h)
            emit_z(w, pend_z[0], pend_z[1])
            if e != NE:
                continue
            if "tap_comb" in tap_t and qt_i == 3 and e == NE:
                w.wait("sync", "VE", w.c["VE"])
                w.y(lambda y: y.dma_start(out=tap_t["tap_comb"],
                                          in_=comb[:, :, :]), [("TAP", 16)])
            if "tap_yacc" in tap_t and qt_i == 0:
                w.wait("sync", "VE", w.c["VE"])
                w.y(lambda y: y.dma_start(out=tap_t["tap_yacc"],
                                          in_=y_acc[:, :, :]), [("TAP", 16)])
                tap_yg[0] = w.c["TAP"]
            w.drain("vector")
            for t_i in range(NQT):
                bi = t_i % 2
                pt_g = qt_i * NQT + t_i
                psl = slice(pt_g * 128, (pt_g + 1) * 128)
                w.wait("sync", "VE", x2res_rd[bi])
                x2s = "X2RES0" if bi == 0 else "X2RES1"
                w.y(lambda y, psl=psl, bi=bi, x2s=x2s: y.dma_start(
                    out=x2res[bi][:, :], in_=d_x1h[psl, :]), [(x2s, 16)])
                sv_r = w.c[x2s]
                w.wait("vector", x2s, sv_r)
                w.v(lambda v, t_i=t_i, bi=bi: v.tensor_tensor(
                    x2pre[bi][:, :], y_acc[:, t_i, :], x2res[bi][:, :],
                    ALU.add), [("VE", 1)])
                x2res_rd[bi] = w.c["VE"]
                w.drain("vector")
                w.v(lambda v, bi=bi: v.bn_stats(st2[:, bi, 0:6],
                                                x2pre[bi][:, 0:512]),
                    [("VE", 1)])
                w.drain("vector")
                w.v(lambda v, bi=bi: v.bn_stats(st2[:, bi, 6:12],
                                                x2pre[bi][:, 512:1024]),
                    [("VE", 1)])
                w.drain("vector")
                w.v(lambda v, bi=bi: v.bn_aggr(st2[:, bi, 12:14],
                                               st2[:, bi, 0:12]),
                    [("VE", 1)])
                w.drain("vector")
                w.v(lambda v, bi=bi: v.tensor_scalar(st2[:, bi, 14:15],
                                                     st2[:, bi, 13:14],
                                                     1.0, EPS, ALU.mult,
                                                     ALU.add), [("VE", 1)])
                ve_var = w.c["VE"]
                pend_l2.append((pt_g, bi, ve_var))
                if len(pend_l2) > 1:
                    ln2_stage2(w, pend_l2.pop(0))
            while pend_l2:
                ln2_stage2(w, pend_l2.pop(0))
        if w.name == "sync":
            w.eng.wait_ge(w.sems["OUT0"], w.c["OUT0"])
            w.eng.wait_ge(w.sems["OUT1"], w.c["OUT1"])
            w.eng.wait_ge(w.sems["TAP"], w.c["TAP"])

    with nc.Block("blkB") as blkB:
        run_block(blkB, emit_B)

    st_b.close()
    st_x1t.close()
    st_persist.close()

    base = walks["tensor"].c
    for n, wk in walks.items():
        assert wk.c == base, (n, wk.c, base)
    return nc


def _host_consts():
    ident = np.eye(128, dtype=np.float32)
    k = np.arange(128)[:, None]
    q = np.arange(128)[None, :]
    cmask = np.ascontiguousarray((q >= k).astype(np.float32))
    ones = np.ones((1, 128), dtype=np.float32)
    o2w = np.zeros((128, 128), dtype=np.float32)
    o2w[:, 0] = 1.0
    o64 = np.ones((128, 64), dtype=np.float32)
    return ident, cmask, ones, o2w, o64


def make_shared_inputs(L0_attn_W_Q, L0_attn_W_K, L0_attn_W_V, L0_attn_W_O,
                       L0_ffn_gate_w, L0_ffn_gate_b, L0_ffn_w13, L0_ffn_w2,
                       L0_ffn_w13_s, L0_ffn_w2_s):
    ident, cmask, ones, o2w, o64 = _host_consts()
    f32 = np.float32
    A = lambda a: np.ascontiguousarray(np.asarray(a, f32))
    return {
        "wqT": A(np.asarray(L0_attn_W_Q, f32).T),
        "wkT": A(np.asarray(L0_attn_W_K, f32).T),
        "wvT": A(np.asarray(L0_attn_W_V, f32).T),
        "woT": A(np.asarray(L0_attn_W_O, f32).T),
        "gwT": A(np.asarray(L0_ffn_gate_w, f32).T),
        "gb": A(np.asarray(L0_ffn_gate_b, f32)[:, None]),
        "gbT": A(np.asarray(L0_ffn_gate_b, f32)[None, :]),
        "w13T": A(np.asarray(L0_ffn_w13, f32).transpose(0, 2, 1)),
        "w2T": A(np.asarray(L0_ffn_w2, f32).transpose(0, 2, 1)),
        "w13sT": A(np.asarray(L0_ffn_w13_s, f32).T),
        "w2sT": A(np.asarray(L0_ffn_w2_s, f32).T),
        "ident": ident, "ltri": cmask, "ones": ones, "ones2w": o2w,
        "ones64": o64,
    }


def kernel(L0_attn_x, L0_attn_W_Q, L0_attn_W_K, L0_attn_W_V, L0_attn_W_O,
           L0_ffn_input_ids, L0_ffn_gate_w, L0_ffn_gate_b, L0_ffn_w13,
           L0_ffn_w2, L0_ffn_w13_s, L0_ffn_w2_s):
    shared = make_shared_inputs(L0_attn_W_Q, L0_attn_W_K, L0_attn_W_V,
                                L0_attn_W_O, L0_ffn_gate_w, L0_ffn_gate_b,
                                L0_ffn_w13, L0_ffn_w2, L0_ffn_w13_s,
                                L0_ffn_w2_s)
    xs = np.asarray(L0_attn_x, np.float32)
    in_maps = []
    for c in range(8):
        x = np.ascontiguousarray(xs[c])
        xT = np.ascontiguousarray(x.T)
        m = {"x": x, "xT": xT}
        m.update(shared)
        in_maps.append(m)
    nc = build_program(S=4096)
    res = run_bass_kernel_spmd(nc, in_maps, core_ids=list(range(8)))
    out = np.stack([np.asarray(r["out"], dtype=np.float32)
                    for r in res.results], axis=0)
    return out
